# version 42
# baseline (speedup 1.0000x reference)
import numpy as np
import ml_dtypes
import concourse.bass as bass
import concourse.mybir as mybir
from concourse.bass_utils import run_bass_kernel_spmd

F32 = mybir.dt.float32
BF16 = mybir.dt.bfloat16
AF = mybir.ActivationFunctionType
ALU = mybir.AluOpType
BF = ml_dtypes.bfloat16

E = 1024
N = 2048
EC = 8
JC = 32
NH = 8
NCH = 16
TF = 512
NTF = 4
TW = 256
NW = 8

ENGS = ("sync", "gpsimd", "tensor", "vector", "scalar")


class Sched:
    def __init__(self):
        self.items = {e: [] for e in ENGS}
        self.cnt = {}

    def sem(self, name):
        self.cnt[name] = 0

    def at(self, sem):
        return self.cnt[sem]

    def emit(self, eng, fn, waits=(), incs=()):
        for sname, a in incs:
            self.cnt[sname] += a
        self.items[eng].append((fn, [w for w in waits if w is not None], list(incs)))


def build_program():
    nc = bass.Bass("TRN2", target_bir_lowering=False)

    di = lambda name, shape, dt: nc.dram_tensor(name, shape, dt, kind="ExternalInput")
    x_fm = di("x_fm", [E, N], F32)
    w1h = di("w1h", [128, 32768], F32)
    w2h = di("w2h", [128, 32768], F32)
    wqt = di("wqt", [128, 8192], F32)
    wkt = di("wkt", [128, 8192], F32)
    wut = di("wut", [128, 8192], F32)
    wvt = di("wvt", [128, 8192], F32)
    wot = di("wot", [128, 8192], F32)
    ident_d = di("ident", [128, 128], BF16)
    ones_d = di("ones", [128, 128], BF16)
    triu4_d = di("triu4", [128, 512], F32)
    b1_d = di("b1l", [128, 32], F32)
    b2_d = di("b2l", [128, 8], F32)
    bq_d = di("bql", [128, 8], F32)
    bk_d = di("bkl", [128, 8], F32)
    bu_d = di("bul", [128, 8], F32)
    bo_d = di("bol", [128, 8], F32)
    scl_d = di("scl", [128, 8], F32)
    bvb_d = di("bvb", [128, 1024], F32)
    cvec_d = di("cvec", [128, 8], F32)
    out_fm = nc.dram_tensor("out_fm", [E, N], F32, kind="ExternalOutput")
    w1scr = nc.dram_tensor("w1scr", [128, 32768], BF16, kind="Internal")
    w2scr = nc.dram_tensor("w2scr", [128, 32768], BF16, kind="Internal")

    sbt = lambda name, shape, dt: nc.alloc_sbuf_tensor(name, shape, dt)
    X = sbt("X", [128, EC * N], F32)
    ID = sbt("ID", [128, 128], BF16)
    ONES = sbt("ONES", [128, 128], BF16)
    B1 = sbt("B1", [128, 32], F32)
    B2 = sbt("B2", [128, 8], F32)
    B2H = sbt("B2H", [128, 8], F32)
    BQ = sbt("BQ", [128, 8], F32)
    BK = sbt("BK", [128, 8], F32)
    BU = sbt("BU", [128, 8], F32)
    BO = sbt("BO", [128, 8], F32)
    CV = sbt("CV", [128, 8], F32)
    DUMW = sbt("DUMW", [128, 1], F32)

    SCRATCH_BYTES = 145408
    base, _ = nc.bump_sbuf(SCRATCH_BYTES)
    at = lambda name, off, shape, dt: nc.alloc_sbuf_tensor_at(name, shape, dt, offset=base + off)
    HS = at("HS", 0, [128, JC * TF], BF16)
    LNA = at("LNA", 32768, [128, EC * N], BF16)
    XSQ = at("XSQ", 65536, [128, EC * TF], BF16)
    W1S = [at(f"W1S{i}", 73728 + i * 8192, [128, 4096], BF16) for i in range(2)]
    W2S = [at(f"W2S{i}", 90112 + i * 8192, [128, 4096], BF16) for i in range(2)]
    MBF = at("MBF", 106496, [128, TF], BF16)
    RBF = at("RBF", 107520, [128, TF], BF16)
    FT1 = at("FT1", 108544, [128, TF], F32)
    FT2 = at("FT2", 110592, [128, TF], F32)
    LNW = at("LNW", 0, [128, EC * TW], BF16)
    ZB = at("ZB", 4096, [128, NH * TW], BF16)
    QF = at("QF", 8192, [128, NH * TW], BF16)
    KF = at("KF", 12288, [128, NH * TW], BF16)
    UB = at("UB", 16384, [128, NH * TW], BF16)
    VT = at("VT", 20480, [128, 2 * 1024], BF16)
    KT = at("KT", 24576, [128, 2 * 1024], BF16)
    XSA = at("XSA", 28672, [128, EC * TW], BF16)
    WQT = at("WQT", 32768, [128, 8192], BF16)
    WKT = at("WKT", 49152, [128, 8192], BF16)
    WUT = at("WUT", 65536, [128, 8192], BF16)
    WVT = at("WVT", 81920, [128, 8192], BF16)
    WOT = at("WOT", 98304, [128, 8192], BF16)
    SCL = sbt("SCL", [128, 8], F32)
    OB = at("OB", 118784, [128, NH * TW], BF16)
    OSQ = at("OSQ", 122880, [128, NH * TW], BF16)
    SSB = at("SSB", 126976, [128, NH * 128], BF16)
    ITR8 = at("ITR8", 129024, [128, 1024], BF16)
    PEL2 = at("PEL2", 131072, [128, 512], F32)
    T1EL = at("T1EL", 133120, [128, 512], F32)
    T2EL = at("T2EL", 135168, [128, 512], F32)
    US = at("US", 131072, [128, NH * TW], BF16)
    MBA = at("MBA", 135168, [128, TW], BF16)
    RBA = at("RBA", 135680, [128, TW], BF16)
    LNT1 = at("LNT1", 136192, [128, TW], F32)
    R1A = at("R1A", 137216, [128, TW], F32)
    R2A = at("R2A", 138240, [128, TW], F32)
    TRIU4 = at("TRIU4", 139264, [128, 512], F32)
    BVB = at("BVB", 141312, [128, 1024], F32)
    MBF2 = at("MBF2", 139264, [128, 512], BF16)
    RBF2 = at("RBF2", 140288, [128, 512], BF16)

    W1X = [at(f"W1X{i}", 118784 + i * 8192, [128, 4096], BF16) for i in range(2)]

    PSA = nc.alloc_psum_tensor("PSA", [128, 4096], F32)
    PS = lambda bank, off=0, width=512: PSA[:, bank * 512 + off: bank * 512 + off + width]

    s = Sched()
    sems = {}
    for name in ("sq", "gq", "sT", "sV", "sS", "so", "sx", "sw",
                 "u0", "u1", "u2", "u3", "v0", "v1"):
        sems[name] = nc.alloc_semaphore(name)
        s.sem(name)

    def dma(eng, out, in_, waits=(), sem=None):
        name = sem or ("gq" if eng == "gpsimd" else "sq")
        s.emit(eng, lambda e, o=out, i=in_: e.dma_start(out=o, in_=i), waits, [(name, 16)])

    def op(eng, fn, waits=(), inc=None):
        s.emit(eng, fn, waits, [(inc, 1)] if inc else [])

    for dst, src in ((ID, ident_d), (ONES, ones_d), (B1, b1_d),
                     (B2, b2_d), (BQ, bq_d), (BK, bk_d), (BU, bu_d), (BO, bo_d),
                     (CV, cvec_d), (SCL, scl_d)):
        dma("sync", dst[:], src[:])
    n_const = s.at("sq")
    for ec in range(EC):
        s.emit("sync", lambda e, o=X[:, ec * N: ec * N + TF],
               i=x_fm[ec * 128:(ec + 1) * 128, 0:TF]: e.dma_start(out=o, in_=i),
               (), [("sx", 16)])
    x_q1 = ("sx", s.at("sx"))
    for ec in range(EC):
        dma("sync", X[:, ec * N + TF:(ec + 1) * N], x_fm[ec * 128:(ec + 1) * 128, TF:N])
    x_loaded = s.at("sq")
    op("vector", lambda e: e.tensor_scalar_mul(B2H[:], B2[:], 0.5),
       waits=[("sq", n_const)], inc="sV")

    w1_free = [0, 0]
    w2_free = [0, 0]
    w1r_free = [[("sT", 0)] for _ in range(4)]
    w1c_free = [0, 0, 0, 0]
    w2c_free = [0, 0, 0, 0]
    w1_scr_sq = [0] * 8
    w2_scr_sq = [0] * 8

    def load_w1(b, cast_pass, ring4=False):
        if cast_pass:
            ring = (W1S[0], W1S[1], W1X[0], W1X[1])[b % 4]
            slot_w = [("u%d" % (b % 4), 16 * (b // 4))] if b >= 4 else []
            dma("gpsimd", ring[:], w1h[:, b * 4096:(b + 1) * 4096],
                waits=[("sT", w1c_free[b % 4])] + slot_w)
            ready = ("gq", s.at("gq"))
            dma("scalar", w1scr[:, b * 4096:(b + 1) * 4096], ring[:], waits=[ready],
                sem="u%d" % (b % 4))
            return ready, ring
        if ring4:
            ring = (W1S[0], W1S[1], W1X[0], W1X[1])[b % 4]
            dma("sync", ring[:], w1scr[:, b * 4096:(b + 1) * 4096],
                waits=list(w1r_free[b % 4]) + [("u%d" % (b % 4), 32)])
            return ("sq", s.at("sq")), ring
        slot = b % 2
        src_w = [("u%d" % (b % 4), 16 * (b // 4 + 1))] if b % 4 >= 2 else []
        dma("sync", W1S[slot][:], w1scr[:, b * 4096:(b + 1) * 4096],
            waits=[("sT", w1_free[slot]), ("u%d" % slot, 32)] + src_w)
        return ("sq", s.at("sq")), W1S[slot]

    def load_w2(e2, cast_pass):
        slot = e2 % 2
        if cast_pass:
            slot_w = [("v%d" % slot, 16 * (e2 // 2))] if e2 >= 2 else []
            dma("gpsimd", W2S[slot][:], w2h[:, e2 * 4096:(e2 + 1) * 4096],
                waits=[("sT", w2_free[slot])] + slot_w)
            ready = ("gq", s.at("gq"))
            dma("scalar", w2scr[:, e2 * 4096:(e2 + 1) * 4096], W2S[slot][:], waits=[ready],
                sem="v%d" % slot)
            return ready, W2S[slot]
        dma("sync", W2S[slot][:], w2scr[:, e2 * 4096:(e2 + 1) * 4096],
            waits=[("sT", w2_free[slot]), ("v%d" % slot, 64)])
        return ("sq", s.at("sq")), W2S[slot]

    marks = {}

    def emit_ffn_ln_casts(t, xw, extra_stats_wait=(), sq_scalar=False):
        tok0 = t * TF
        for ec2 in range(EC):
            xs = X[:, ec2 * N + tok0: ec2 * N + tok0 + TF]
            op("vector", lambda e, o=LNA[:, ec2 * N + tok0: ec2 * N + tok0 + TF], i=xs:
               e.tensor_copy(o, i), waits=xw, inc="sV" if sq_scalar else None)
            if sq_scalar:
                op("scalar", lambda e, o=XSQ[:, ec2 * TF:(ec2 + 1) * TF], i=xs:
                   e.activation(o, i, AF.Square), waits=xw, inc="sS")
            else:
                op("vector", lambda e, o=XSQ[:, ec2 * TF:(ec2 + 1) * TF], i=xs:
                   e.tensor_mul(o, i, i), inc="sV")
        if sq_scalar:
            return (s.at("sV"), s.at("sS"))
        return s.at("sV")

    def emit_ffn_ln_rest(t, casts_done, extra_waits=(), chain_wait=None, bufs=None):
        tok0 = t * TF
        mb, rb, f1, f2 = bufs if bufs else (MBF[:], RBF[:], FT1[:], FT2[:])

        def stats_mm(e, t0=tok0):
            for c2 in range(EC):
                e.matmul(PS(4), ONES[:], LNA[:, c2 * N + t0: c2 * N + t0 + TF],
                         start=(c2 == 0), stop=(c2 == EC - 1))
            for c2 in range(EC):
                inst = e.matmul(PS(5), ONES[:], XSQ[:, c2 * TF:(c2 + 1) * TF],
                                start=(c2 == 0), stop=(c2 == EC - 1))
            return inst
        if isinstance(casts_done, list):
            cw = list(casts_done)
        elif isinstance(casts_done, tuple):
            cw = [("sV", casts_done[0]), ("sS", casts_done[1])]
        else:
            cw = [("sV", casts_done)]
        op("tensor", stats_mm, waits=cw + list(extra_waits), inc="sT")
        marks["ffn_stats_T"] = s.at("sT")
        op("vector", lambda e: e.tensor_scalar_mul(mb, PS(4), 1.0 / E),
           waits=[("sT", s.at("sT"))] + ([chain_wait] if chain_wait else []), inc="sV")
        op("scalar", lambda e: e.activation(f1, mb, AF.Square),
           waits=[("sV", s.at("sV"))], inc="sS")
        op("vector", lambda e: e.scalar_tensor_tensor(f2, PS(5), 1.0 / E, f1,
                                                      ALU.mult, ALU.subtract),
           waits=[("sS", s.at("sS"))], inc="sV")
        op("scalar", lambda e: e.activation(f1, f2, AF.Ln, bias=CV[:, 0:1]),
           waits=[("sV", s.at("sV"))])
        op("scalar", lambda e: e.activation(rb, f1, AF.Exp, scale=CV[:, 1:2]),
           inc="sS")
        for ec2 in range(EC):
            sl = LNA[:, ec2 * N + tok0: ec2 * N + tok0 + TF]
            op("vector", lambda e, a=sl, m2=mb: e.tensor_sub(a, a, m2),
               waits=[("sS", s.at("sS"))] if ec2 == 0 else ())
            op("vector", lambda e, a=sl, r2=rb: e.tensor_mul(a, a, r2),
               inc="sV" if ec2 == EC - 1 else None)
        return s.at("sV")

    def ffn_pass(pass_idx, x_ready_waits, ln0_done=None):
        if pass_idx == 1:
            w1_free[0] = w1_free[1] = marks["uv_T"]
            w2_free[0] = w2_free[1] = marks["wo_T"]
            w1r_free[0] = [("sT", marks["uv_T"])]
            w1r_free[1] = [("sT", marks["uv_T"])]
            w1r_free[2] = [marks["ob_free"], ("sS", marks["osq_sS"]),
                           ("sT", marks["wo_T"])]
            w1r_free[3] = [("sT", marks["wo_T"])]

        def emit_ln_casts(t, first=False, guard_T=0):
            xw = (list(x_ready_waits) if (pass_idx == 1 or not first) else [x_q1]) \
                + [("sT", guard_T)]
            return emit_ffn_ln_casts(t, xw)

        def emit_ln_rest(t, casts_done):
            return emit_ffn_ln_rest(t, casts_done)

        if ln0_done is None:
            c0 = emit_ln_casts(0, first=(pass_idx == 0))
            ln_done_t = [emit_ln_rest(0, c0)]
        else:
            ln_done_t = [ln0_done]
        prev_y_done = s.at("sV")

        for t in range(NTF):
            tok0 = t * TF
            ln_done = ln_done_t[t]
            cast_pass = (pass_idx == 0 and t == 0)
            gelu_base = s.at("sS")
            ring4 = (pass_idx == 1 and t == 0)
            for b in range(EC):
                wready, w1t = load_w1(b, cast_pass, ring4=ring4)
                for jl in range(4):
                    j = b * 4 + jl
                    pbank = j % 4
                    waits = [wready, ("sV", ln_done)]
                    if j < 4:
                        waits.append(("sV", prev_y_done))
                        if t == 0:
                            waits.append(("sS", s.at("sS")))
                    if j >= 4:
                        waits.append(("sS", gelu_base + j - 3))

                    def g1(e, wt=w1t, jl2=jl, pb2=pbank, t0=tok0):
                        for c2 in range(EC):
                            inst = e.matmul(
                                PS(pb2),
                                wt[:, c2 * 512 + jl2 * 128: c2 * 512 + jl2 * 128 + 128],
                                LNA[:, c2 * N + t0: c2 * N + t0 + TF],
                                start=(c2 == 0), stop=(c2 == EC - 1))
                        return inst
                    op("tensor", g1, waits=waits, inc="sT")
                    op("scalar", lambda e, j2=j, pb2=pbank:
                       e.activation(HS[:, j2 * TF:(j2 + 1) * TF], PS(pb2), AF.Gelu,
                                    bias=B1[:, j2:j2 + 1]),
                       waits=[("sT", s.at("sT"))], inc="sS")
                if cast_pass:
                    w1c_free[b % 4] = s.at("sT")
                    if b % 4 in (0, 1):
                        w1_free[b % 2] = s.at("sT")
                elif ring4:
                    w1r_free[b % 4] = [("sT", s.at("sT"))]
                    if b % 4 in (0, 1):
                        w1_free[b % 2] = s.at("sT")
                else:
                    w1_free[b % 2] = s.at("sT")
            gelus_done = s.at("sS")
            if pass_idx == 0 and t == NTF - 1:
                marks["ffn1_g1_end"] = s.at("sT")

            if t + 1 < NTF:
                ln_done_t.append(emit_ln_rest(
                    t + 1, emit_ln_casts(t + 1, guard_T=marks.get("ffn_stats_T", 0))))

            for e2 in range(EC):
                xs = X[:, e2 * N + tok0: e2 * N + tok0 + TF]
                op("vector", lambda e, o=xs, e3=e2:
                   e.tensor_scalar_add(o, o, B2H[:, e3:e3 + 1]),
                   waits=[("sV", ln_done)] if e2 == 0 else ())
            y_evac_base = s.at("sV")
            for e2 in range(EC):
                wready, w2t = load_w2(e2, cast_pass)
                pbank = e2 % 4
                waits = [wready, ("sS", gelus_done)]
                if e2 >= 4:
                    waits.append(("sV", y_evac_base + e2 - 3))

                def g2(e, wt=w2t, pb2=pbank):
                    for jc2 in range(JC):
                        inst = e.matmul(PS(pb2), wt[:, jc2 * 128:(jc2 + 1) * 128],
                                        HS[:, jc2 * TF:(jc2 + 1) * TF],
                                        start=(jc2 == 0), stop=(jc2 == JC - 1))
                    return inst
                op("tensor", g2, waits=waits, inc="sT")
                xs = X[:, e2 * N + tok0: e2 * N + tok0 + TF]
                op("vector", lambda e, o=xs, pb2=pbank:
                   e.scalar_tensor_tensor(o, PS(pb2), 0.5, o, ALU.mult, ALU.add),
                   waits=[("sT", s.at("sT"))], inc="sV")
                w2_free[e2 % 2] = s.at("sT")
                if pass_idx == 1:
                    s.emit("gpsimd",
                           lambda e, e3=e2, t0=tok0: e.dma_start(
                               out=out_fm[e3 * 128:(e3 + 1) * 128, t0:t0 + TF],
                               in_=X[:, e3 * N + t0: e3 * N + t0 + TF]),
                           [("sV", s.at("sV"))], [("so", 16)])
                if pass_idx == 0 and t == NTF - 1 and e2 == 6:
                    marks["ffn1_g2_e6"] = s.at("sT")
            prev_y_done = s.at("sV")

    ffn_pass(0, [("sq", x_loaded)])
    ffn1_T, ffn1_V, ffn1_S = s.at("sT"), s.at("sV"), s.at("sS")

    sq_now = ("sq", s.at("sq"))
    stores_done = [("u0", 32), ("u1", 32), ("u2", 32), ("u3", 32),
                   ("v0", 64), ("v1", 64)]
    aw = {}
    dma("gpsimd", WQT[:], wqt[:],
        waits=[("sT", marks["ffn1_g1_end"]), sq_now] + stores_done)
    aw["wqt"] = s.at("gq")
    dma("gpsimd", WKT[:], wkt[:], waits=[("sT", marks["ffn1_g1_end"])])
    aw["wkt"] = s.at("gq")
    dma("gpsimd", WUT[:], wut[:], waits=[("sT", marks["ffn1_g1_end"]), sq_now])
    aw["wut"] = s.at("gq")
    dma("gpsimd", WVT[:], wvt[:], waits=[("sT", marks["ffn1_g2_e6"]), sq_now])
    aw["wvt"] = s.at("gq")
    dma("gpsimd", WOT[:], wot[:], waits=[("sT", ffn1_T), ("sV", ffn1_V), ("sS", ffn1_S)])
    aw["wot"] = s.at("gq")
    dma("sync", TRIU4[:], triu4_d[:])
    dma("sync", BVB[:], bvb_d[:])
    consts_a = s.at("sq")

    def emit_attn_xsq(w, x_waits, eng="vector"):
        tok0 = w * TW
        sem = "sw" if eng == "gpsimd" else "sV"
        for ec2 in range(EC):
            xs = X[:, ec2 * N + tok0: ec2 * N + tok0 + TW]
            op(eng, lambda e, o=XSA[:, ec2 * TW:(ec2 + 1) * TW], i=xs:
               e.tensor_mul(o, i, i),
               waits=(list(x_waits) + [("sT", s.at("sT"))]) if ec2 == 0 else (),
               inc=sem if ec2 == EC - 1 else None)
        return (sem, s.at(sem))

    def emit_attn_ln_casts(w, x_waits, eng="vector"):
        tok0 = w * TW
        sem = "sw" if eng == "gpsimd" else "sV"
        for ec2 in range(EC):
            xs = X[:, ec2 * N + tok0: ec2 * N + tok0 + TW]
            op(eng, lambda e, o=LNW[:, ec2 * TW:(ec2 + 1) * TW], i=xs:
               e.tensor_copy(o, i),
               waits=(list(x_waits) + [("sT", s.at("sT"))]) if ec2 == 0 else (),
               inc=sem if ec2 == EC - 1 else None)
        return (sem, s.at(sem))

    def emit_attn_ln_rest(w, casts_done, extra_waits=()):
        def stats_mm_a(e):
            for c2 in range(EC):
                e.matmul(PS(4, 0, TW), ONES[:], LNW[:, c2 * TW:(c2 + 1) * TW],
                         start=(c2 == 0), stop=(c2 == EC - 1))
            for c2 in range(EC):
                inst = e.matmul(PS(5, 0, TW), ONES[:], XSA[:, c2 * TW:(c2 + 1) * TW],
                                start=(c2 == 0), stop=(c2 == EC - 1))
            return inst
        op("tensor", stats_mm_a,
           waits=[casts_done, ("sS", s.at("sS"))] + list(extra_waits), inc="sT")
        op("vector", lambda e: e.tensor_scalar_mul(MBA[:], PS(4, 0, TW), 1.0 / E),
           waits=[("sT", s.at("sT"))], inc="sV")
        op("scalar", lambda e: e.activation(LNT1[:], MBA[:], AF.Square),
           waits=[("sV", s.at("sV"))], inc="sS")
        op("vector", lambda e: e.scalar_tensor_tensor(LNT1[:], PS(5, 0, TW), 1.0 / E, LNT1[:],
                                                      ALU.mult, ALU.subtract),
           waits=[("sS", s.at("sS"))], inc="sV")
        ln_bank_free = s.at("sV")
        op("scalar", lambda e: e.activation(LNT1[:], LNT1[:], AF.Ln, bias=CV[:, 0:1]),
           waits=[("sV", s.at("sV"))])
        op("scalar", lambda e: e.activation(RBA[:], LNT1[:], AF.Exp, scale=CV[:, 1:2]),
           inc="sS")
        chain_done = s.at("sS")
        return ln_bank_free, chain_done

    def emit_attn_ln_norm(chain_done):
        for ec2 in range(EC):
            sl = LNW[:, ec2 * TW:(ec2 + 1) * TW]
            op("vector", lambda e, a=sl: e.tensor_sub(a, a, MBA[:]),
               waits=[("sS", chain_done)] if ec2 == 0 else ())
            op("vector", lambda e, a=sl: e.tensor_mul(a, a, RBA[:]),
               inc="sV" if ec2 == EC - 1 else None)
        return s.at("sV")

    emit_attn_xsq(0, [("sV", ffn1_V), ("sS", ffn1_S)])
    cd0 = emit_attn_ln_casts(0, [("sV", ffn1_V), ("sS", ffn1_S)])
    ln_bank_free, _cd = emit_attn_ln_rest(0, cd0)
    lnw_done = emit_attn_ln_norm(_cd)
    wot_scaled = None

    def proj_pair_mm(wsb, pr, bank):
        hc0, hc1 = 2 * pr, 2 * pr + 1

        def f(e, wsb2=wsb, h0=hc0, h1=hc1, b2=bank):
            for sub, hcx in ((0, h0), (1, h1)):
                for c2 in range(EC):
                    inst = e.matmul(
                        PS(b2, sub * TW, TW),
                        wsb2[:, c2 * 1024 + hcx * 128: c2 * 1024 + hcx * 128 + 128],
                        LNW[:, c2 * TW:(c2 + 1) * TW],
                        start=(sub == 0 and c2 == 0), stop=(c2 == EC - 1),
                        skip_group_check=(sub == 1))
            return inst
        return f

    for w in range(NW):
        tok0 = w * TW

        elu_free = {}
        for pi, (wname, wsb, bsb, osb) in enumerate((("wqt", WQT, BQ, QF),
                                                     ("wkt", WKT, BK, KF))):
            for pr in range(4):
                gi = pi * 4 + pr
                bank = gi % 2
                waits = [("gq", aw[wname]), ("sV", lnw_done)]
                if gi >= 2:
                    waits.append(("sV", elu_free[gi - 2]))
                else:
                    waits.append(("sV", marks.get(f"ev_b{bank}", ffn1_V)))
                op("tensor", proj_pair_mm(wsb, pr, bank), waits=waits, inc="sT")
                pm = s.at("sT")
                hc0, hc1 = 2 * pr, 2 * pr + 1
                op("vector", lambda e, b2=bank, b3=bsb, h=hc0:
                   e.tensor_scalar_add(PEL2[:, 0:TW], PS(b2, 0, TW), b3[:, h:h + 1]),
                   waits=[("sT", pm)])
                op("vector", lambda e, b2=bank, b3=bsb, h=hc1:
                   e.tensor_scalar_add(PEL2[:, TW:2 * TW], PS(b2, TW, TW), b3[:, h:h + 1]),
                   inc="sV")
                elu_free[gi] = s.at("sV")
                op("vector", lambda e: e.tensor_single_scalar(T1EL[:], PEL2[:], 0.0, ALU.max))
                op("vector", lambda e: e.tensor_single_scalar(T2EL[:], PEL2[:], 0.0, ALU.min),
                   inc="sV")
                op("scalar", lambda e: e.activation(T2EL[:], T2EL[:], AF.Exp),
                   waits=[("sV", s.at("sV"))], inc="sS")
                op("vector", lambda e, o=osb[:, hc0 * TW: hc0 * TW + 2 * TW]:
                   e.scalar_tensor_tensor(o, T1EL[:], -1.0, T2EL[:], ALU.add, ALU.add),
                   waits=[("sS", s.at("sS"))], inc="sV")
        kf_done = s.at("sV")
        marks["qk_T"] = s.at("sT")

        v_evac_g = {}
        for c in range(2):
            for half in range(2):
                bank = c * 2 + half
                waits = [("gq", aw["wvt"]), ("sq", consts_a)]
                if bank in (0, 1):
                    waits.append(("sV", elu_free[6 + bank]))
                else:
                    waits.append(("sS", marks.get("b23_scal_free", 0)))
                    waits.append(("sV", marks.get("b01_vec_free", ffn1_V)))

                def vmm(e, c2=c, h2=half, b2=bank):
                    for e3 in range(EC):
                        inst = e.matmul(PS(b2),
                                        LNW[:, e3 * TW + c2 * 128: e3 * TW + c2 * 128 + 128],
                                        WVT[:, e3 * 1024 + h2 * 512: e3 * 1024 + h2 * 512 + 512],
                                        start=(e3 == 0), stop=(e3 == EC - 1))
                    return inst
                op("tensor", vmm, waits=waits, inc="sT")
                op("vector", lambda e, c2=c, h2=half, b2=bank:
                   e.tensor_tensor(VT[:, c2 * 1024 + h2 * 512: c2 * 1024 + h2 * 512 + 512],
                                   PS(b2), BVB[:, h2 * 512:(h2 + 1) * 512], ALU.add),
                   waits=[("sT", s.at("sT"))], inc="sV")
                v_evac_g[bank] = s.at("sV")
            op("scalar", lambda e, c2=c:
               e.activation(VT[:, c2 * 1024:(c2 + 1) * 1024], VT[:, c2 * 1024:(c2 + 1) * 1024],
                            AF.Silu),
               waits=[("sV", s.at("sV"))], inc="sS")
        v_done_S = s.at("sS")
        if w == 0:
            for hc in range(NH):
                op("vector", lambda e, h2=hc:
                   e.tensor_scalar_mul(WOT[:, h2 * 1024:(h2 + 1) * 1024],
                                       WOT[:, h2 * 1024:(h2 + 1) * 1024], SCL[:, h2:h2 + 1]),
                   waits=[("gq", aw["wot"]), ("sq", consts_a)] if hc == 0 else (),
                   inc="sV" if hc == NH - 1 else None)
            wot_scaled = s.at("sV")
        if w + 1 < NW:
            emit_attn_xsq(w + 1, [("sV", ffn1_V)], eng="gpsimd")

        u_free = {}

        def emit_u_pair(pr):
            bank = 2 + pr % 2
            waits = [("gq", aw["wut"]), ("sV", lnw_done),
                     ("sV", v_evac_g[bank])]
            if pr >= 2:
                waits.append(("sS", u_free[pr - 2]))
            else:
                waits.append(("sS", marks.get("b23_scal_free", 0)))
            op("tensor", proj_pair_mm(WUT, pr, bank), waits=waits, inc="sT")
            pm = s.at("sT")
            for sub, hcx in ((0, 2 * pr), (1, 2 * pr + 1)):
                op("scalar", lambda e, b2=bank, sub2=sub, h=hcx:
                   e.activation(UB[:, h * TW:(h + 1) * TW], PS(b2, sub2 * TW, TW), AF.Silu,
                                bias=BU[:, h:h + 1]),
                   waits=[("sT", pm)], inc="sS")
            u_free[pr] = s.at("sS")

        g0, g1 = 2 * w, 2 * w + 1

        def ktr(c2, banks):
            def f(e, c3=c2, bk=banks):
                for h in range(NH):
                    inst = e.matmul(PS(bk[h // 4], (h % 4) * 128, 128),
                                    KF[:, h * TW + c3 * 128: h * TW + c3 * 128 + 128],
                                    ID[:], start=(h % 4 == 0), stop=True,
                                    skip_group_check=(h % 4 != 0))
                return inst
            return f

        ktrA_waits = [("sV", kf_done), marks.get("scan45_free", ("sS", 0))]
        if w == 0:
            ktrA_waits.append(("sq", consts_a))
        op("tensor", ktr(0, (4, 5)), waits=ktrA_waits, inc="sT")
        ktm = s.at("sT")
        op("scalar", lambda e: e.activation(KT[:, 0:512], PS(4), AF.Copy),
           waits=[("sT", ktm)])
        op("scalar", lambda e: e.activation(KT[:, 512:1024], PS(5), AF.Copy), inc="sS")
        kt0_done = ("sS", s.at("sS"))
        op("tensor", ktr(1, (0, 1)), waits=[("sV", v_evac_g[1])], inc="sT")
        ktm = s.at("sT")
        op("scalar", lambda e: e.activation(KT[:, 1024:1536], PS(0), AF.Copy),
           waits=[("sT", ktm)])
        op("scalar", lambda e: e.activation(KT[:, 1536:2048], PS(1), AF.Copy), inc="sS")
        kt1_done = ("sS", s.at("sS"))
        emit_u_pair(0)

        def imm(c2, banks):
            def f(e, c3=c2, bk=banks):
                for h in range(NH):
                    inst = e.matmul(PS(bk[h // 4], (h % 4) * 128, 128),
                                    KF[:, h * TW + c3 * 128: h * TW + c3 * 128 + 128],
                                    QF[:, h * TW + c3 * 128: h * TW + c3 * 128 + 128],
                                    start=(h % 4 == 0), stop=True,
                                    skip_group_check=(h % 4 != 0))
                return inst
            return f

        def omm(c2, g2, banks):
            def f(e, c3=c2, g3=g2, bk=banks):
                for h in range(NH):
                    inst = e.matmul(PS(bk[h // 4], (h % 4) * 128, 128),
                                    VT[:, c3 * 1024 + h * 128: c3 * 1024 + h * 128 + 128],
                                    ITR8[:, h * 128:(h + 1) * 128],
                                    start=(h % 4 == 0), stop=(g3 == 0),
                                    skip_group_check=(h % 4 != 0))
                    if g3 > 0:
                        inst = e.matmul(PS(bk[h // 4], (h % 4) * 128, 128),
                                        SSB[:, h * 128:(h + 1) * 128],
                                        QF[:, h * TW + c3 * 128: h * TW + c3 * 128 + 128],
                                        start=False, stop=True, skip_group_check=True)
                return inst
            return f

        def smm(c2, g2):
            def f(e, c3=c2, g3=g2):
                for h in range(NH):
                    inst = e.matmul(PS(6 + h // 4, (h % 4) * 128, 128),
                                    KT[:, c3 * 1024 + h * 128: c3 * 1024 + h * 128 + 128],
                                    VT[:, c3 * 1024 + h * 128: c3 * 1024 + h * 128 + 128],
                                    start=(g3 == 0 and h % 4 == 0), stop=True,
                                    skip_group_check=True)
                return inst
            return f

        def oevac(c2, banks, first_wait):
            for bi in range(2):
                src_ap = PS(banks[bi]).rearrange("p (h n) -> p h n", h=4)
                dst_ap = OB[:].rearrange("p (h n) -> p h n", h=8)[:, bi * 4:(bi + 1) * 4,
                                                                 c2 * 128:(c2 + 1) * 128]
                op("scalar", lambda e, si=src_ap, di_=dst_ap:
                   e.activation(di_, si, AF.Copy),
                   waits=[("sT", first_wait)] if bi == 0 else (),
                   inc="sS" if bi == 1 else None)
            return ("sS", s.at("sS"))

        op("tensor", imm(0, (4, 5)), waits=[kt0_done], inc="sT")
        im0 = s.at("sT")
        op("tensor", imm(1, (0, 1)), waits=[kt1_done, ("sV", v_evac_g[3])], inc="sT")
        im1 = s.at("sT")
        emit_u_pair(1)
        op("vector", lambda e: e.tensor_tensor(ITR8[:, 0:512], PS(4), TRIU4[:], ALU.mult),
           waits=[("sT", im0)])
        op("vector", lambda e: e.tensor_tensor(ITR8[:, 512:1024], PS(5), TRIU4[:], ALU.mult),
           inc="sV")
        mask0 = s.at("sV")
        op("tensor", omm(0, g0, (4, 5)),
           waits=[("sV", mask0), ("sS", max(v_done_S, marks.get("s_copy", 0)))],
           inc="sT")
        om0 = s.at("sT")
        marks["scan45_free"] = None
        o_evac_0 = oevac(0, (4, 5), om0)
        marks["scan45_free"] = o_evac_0
        op("tensor", smm(0, g0), waits=[("sS", max(v_done_S, marks.get("s_copy", 0)))], inc="sT")
        sm = s.at("sT")
        op("scalar", lambda e: e.activation(SSB[:, 0:512], PS(6), AF.Copy),
           waits=[("sT", sm)], inc="sS")
        op("scalar", lambda e: e.activation(SSB[:, 512:1024], PS(7), AF.Copy), inc="sS")
        marks["s_copy"] = s.at("sS")
        emit_u_pair(2)

        ln_next = None

        op("vector", lambda e: e.tensor_tensor(ITR8[:, 0:512], PS(0), TRIU4[:], ALU.mult),
           waits=[("sT", im1), ("sT", om0)])
        op("vector", lambda e: e.tensor_tensor(ITR8[:, 512:1024], PS(1), TRIU4[:], ALU.mult),
           inc="sV")
        mask1 = s.at("sV")
        op("tensor", omm(1, g1, (0, 1)),
           waits=[("sV", mask1), ("sS", marks["s_copy"])], inc="sT")
        om1 = s.at("sT")
        emit_u_pair(3)
        u_done = u_free[3]
        marks["uv_T"] = s.at("sT")
        ln_casts = (emit_attn_ln_casts(w + 1, [("sV", ffn1_V)], eng="gpsimd")
                    if w + 1 < NW else None)
        if w == NW - 1:
            c0xw = [("sV", marks["x_t01_final"]), ("sT", marks["uv_T"])]
            for ec2 in range(EC):
                op("gpsimd", lambda e, o=LNA[:, ec2 * N: ec2 * N + TF],
                   i=X[:, ec2 * N: ec2 * N + TF]: e.tensor_copy(o, i),
                   waits=c0xw if ec2 == 0 else (),
                   inc="sw" if ec2 == EC - 1 else None)
            marks["ffn2_c0_cp"] = ("sw", s.at("sw"))
        o_evac_1 = oevac(1, (0, 1), om1)
        marks["o_evac_1"] = o_evac_1
        if g1 < NCH - 1:
            op("tensor", smm(1, g1), waits=[("sS", marks["s_copy"])], inc="sT")
            sm = s.at("sT")
            op("scalar", lambda e: e.activation(SSB[:, 0:512], PS(6), AF.Copy),
               waits=[("sT", sm)], inc="sS")
            op("scalar", lambda e: e.activation(SSB[:, 512:1024], PS(7), AF.Copy), inc="sS")
            marks["s_copy"] = s.at("sS")

        op("scalar", lambda e: e.activation(DUMW[:], CV[:, 0:1], AF.Ln))
        if w == NW - 1:
            c0xw = [("sV", marks["x_t01_final"]), ("sT", marks["uv_T"])]
            for ec2 in range(EC):
                op("scalar", lambda e, o=XSQ[:, ec2 * TF:(ec2 + 1) * TF],
                   i=X[:, ec2 * N: ec2 * N + TF]: e.activation(o, i, AF.Square),
                   waits=c0xw if ec2 == 0 else (),
                   inc="sS" if ec2 == EC - 1 else None)
            marks["ffn2_c0"] = [marks["ffn2_c0_cp"], ("sS", s.at("sS"))]

        o_done = marks["o_evac_1"]
        op("vector", lambda e: e.tensor_mul(ZB[:], OB[:], UB[:]),
           waits=[o_done, ("sS", u_done)], inc="sV")
        zb_done = s.at("sV")
        marks["ob_free"] = ("sV", zb_done)
        op("scalar", lambda e: e.activation(OSQ[:], OB[:], AF.Square),
           waits=[o_done, ("sT", s.at("sT"))], inc="sS")
        osq_done = s.at("sS")
        marks["osq_sS"] = osq_done

        def wo_mm(ea, eb, b2):
            def f(e, ea2=ea, eb2=eb, b3=b2):
                for sub, ex in ((0, ea2), (1, eb2)):
                    for h in range(NH):
                        inst = e.matmul(
                            PS(b3, sub * TW, TW),
                            WOT[:, h * 1024 + ex * 128: h * 1024 + ex * 128 + 128],
                            ZB[:, h * TW:(h + 1) * TW],
                            start=(sub == 0 and h == 0), stop=(h == NH - 1),
                            skip_group_check=(sub == 1))
                return inst
            return f

        def wo_evac(ea, eb, b2, wm, rstd_sS):
            for sub, ex in ((0, ea), (1, eb)):
                xs = X[:, ex * N + tok0: ex * N + tok0 + TW]
                op("vector", lambda e, b3=b2, sub2=sub:
                   e.tensor_mul(R2A[:], PS(b3, sub2 * TW, TW), R1A[:]),
                   waits=[("sT", wm), ("sS", rstd_sS)] if sub == 0 else ())
                op("vector", lambda e, o=xs, e3=ex:
                   e.scalar_tensor_tensor(o, R2A[:], BO[:, e3:e3 + 1], o, ALU.add, ALU.add),
                   inc="sV" if sub == 1 else None)
            return s.at("sV")

        op("tensor", wo_mm(0, 1, 0), waits=[("sV", zb_done), ("sV", wot_scaled)], inc="sT")
        wm0 = s.at("sT")
        ln_next = None
        if w + 1 < NW:
            ln_next = emit_attn_ln_rest(w + 1, ln_casts,
                                        extra_waits=[marks["scan45_free"]])
            marks["scan45_free"] = ("sV", ln_next[0])
        else:
            marks["ffn2_ln0"] = emit_ffn_ln_rest(
                0, marks["ffn2_c0"],
                extra_waits=[marks["scan45_free"], ("sV", zb_done)],
                bufs=(MBF2[:], RBF2[:], BVB[:, 0:512], BVB[:, 512:1024]))
        op("tensor", wo_mm(2, 3, 1), waits=[("sV", zb_done)], inc="sT")
        wm1 = s.at("sT")

        def rms_mm(e):
            for h in range(NH):
                inst = e.matmul(PS(2, 0, TW), ONES[:], OSQ[:, h * TW:(h + 1) * TW],
                                start=(h == 0), stop=(h == NH - 1))
            return inst
        rms_waits = [("sS", osq_done)]
        op("tensor", rms_mm, waits=rms_waits, inc="sT")
        op("scalar", lambda e: e.activation(R1A[:], PS(2, 0, TW), AF.Ln, bias=CV[:, 2:3]),
           waits=[("sT", s.at("sT"))], inc="sS")
        rms_ln1 = s.at("sS")
        op("scalar", lambda e: e.activation(R1A[:], R1A[:], AF.Exp, scale=CV[:, 3:4]))
        op("scalar", lambda e: e.activation(R1A[:], R1A[:], AF.Ln, bias=CV[:, 6:7]))
        op("scalar", lambda e: e.activation(R1A[:], R1A[:], AF.Exp, scale=CV[:, 4:5],
                                            bias=CV[:, 5:6]), inc="sS")
        rstd_sS = s.at("sS")
        marks["b23_scal_free"] = s.at("sS")

        op("tensor", wo_mm(4, 5, 3), waits=[("sV", zb_done)], inc="sT")
        wm2 = s.at("sT")
        op("tensor", wo_mm(6, 7, 2), waits=[("sV", zb_done), ("sS", rms_ln1)], inc="sT")
        wm3 = s.at("sT")
        marks["wo_T"] = wm3
        if ln_next is not None:
            lnw_done = emit_attn_ln_norm(ln_next[1])
        marks["ev_b0"] = wo_evac(0, 1, 0, wm0, rstd_sS)
        marks["ev_b1"] = wo_evac(2, 3, 1, wm1, rstd_sS)
        wo_evac(4, 5, 3, wm2, rstd_sS)
        wo_evac(6, 7, 2, wm3, rstd_sS)
        marks["b01_vec_free"] = s.at("sV")
        if w == 1:
            marks["x_t01_final"] = s.at("sV")


    attn_V = s.at("sV")

    ffn_pass(1, [("sV", attn_V), ("sS", s.at("sS"))], ln0_done=marks["ffn2_ln0"])

    so_total = s.at("so")
    s.emit("sync", lambda e: e.wait_ge(sems["so"], so_total), [], [])

    with nc.Block() as blk:
        def make_body(eng_name):
            def body(eng):
                last_wait = {}
                for fn, waits, incs in s.items[eng_name]:
                    for wsem, wval in waits:
                        if wval <= 0 or last_wait.get(wsem, -1) >= wval:
                            continue
                        eng.wait_ge(sems[wsem], wval)
                        last_wait[wsem] = wval
                    inst = fn(eng)
                    for isem, amt in incs:
                        inst = inst.then_inc(sems[isem], amt)
            return body
        blk.sync(make_body("sync"))
        blk.gpsimd(make_body("gpsimd"))
        blk.tensor(make_body("tensor"))
        blk.vector(make_body("vector"))
        blk.scalar(make_body("scalar"))

    return nc


def _host_inputs(Wq, bq, Wk, bk, Wv, bv, Wu, bu, Wo, bo, scale, w1, b1, w2, b2):
    asf = lambda a: np.ascontiguousarray(np.asarray(a, dtype=np.float32))
    w1 = asf(w1); w2 = asf(w2)
    fmw = lambda W: np.ascontiguousarray(
        asf(W).T.reshape(8, 128, 1024).transpose(1, 0, 2).reshape(128, 8192))
    triu = np.triu(np.ones((128, 128), dtype=np.float32))
    scl = np.ascontiguousarray(asf(scale).reshape(8, 128).T)
    return {
        "w1h": np.ascontiguousarray(w1.reshape(8, 128, 8, 512).transpose(1, 2, 0, 3).reshape(128, 32768)),
        "w2h": np.ascontiguousarray(w2.reshape(32, 128, 8, 128).transpose(1, 2, 0, 3).reshape(128, 32768)),
        "wqt": fmw(Wq), "wkt": fmw(Wk), "wut": fmw(Wu), "wvt": fmw(Wv), "wot": fmw(Wo),
        "ident": np.eye(128, dtype=BF),
        "ones": np.ones((128, 128), dtype=BF),
        "triu4": np.ascontiguousarray(np.tile(triu, (1, 4))),
        "b1l": np.ascontiguousarray(asf(b1).reshape(32, 128).T),
        "b2l": np.ascontiguousarray(asf(b2).reshape(8, 128).T),
        "bql": np.ascontiguousarray(asf(bq).reshape(8, 128).T),
        "bkl": np.ascontiguousarray(asf(bk).reshape(8, 128).T),
        "bul": np.ascontiguousarray(asf(bu).reshape(8, 128).T),
        "bol": np.ascontiguousarray(asf(bo).reshape(8, 128).T),
        "scl": scl,
        "bvb": np.ascontiguousarray(np.broadcast_to(asf(bv), (128, 1024))),
        "cvec": np.ascontiguousarray(np.tile(np.array(
            [1e-5, -0.5, 1e-30, 0.5, -1.0, np.log(32.0), 3.2e-7, 0.0],
            dtype=np.float32), (128, 1))),
    }


def kernel(L0_attn_x, L0_attn_Wq, L0_attn_bq, L0_attn_Wk, L0_attn_bk, L0_attn_Wv,
           L0_attn_bv, L0_attn_Wu, L0_attn_bu, L0_attn_Wo, L0_attn_bo, L0_attn_scale,
           L0_ffn_w1, L0_ffn_b1, L0_ffn_w2, L0_ffn_b2):
    x = np.asarray(L0_attn_x, dtype=np.float32)
    com = _host_inputs(L0_attn_Wq, L0_attn_bq, L0_attn_Wk, L0_attn_bk, L0_attn_Wv,
                       L0_attn_bv, L0_attn_Wu, L0_attn_bu, L0_attn_Wo, L0_attn_bo,
                       L0_attn_scale, L0_ffn_w1, L0_ffn_b1, L0_ffn_w2, L0_ffn_b2)
    nc = build_program()
    in_maps = []
    for b in range(8):
        m = dict(com)
        m["x_fm"] = np.ascontiguousarray(x[b].T)
        in_maps.append(m)
    res = run_bass_kernel_spmd(nc, in_maps, core_ids=list(range(8)))
    out = np.stack([np.asarray(r["out_fm"], dtype=np.float32).T for r in res.results])
    return np.ascontiguousarray(out)


if __name__ == "__main__":
    nc = build_program()
    print("built ok, instructions:", sum(1 for _ in nc.all_instructions()))


# version 47
# speedup vs baseline: 1.0017x; 1.0017x over previous
import numpy as np
import ml_dtypes
import concourse.bass as bass
import concourse.mybir as mybir
from concourse.bass_utils import run_bass_kernel_spmd

F32 = mybir.dt.float32
BF16 = mybir.dt.bfloat16
AF = mybir.ActivationFunctionType
ALU = mybir.AluOpType
BF = ml_dtypes.bfloat16

E = 1024
N = 2048
EC = 8
JC = 32
NH = 8
NCH = 16
TF = 512
NTF = 4
TW = 256
NW = 8

ENGS = ("sync", "gpsimd", "tensor", "vector", "scalar")


class Sched:
    def __init__(self):
        self.items = {e: [] for e in ENGS}
        self.cnt = {}

    def sem(self, name):
        self.cnt[name] = 0

    def at(self, sem):
        return self.cnt[sem]

    def emit(self, eng, fn, waits=(), incs=()):
        for sname, a in incs:
            self.cnt[sname] += a
        self.items[eng].append((fn, [w for w in waits if w is not None], list(incs)))


def build_program():
    nc = bass.Bass("TRN2", target_bir_lowering=False)

    di = lambda name, shape, dt: nc.dram_tensor(name, shape, dt, kind="ExternalInput")
    x_fm = di("x_fm", [E, N], F32)
    w1h = di("w1h", [128, 32768], F32)
    w2h = di("w2h", [128, 32768], F32)
    wqt = di("wqt", [128, 8192], F32)
    wkt = di("wkt", [128, 8192], F32)
    wut = di("wut", [128, 8192], F32)
    wvt = di("wvt", [128, 8192], F32)
    wot = di("wot", [128, 8192], F32)
    ident_d = di("ident", [128, 128], BF16)
    ones_d = di("ones", [128, 128], BF16)
    triu4_d = di("triu4", [128, 512], F32)
    b1_d = di("b1l", [128, 32], F32)
    b2_d = di("b2l", [128, 8], F32)
    bq_d = di("bql", [128, 8], F32)
    bk_d = di("bkl", [128, 8], F32)
    bu_d = di("bul", [128, 8], F32)
    bo_d = di("bol", [128, 8], F32)
    scl_d = di("scl", [128, 8], F32)
    bvb_d = di("bvb", [128, 1024], F32)
    cvec_d = di("cvec", [128, 8], F32)
    out_fm = nc.dram_tensor("out_fm", [E, N], F32, kind="ExternalOutput")
    w1scr = nc.dram_tensor("w1scr", [128, 32768], BF16, kind="Internal")
    w2scr = nc.dram_tensor("w2scr", [128, 32768], BF16, kind="Internal")

    sbt = lambda name, shape, dt: nc.alloc_sbuf_tensor(name, shape, dt)
    X = sbt("X", [128, EC * N], F32)
    ID = sbt("ID", [128, 128], BF16)
    ONES = sbt("ONES", [128, 128], BF16)
    B1 = sbt("B1", [128, 32], F32)
    B2 = sbt("B2", [128, 8], F32)
    B2H = sbt("B2H", [128, 8], F32)
    BQ = sbt("BQ", [128, 8], F32)
    BK = sbt("BK", [128, 8], F32)
    BU = sbt("BU", [128, 8], F32)
    BO = sbt("BO", [128, 8], F32)
    CV = sbt("CV", [128, 8], F32)
    DUMW = sbt("DUMW", [128, 1], F32)

    SCRATCH_BYTES = 145408
    base, _ = nc.bump_sbuf(SCRATCH_BYTES)
    at = lambda name, off, shape, dt: nc.alloc_sbuf_tensor_at(name, shape, dt, offset=base + off)
    HS = at("HS", 0, [128, JC * TF], BF16)
    LNA = at("LNA", 32768, [128, EC * N], BF16)
    XSQ = at("XSQ", 65536, [128, EC * TF], BF16)
    W1S = [at(f"W1S{i}", 73728 + i * 8192, [128, 4096], BF16) for i in range(2)]
    W2S = [at(f"W2S{i}", 90112 + i * 8192, [128, 4096], BF16) for i in range(2)]
    MBF = at("MBF", 106496, [128, TF], BF16)
    RBF = at("RBF", 107520, [128, TF], BF16)
    FT1 = at("FT1", 108544, [128, TF], F32)
    FT2 = at("FT2", 110592, [128, TF], F32)
    LNW = at("LNW", 0, [128, EC * TW], BF16)
    ZB = at("ZB", 4096, [128, NH * TW], BF16)
    QF = at("QF", 8192, [128, NH * TW], BF16)
    KF = at("KF", 12288, [128, NH * TW], BF16)
    UB = at("UB", 16384, [128, NH * TW], BF16)
    VT = at("VT", 20480, [128, 2 * 1024], BF16)
    KT = at("KT", 24576, [128, 2 * 1024], BF16)
    XSA = at("XSA", 28672, [128, EC * TW], BF16)
    WQT = at("WQT", 32768, [128, 8192], BF16)
    WKT = at("WKT", 49152, [128, 8192], BF16)
    WUT = at("WUT", 65536, [128, 8192], BF16)
    WVT = at("WVT", 81920, [128, 8192], BF16)
    WOT = at("WOT", 98304, [128, 8192], BF16)
    SCL = sbt("SCL", [128, 8], F32)
    OB = at("OB", 118784, [128, NH * TW], BF16)
    OSQ = at("OSQ", 122880, [128, NH * TW], BF16)
    SSB = at("SSB", 126976, [128, NH * 128], BF16)
    ITR8 = at("ITR8", 129024, [128, 1024], BF16)
    PEL2 = at("PEL2", 131072, [128, 512], F32)
    T1EL = at("T1EL", 133120, [128, 512], F32)
    T2EL = at("T2EL", 135168, [128, 512], F32)
    US = at("US", 131072, [128, NH * TW], BF16)
    MBA = at("MBA", 135168, [128, TW], BF16)
    RBA = at("RBA", 135680, [128, TW], BF16)
    LNT1 = at("LNT1", 136192, [128, TW], F32)
    R1A = at("R1A", 137216, [128, TW], F32)
    R2A = at("R2A", 138240, [128, TW], F32)
    TRIU4 = at("TRIU4", 139264, [128, 512], F32)
    BVB = at("BVB", 141312, [128, 1024], F32)
    MBF2 = at("MBF2", 139264, [128, 512], BF16)
    RBF2 = at("RBF2", 140288, [128, 512], BF16)

    W1X = [at(f"W1X{i}", 118784 + i * 8192, [128, 4096], BF16) for i in range(2)]

    PSA = nc.alloc_psum_tensor("PSA", [128, 4096], F32)
    PS = lambda bank, off=0, width=512: PSA[:, bank * 512 + off: bank * 512 + off + width]

    s = Sched()
    sems = {}
    for name in ("sq", "gq", "sT", "sV", "sS", "so", "sx", "sw",
                 "u0", "u1", "u2", "u3", "v0", "v1"):
        sems[name] = nc.alloc_semaphore(name)
        s.sem(name)

    def dma(eng, out, in_, waits=(), sem=None):
        name = sem or ("gq" if eng == "gpsimd" else "sq")
        s.emit(eng, lambda e, o=out, i=in_: e.dma_start(out=o, in_=i), waits, [(name, 16)])

    def op(eng, fn, waits=(), inc=None):
        s.emit(eng, fn, waits, [(inc, 1)] if inc else [])

    for dst, src in ((ID, ident_d), (ONES, ones_d), (B1, b1_d),
                     (B2, b2_d), (BQ, bq_d), (BK, bk_d), (BU, bu_d), (BO, bo_d),
                     (CV, cvec_d), (SCL, scl_d)):
        dma("sync", dst[:], src[:])
    n_const = s.at("sq")
    for ec in range(EC):
        s.emit("sync", lambda e, o=X[:, ec * N: ec * N + TF],
               i=x_fm[ec * 128:(ec + 1) * 128, 0:TF]: e.dma_start(out=o, in_=i),
               (), [("sx", 16)])
    x_q1 = ("sx", s.at("sx"))
    for ec in range(EC):
        dma("sync", X[:, ec * N + TF:(ec + 1) * N], x_fm[ec * 128:(ec + 1) * 128, TF:N],
            waits=[x_q1] if ec == 0 else ())
    x_loaded = s.at("sq")
    op("vector", lambda e: e.tensor_scalar_mul(B2H[:], B2[:], 0.5),
       waits=[("sq", n_const)], inc="sV")
    op("scalar", lambda e: e.activation(DUMW[:], CV[:, 0:1], AF.Ln),
       waits=[("sq", n_const)])

    w1_free = [0, 0]
    w2_free = [0, 0]
    w1r_free = [[("sT", 0)] for _ in range(4)]
    w1c_free = [0, 0, 0, 0]
    w2c_free = [0, 0, 0, 0]
    w1_scr_sq = [0] * 8
    w2_scr_sq = [0] * 8

    def load_w1(b, cast_pass, ring4=False):
        if cast_pass:
            ring = (W1S[0], W1S[1], W1X[0], W1X[1])[b % 4]
            slot_w = [("u%d" % (b % 4), 16 * (b // 4))] if b >= 4 else []
            if b >= 1:
                slot_w.append(("sx", 128))
            dma("gpsimd", ring[:], w1h[:, b * 4096:(b + 1) * 4096],
                waits=[("sT", w1c_free[b % 4])] + slot_w)
            ready = ("gq", s.at("gq"))
            dma("scalar", w1scr[:, b * 4096:(b + 1) * 4096], ring[:], waits=[ready],
                sem="u%d" % (b % 4))
            return ready, ring
        if ring4:
            ring = (W1S[0], W1S[1], W1X[0], W1X[1])[b % 4]
            dma("sync", ring[:], w1scr[:, b * 4096:(b + 1) * 4096],
                waits=list(w1r_free[b % 4]) + [("u%d" % (b % 4), 32)])
            return ("sq", s.at("sq")), ring
        slot = b % 2
        src_w = [("u%d" % (b % 4), 16 * (b // 4 + 1))] if b % 4 >= 2 else []
        dma("sync", W1S[slot][:], w1scr[:, b * 4096:(b + 1) * 4096],
            waits=[("sT", w1_free[slot]), ("u%d" % slot, 32)] + src_w)
        return ("sq", s.at("sq")), W1S[slot]

    def load_w2(e2, cast_pass):
        slot = e2 % 2
        if cast_pass:
            slot_w = [("v%d" % slot, 16 * (e2 // 2))] if e2 >= 2 else []
            dma("gpsimd", W2S[slot][:], w2h[:, e2 * 4096:(e2 + 1) * 4096],
                waits=[("sT", w2_free[slot])] + slot_w)
            ready = ("gq", s.at("gq"))
            dma("scalar", w2scr[:, e2 * 4096:(e2 + 1) * 4096], W2S[slot][:], waits=[ready],
                sem="v%d" % slot)
            return ready, W2S[slot]
        dma("sync", W2S[slot][:], w2scr[:, e2 * 4096:(e2 + 1) * 4096],
            waits=[("sT", w2_free[slot]), ("v%d" % slot, 64)])
        return ("sq", s.at("sq")), W2S[slot]

    marks = {}

    def emit_ffn_ln_casts(t, xw, extra_stats_wait=(), sq_scalar=False):
        tok0 = t * TF
        for ec2 in range(EC):
            xs = X[:, ec2 * N + tok0: ec2 * N + tok0 + TF]
            op("vector", lambda e, o=LNA[:, ec2 * N + tok0: ec2 * N + tok0 + TF], i=xs:
               e.tensor_copy(o, i), waits=xw, inc="sV" if sq_scalar else None)
            if sq_scalar:
                op("scalar", lambda e, o=XSQ[:, ec2 * TF:(ec2 + 1) * TF], i=xs:
                   e.activation(o, i, AF.Square), waits=xw, inc="sS")
            else:
                op("vector", lambda e, o=XSQ[:, ec2 * TF:(ec2 + 1) * TF], i=xs:
                   e.tensor_mul(o, i, i), inc="sV")
        if sq_scalar:
            return (s.at("sV"), s.at("sS"))
        return s.at("sV")

    def emit_ffn_ln_stats(t, casts_done, extra_waits=()):
        tok0 = t * TF

        def stats_mm(e, t0=tok0):
            for c2 in range(EC):
                e.matmul(PS(4), ONES[:], LNA[:, c2 * N + t0: c2 * N + t0 + TF],
                         start=(c2 == 0), stop=(c2 == EC - 1))
            for c2 in range(EC):
                inst = e.matmul(PS(5), ONES[:], XSQ[:, c2 * TF:(c2 + 1) * TF],
                                start=(c2 == 0), stop=(c2 == EC - 1))
            return inst
        if isinstance(casts_done, list):
            cw = list(casts_done)
        elif isinstance(casts_done, tuple):
            cw = [("sV", casts_done[0]), ("sS", casts_done[1])]
        else:
            cw = [("sV", casts_done)]
        op("tensor", stats_mm, waits=cw + list(extra_waits), inc="sT")
        marks["ffn_stats_T"] = s.at("sT")
        return s.at("sT")

    def emit_ffn_ln_chain_norm(t, stats_T, chain_wait=None, bufs=None):
        tok0 = t * TF
        mb, rb, f1, f2 = bufs if bufs else (MBF[:], RBF[:], FT1[:], FT2[:])
        op("vector", lambda e: e.tensor_scalar_mul(mb, PS(4), 1.0 / E),
           waits=[("sT", stats_T)] + ([chain_wait] if chain_wait else []), inc="sV")
        op("scalar", lambda e: e.activation(f1, mb, AF.Square),
           waits=[("sV", s.at("sV"))], inc="sS")
        op("vector", lambda e: e.scalar_tensor_tensor(f2, PS(5), 1.0 / E, f1,
                                                      ALU.mult, ALU.subtract),
           waits=[("sS", s.at("sS"))], inc="sV")
        op("scalar", lambda e: e.activation(f1, f2, AF.Ln, bias=CV[:, 0:1]),
           waits=[("sV", s.at("sV"))])
        op("scalar", lambda e: e.activation(rb, f1, AF.Exp, scale=CV[:, 1:2]),
           inc="sS")
        nmarks = []
        for ec2 in range(EC):
            sl = LNA[:, ec2 * N + tok0: ec2 * N + tok0 + TF]
            op("vector", lambda e, a=sl, m2=mb: e.tensor_sub(a, a, m2),
               waits=[("sS", s.at("sS"))] if ec2 == 0 else ())
            op("vector", lambda e, a=sl, r2=rb: e.tensor_mul(a, a, r2), inc="sV")
            nmarks.append(s.at("sV"))
        return (s.at("sV"), nmarks)

    def emit_ffn_ln_rest(t, casts_done, extra_waits=(), chain_wait=None, bufs=None):
        stats_T = emit_ffn_ln_stats(t, casts_done, extra_waits)
        return emit_ffn_ln_chain_norm(t, stats_T, chain_wait, bufs)

    def ffn_pass(pass_idx, x_ready_waits, ln0_done=None):
        if pass_idx == 1:
            w1_free[0] = w1_free[1] = marks["uv_T"]
            w2_free[0] = w2_free[1] = marks["wo_T"]
            w1r_free[0] = [("sT", marks["uv_T"])]
            w1r_free[1] = [("sT", marks["uv_T"])]
            w1r_free[2] = [marks["ob_free"], ("sS", marks["osq_sS"]),
                           ("sT", marks["wo_T"])]
            w1r_free[3] = [("sT", marks["wo_T"])]

        def emit_ln_casts(t, first=False, guard_T=0):
            xw = (list(x_ready_waits) if (pass_idx == 1 or not first) else [x_q1]) \
                + [("sT", guard_T)]
            return emit_ffn_ln_casts(t, xw)

        def emit_ln_rest(t, casts_done):
            return emit_ffn_ln_rest(t, casts_done)

        if ln0_done is None:
            xw0 = (list(x_ready_waits) if pass_idx == 1 else [x_q1]) + [("sT", 0)]
            c0 = emit_ffn_ln_casts(0, xw0, sq_scalar=True)
            ln_done_t = [emit_ln_rest(0, c0)]
            op("scalar", lambda e: e.activation(DUMW[:], CV[:, 0:1], AF.Gelu))
        else:
            ln_done_t = [ln0_done]
        prev_y_done = s.at("sV")

        for t in range(NTF):
            tok0 = t * TF
            ln_done = ln_done_t[t]
            cast_pass = (pass_idx == 0 and t == 0)
            gelu_base = s.at("sS")
            ring4 = (pass_idx == 1 and t == 0)
            for b in range(EC):
                wready, w1t = load_w1(b, cast_pass, ring4=ring4)
                for jl in range(4):
                    j = b * 4 + jl
                    pbank = j % 4
                    waits = [wready]
                    if j < 4:
                        if pass_idx == 1 and t == 0:
                            waits.append(("sV", marks["ev_bank"][pbank]))
                        else:
                            waits.append(("sV", prev_y_done))
                        if t == 0 and pass_idx == 0:
                            waits.append(("sS", s.at("sS")))
                    if j >= 4:
                        waits.append(("sS", gelu_base + j - 3))

                    if j == 0:
                        for half in (0, 1):
                            def g1c(e, wt=w1t, hf=half, t0=tok0):
                                for c3 in range(hf * 4, hf * 4 + 4):
                                    inst = e.matmul(
                                        PS(0), wt[:, c3 * 512: c3 * 512 + 128],
                                        LNA[:, c3 * N + t0: c3 * N + t0 + TF],
                                        start=(c3 == 0), stop=(c3 == EC - 1))
                                return inst
                            w_ = (waits if half == 0 else [])                                 + [("sV", ln_done[1][half * 4 + 3])]
                            op("tensor", g1c, waits=w_,
                               inc="sT" if half == 1 else None)
                    else:
                        waits.append(("sV", ln_done[0]))

                        def g1(e, wt=w1t, jl2=jl, pb2=pbank, t0=tok0):
                            for c2 in range(EC):
                                inst = e.matmul(
                                    PS(pb2),
                                    wt[:, c2 * 512 + jl2 * 128: c2 * 512 + jl2 * 128 + 128],
                                    LNA[:, c2 * N + t0: c2 * N + t0 + TF],
                                    start=(c2 == 0), stop=(c2 == EC - 1))
                            return inst
                        op("tensor", g1, waits=waits, inc="sT")
                    op("scalar", lambda e, j2=j, pb2=pbank:
                       e.activation(HS[:, j2 * TF:(j2 + 1) * TF], PS(pb2), AF.Gelu,
                                    bias=B1[:, j2:j2 + 1]),
                       waits=[("sT", s.at("sT"))], inc="sS")
                if cast_pass:
                    w1c_free[b % 4] = s.at("sT")
                    if b % 4 in (0, 1):
                        w1_free[b % 2] = s.at("sT")
                elif ring4:
                    w1r_free[b % 4] = [("sT", s.at("sT"))]
                    if b % 4 in (0, 1):
                        w1_free[b % 2] = s.at("sT")
                else:
                    w1_free[b % 2] = s.at("sT")
            gelus_done = s.at("sS")
            if pass_idx == 0 and t == NTF - 1:
                marks["ffn1_g1_end"] = s.at("sT")

            if t + 1 < NTF:
                ln_done_t.append(emit_ln_rest(
                    t + 1, emit_ln_casts(t + 1, guard_T=marks.get("ffn_stats_T", 0))))

            for e2 in range(EC):
                xs = X[:, e2 * N + tok0: e2 * N + tok0 + TF]
                op("vector", lambda e, o=xs, e3=e2:
                   e.tensor_scalar_add(o, o, B2H[:, e3:e3 + 1]),
                   waits=[("sV", ln_done[0])] if e2 == 0 else ())
            y_evac_base = s.at("sV")
            for e2 in range(EC):
                wready, w2t = load_w2(e2, cast_pass)
                pbank = e2 % 4
                waits = [wready, ("sS", gelus_done)]
                if e2 >= 4:
                    waits.append(("sV", y_evac_base + e2 - 3))

                def g2(e, wt=w2t, pb2=pbank):
                    for jc2 in range(JC):
                        inst = e.matmul(PS(pb2), wt[:, jc2 * 128:(jc2 + 1) * 128],
                                        HS[:, jc2 * TF:(jc2 + 1) * TF],
                                        start=(jc2 == 0), stop=(jc2 == JC - 1))
                    return inst
                op("tensor", g2, waits=waits, inc="sT")
                xs = X[:, e2 * N + tok0: e2 * N + tok0 + TF]
                op("vector", lambda e, o=xs, pb2=pbank:
                   e.scalar_tensor_tensor(o, PS(pb2), 0.5, o, ALU.mult, ALU.add),
                   waits=[("sT", s.at("sT"))], inc="sV")
                w2_free[e2 % 2] = s.at("sT")
                if pass_idx == 1:
                    s.emit("scalar",
                           lambda e, e3=e2, t0=tok0: e.dma_start(
                               out=out_fm[e3 * 128:(e3 + 1) * 128, t0:t0 + TF],
                               in_=X[:, e3 * N + t0: e3 * N + t0 + TF]),
                           [("sV", s.at("sV"))], [("so", 16)])
                if pass_idx == 0 and t == NTF - 1 and e2 == 6:
                    marks["ffn1_g2_e6"] = s.at("sT")
            prev_y_done = s.at("sV")

    ffn_pass(0, [("sq", x_loaded)])
    ffn1_T, ffn1_V, ffn1_S = s.at("sT"), s.at("sV"), s.at("sS")

    sq_now = ("sq", s.at("sq"))
    stores_done = [("u0", 32), ("u1", 32), ("u2", 32), ("u3", 32),
                   ("v0", 64), ("v1", 64)]
    aw = {}
    dma("gpsimd", WQT[:], wqt[:],
        waits=[("sT", marks["ffn1_g1_end"]), sq_now] + stores_done)
    aw["wqt"] = s.at("gq")
    dma("gpsimd", WKT[:], wkt[:], waits=[("sT", marks["ffn1_g1_end"])])
    aw["wkt"] = s.at("gq")
    dma("gpsimd", WUT[:], wut[:], waits=[("sT", marks["ffn1_g1_end"]), sq_now])
    aw["wut"] = s.at("gq")
    dma("gpsimd", WVT[:], wvt[:], waits=[("sT", marks["ffn1_g2_e6"]), sq_now])
    aw["wvt"] = s.at("gq")
    dma("gpsimd", WOT[:], wot[:], waits=[("sT", ffn1_T), ("sV", ffn1_V), ("sS", ffn1_S)])
    aw["wot"] = s.at("gq")
    dma("sync", TRIU4[:], triu4_d[:])
    dma("sync", BVB[:], bvb_d[:])
    consts_a = s.at("sq")

    def emit_attn_xsq(w, x_waits, eng="vector"):
        tok0 = w * TW
        sem = {"gpsimd": "sw", "scalar": "sS"}.get(eng, "sV")
        for ec2 in range(EC):
            xs = X[:, ec2 * N + tok0: ec2 * N + tok0 + TW]
            if eng == "scalar":
                fn = lambda e, o=XSA[:, ec2 * TW:(ec2 + 1) * TW], i=xs:                     e.activation(o, i, AF.Square)
            else:
                fn = lambda e, o=XSA[:, ec2 * TW:(ec2 + 1) * TW], i=xs:                     e.tensor_mul(o, i, i)
            op(eng, fn,
               waits=(list(x_waits) + [("sT", s.at("sT"))]) if ec2 == 0 else (),
               inc=sem if ec2 == EC - 1 else None)
        return (sem, s.at(sem))

    def emit_attn_ln_casts(w, x_waits, eng="vector"):
        tok0 = w * TW
        sem = "sw" if eng == "gpsimd" else "sV"
        for ec2 in range(EC):
            xs = X[:, ec2 * N + tok0: ec2 * N + tok0 + TW]
            op(eng, lambda e, o=LNW[:, ec2 * TW:(ec2 + 1) * TW], i=xs:
               e.tensor_copy(o, i),
               waits=(list(x_waits) + [("sT", s.at("sT"))]) if ec2 == 0 else (),
               inc=sem if ec2 == EC - 1 else None)
        return (sem, s.at(sem))

    def emit_attn_ln_rest(w, casts_done, extra_waits=()):
        def stats_mm_a(e):
            for c2 in range(EC):
                e.matmul(PS(4, 0, TW), ONES[:], LNW[:, c2 * TW:(c2 + 1) * TW],
                         start=(c2 == 0), stop=(c2 == EC - 1))
            for c2 in range(EC):
                inst = e.matmul(PS(5, 0, TW), ONES[:], XSA[:, c2 * TW:(c2 + 1) * TW],
                                start=(c2 == 0), stop=(c2 == EC - 1))
            return inst
        op("tensor", stats_mm_a,
           waits=[casts_done, ("sS", s.at("sS"))] + list(extra_waits), inc="sT")
        op("vector", lambda e: e.tensor_scalar_mul(MBA[:], PS(4, 0, TW), 1.0 / E),
           waits=[("sT", s.at("sT"))], inc="sV")
        op("scalar", lambda e: e.activation(LNT1[:], MBA[:], AF.Square),
           waits=[("sV", s.at("sV"))], inc="sS")
        op("vector", lambda e: e.scalar_tensor_tensor(LNT1[:], PS(5, 0, TW), 1.0 / E, LNT1[:],
                                                      ALU.mult, ALU.subtract),
           waits=[("sS", s.at("sS"))], inc="sV")
        ln_bank_free = s.at("sV")
        op("scalar", lambda e: e.activation(LNT1[:], LNT1[:], AF.Ln, bias=CV[:, 0:1]),
           waits=[("sV", s.at("sV"))])
        op("scalar", lambda e: e.activation(RBA[:], LNT1[:], AF.Exp, scale=CV[:, 1:2]),
           inc="sS")
        chain_done = s.at("sS")
        return ln_bank_free, chain_done

    def emit_attn_ln_norm(chain_done):
        nmarks = []
        for ec2 in range(EC):
            sl = LNW[:, ec2 * TW:(ec2 + 1) * TW]
            op("vector", lambda e, a=sl: e.tensor_sub(a, a, MBA[:]),
               waits=[("sS", chain_done)] if ec2 == 0 else ())
            op("vector", lambda e, a=sl: e.tensor_mul(a, a, RBA[:]), inc="sV")
            nmarks.append(s.at("sV"))
        return (s.at("sV"), nmarks)

    op("scalar", lambda e: e.activation(DUMW[:], CV[:, 0:1], AF.Ln))
    emit_attn_xsq(0, [("sV", ffn1_V), ("sS", ffn1_S)], eng="scalar")
    cd0 = emit_attn_ln_casts(0, [("sV", ffn1_V), ("sS", ffn1_S)])
    ln_bank_free, _cd = emit_attn_ln_rest(0, cd0)
    lnw_done = emit_attn_ln_norm(_cd)
    wot_scaled = None

    def proj_pair_mm(wsb, pr, bank):
        hc0, hc1 = 2 * pr, 2 * pr + 1

        def f(e, wsb2=wsb, h0=hc0, h1=hc1, b2=bank):
            for sub, hcx in ((0, h0), (1, h1)):
                for c2 in range(EC):
                    inst = e.matmul(
                        PS(b2, sub * TW, TW),
                        wsb2[:, c2 * 1024 + hcx * 128: c2 * 1024 + hcx * 128 + 128],
                        LNW[:, c2 * TW:(c2 + 1) * TW],
                        start=(sub == 0 and c2 == 0), stop=(c2 == EC - 1),
                        skip_group_check=(sub == 1))
            return inst
        return f

    for w in range(NW):
        tok0 = w * TW

        elu_free = {}
        for pi, (wname, wsb, bsb, osb) in enumerate((("wqt", WQT, BQ, QF),
                                                     ("wkt", WKT, BK, KF))):
            for pr in range(4):
                gi = pi * 4 + pr
                bank = gi % 2
                if gi < 2:
                    first_w = [("gq", aw[wname]),
                               ("sV", marks.get(f"ev_b{bank}", ffn1_V))]
                    for sub in (0, 1):
                        hcx = 2 * pr + sub
                        for half in (0, 1):
                            def pmf(e, wsb2=wsb, h2=hcx, b2=bank, s2=sub, hf=half):
                                for c3 in range(hf * 4, hf * 4 + 4):
                                    inst = e.matmul(
                                        PS(b2, s2 * TW, TW),
                                        wsb2[:, c3 * 1024 + h2 * 128:
                                             c3 * 1024 + h2 * 128 + 128],
                                        LNW[:, c3 * TW:(c3 + 1) * TW],
                                        start=(s2 == 0 and c3 == 0),
                                        stop=(c3 == EC - 1),
                                        skip_group_check=(s2 == 1))
                                return inst
                            w_ = (first_w if (sub == 0 and half == 0) else [])                                 + [("sV", lnw_done[1][half * 4 + 3])]
                            op("tensor", pmf, waits=w_,
                               inc="sT" if (sub == 1 and half == 1) else None)
                else:
                    waits = [("gq", aw[wname]), ("sV", lnw_done[0]),
                             ("sV", elu_free[gi - 2])]
                    op("tensor", proj_pair_mm(wsb, pr, bank), waits=waits, inc="sT")
                pm = s.at("sT")
                hc0, hc1 = 2 * pr, 2 * pr + 1
                op("vector", lambda e, b2=bank, b3=bsb, h=hc0:
                   e.tensor_scalar_add(PEL2[:, 0:TW], PS(b2, 0, TW), b3[:, h:h + 1]),
                   waits=[("sT", pm)])
                op("vector", lambda e, b2=bank, b3=bsb, h=hc1:
                   e.tensor_scalar_add(PEL2[:, TW:2 * TW], PS(b2, TW, TW), b3[:, h:h + 1]),
                   inc="sV")
                elu_free[gi] = s.at("sV")
                op("vector", lambda e: e.tensor_single_scalar(T1EL[:], PEL2[:], 0.0, ALU.max))
                op("vector", lambda e: e.tensor_single_scalar(T2EL[:], PEL2[:], 0.0, ALU.min),
                   inc="sV")
                op("scalar", lambda e: e.activation(T2EL[:], T2EL[:], AF.Exp),
                   waits=[("sV", s.at("sV"))], inc="sS")
                op("vector", lambda e, o=osb[:, hc0 * TW: hc0 * TW + 2 * TW]:
                   e.scalar_tensor_tensor(o, T1EL[:], -1.0, T2EL[:], ALU.add, ALU.add),
                   waits=[("sS", s.at("sS"))], inc="sV")
        kf_done = s.at("sV")
        marks["qk_T"] = s.at("sT")

        v_evac_g = {}
        for c in range(2):
            for half in range(2):
                bank = c * 2 + half
                waits = [("gq", aw["wvt"]), ("sq", consts_a)]
                if bank in (0, 1):
                    waits.append(("sV", elu_free[6 + bank]))
                else:
                    waits.append(("sS", marks.get("b23_scal_free", 0)))
                    waits.append(("sV", marks.get("b01_vec_free", ffn1_V)))

                def vmm(e, c2=c, h2=half, b2=bank):
                    for e3 in range(EC):
                        inst = e.matmul(PS(b2),
                                        LNW[:, e3 * TW + c2 * 128: e3 * TW + c2 * 128 + 128],
                                        WVT[:, e3 * 1024 + h2 * 512: e3 * 1024 + h2 * 512 + 512],
                                        start=(e3 == 0), stop=(e3 == EC - 1))
                    return inst
                op("tensor", vmm, waits=waits, inc="sT")
                op("vector", lambda e, c2=c, h2=half, b2=bank:
                   e.tensor_tensor(VT[:, c2 * 1024 + h2 * 512: c2 * 1024 + h2 * 512 + 512],
                                   PS(b2), BVB[:, h2 * 512:(h2 + 1) * 512], ALU.add),
                   waits=[("sT", s.at("sT"))], inc="sV")
                v_evac_g[bank] = s.at("sV")
            op("scalar", lambda e, c2=c:
               e.activation(VT[:, c2 * 1024:(c2 + 1) * 1024], VT[:, c2 * 1024:(c2 + 1) * 1024],
                            AF.Silu),
               waits=[("sV", s.at("sV"))], inc="sS")
        v_done_S = s.at("sS")
        if w == 0:
            for hc in range(NH):
                op("vector", lambda e, h2=hc:
                   e.tensor_scalar_mul(WOT[:, h2 * 1024:(h2 + 1) * 1024],
                                       WOT[:, h2 * 1024:(h2 + 1) * 1024], SCL[:, h2:h2 + 1]),
                   waits=[("gq", aw["wot"]), ("sq", consts_a)] if hc == 0 else (),
                   inc="sV" if hc == NH - 1 else None)
            wot_scaled = ("sV", s.at("sV"))
        if w + 1 < NW:
            emit_attn_xsq(w + 1, [("sV", ffn1_V)], eng="gpsimd")

        u_free = {}

        def emit_u_pair(pr):
            bank = 2 + pr % 2
            waits = [("gq", aw["wut"]), ("sV", lnw_done[0]),
                     ("sV", v_evac_g[bank])]
            if pr >= 2:
                waits.append(("sS", u_free[pr - 2]))
            else:
                waits.append(("sS", marks.get("b23_scal_free", 0)))
            op("tensor", proj_pair_mm(WUT, pr, bank), waits=waits, inc="sT")
            pm = s.at("sT")
            for sub, hcx in ((0, 2 * pr), (1, 2 * pr + 1)):
                op("scalar", lambda e, b2=bank, sub2=sub, h=hcx:
                   e.activation(UB[:, h * TW:(h + 1) * TW], PS(b2, sub2 * TW, TW), AF.Silu,
                                bias=BU[:, h:h + 1]),
                   waits=[("sT", pm)], inc="sS")
            u_free[pr] = s.at("sS")

        g0, g1 = 2 * w, 2 * w + 1

        def ktr(c2, banks):
            def f(e, c3=c2, bk=banks):
                for h in range(NH):
                    inst = e.matmul(PS(bk[h // 4], (h % 4) * 128, 128),
                                    KF[:, h * TW + c3 * 128: h * TW + c3 * 128 + 128],
                                    ID[:], start=(h % 4 == 0), stop=True,
                                    skip_group_check=(h % 4 != 0))
                return inst
            return f

        ktrA_waits = [("sV", kf_done), marks.get("scan45_free", ("sS", 0))]
        if w == 0:
            ktrA_waits.append(("sq", consts_a))
        op("tensor", ktr(0, (4, 5)), waits=ktrA_waits, inc="sT")
        ktm = s.at("sT")
        op("scalar", lambda e: e.activation(KT[:, 0:512], PS(4), AF.Copy),
           waits=[("sT", ktm)])
        op("scalar", lambda e: e.activation(KT[:, 512:1024], PS(5), AF.Copy), inc="sS")
        kt0_done = ("sS", s.at("sS"))
        op("tensor", ktr(1, (0, 1)), waits=[("sV", v_evac_g[1])], inc="sT")
        ktm = s.at("sT")
        op("scalar", lambda e: e.activation(KT[:, 1024:1536], PS(0), AF.Copy),
           waits=[("sT", ktm)])
        op("scalar", lambda e: e.activation(KT[:, 1536:2048], PS(1), AF.Copy), inc="sS")
        kt1_done = ("sS", s.at("sS"))
        emit_u_pair(0)

        def imm(c2, banks):
            def f(e, c3=c2, bk=banks):
                for h in range(NH):
                    inst = e.matmul(PS(bk[h // 4], (h % 4) * 128, 128),
                                    KF[:, h * TW + c3 * 128: h * TW + c3 * 128 + 128],
                                    QF[:, h * TW + c3 * 128: h * TW + c3 * 128 + 128],
                                    start=(h % 4 == 0), stop=True,
                                    skip_group_check=(h % 4 != 0))
                return inst
            return f

        def omm(c2, g2, banks):
            def f(e, c3=c2, g3=g2, bk=banks):
                for h in range(NH):
                    inst = e.matmul(PS(bk[h // 4], (h % 4) * 128, 128),
                                    VT[:, c3 * 1024 + h * 128: c3 * 1024 + h * 128 + 128],
                                    ITR8[:, h * 128:(h + 1) * 128],
                                    start=(h % 4 == 0), stop=(g3 == 0),
                                    skip_group_check=(h % 4 != 0))
                    if g3 > 0:
                        inst = e.matmul(PS(bk[h // 4], (h % 4) * 128, 128),
                                        SSB[:, h * 128:(h + 1) * 128],
                                        QF[:, h * TW + c3 * 128: h * TW + c3 * 128 + 128],
                                        start=False, stop=True, skip_group_check=True)
                return inst
            return f

        def smm(c2, g2):
            def f(e, c3=c2, g3=g2):
                for h in range(NH):
                    inst = e.matmul(PS(6 + h // 4, (h % 4) * 128, 128),
                                    KT[:, c3 * 1024 + h * 128: c3 * 1024 + h * 128 + 128],
                                    VT[:, c3 * 1024 + h * 128: c3 * 1024 + h * 128 + 128],
                                    start=(g3 == 0 and h % 4 == 0), stop=True,
                                    skip_group_check=True)
                return inst
            return f

        def oevac(c2, banks, first_wait):
            for bi in range(2):
                src_ap = PS(banks[bi]).rearrange("p (h n) -> p h n", h=4)
                dst_ap = OB[:].rearrange("p (h n) -> p h n", h=8)[:, bi * 4:(bi + 1) * 4,
                                                                 c2 * 128:(c2 + 1) * 128]
                op("scalar", lambda e, si=src_ap, di_=dst_ap:
                   e.activation(di_, si, AF.Copy),
                   waits=[("sT", first_wait)] if bi == 0 else (),
                   inc="sS" if bi == 1 else None)
            return ("sS", s.at("sS"))

        op("tensor", imm(0, (4, 5)), waits=[kt0_done], inc="sT")
        im0 = s.at("sT")
        op("tensor", imm(1, (0, 1)), waits=[kt1_done, ("sV", v_evac_g[3])], inc="sT")
        im1 = s.at("sT")
        emit_u_pair(1)
        op("vector", lambda e: e.tensor_tensor(ITR8[:, 0:512], PS(4), TRIU4[:], ALU.mult),
           waits=[("sT", im0)])
        op("vector", lambda e: e.tensor_tensor(ITR8[:, 512:1024], PS(5), TRIU4[:], ALU.mult),
           inc="sV")
        mask0 = s.at("sV")
        op("tensor", omm(0, g0, (4, 5)),
           waits=[("sV", mask0), ("sS", max(v_done_S, marks.get("s_copy", 0)))],
           inc="sT")
        om0 = s.at("sT")
        marks["scan45_free"] = None
        o_evac_0 = oevac(0, (4, 5), om0)
        marks["scan45_free"] = o_evac_0
        op("tensor", smm(0, g0), waits=[("sS", max(v_done_S, marks.get("s_copy", 0)))], inc="sT")
        sm = s.at("sT")
        op("scalar", lambda e: e.activation(SSB[:, 0:512], PS(6), AF.Copy),
           waits=[("sT", sm)], inc="sS")
        op("scalar", lambda e: e.activation(SSB[:, 512:1024], PS(7), AF.Copy), inc="sS")
        marks["s_copy"] = s.at("sS")
        emit_u_pair(2)

        ln_next = None

        op("vector", lambda e: e.tensor_tensor(ITR8[:, 0:512], PS(0), TRIU4[:], ALU.mult),
           waits=[("sT", im1), ("sT", om0)])
        op("vector", lambda e: e.tensor_tensor(ITR8[:, 512:1024], PS(1), TRIU4[:], ALU.mult),
           inc="sV")
        mask1 = s.at("sV")
        op("tensor", omm(1, g1, (0, 1)),
           waits=[("sV", mask1), ("sS", marks["s_copy"])], inc="sT")
        om1 = s.at("sT")
        emit_u_pair(3)
        u_done = u_free[3]
        marks["uv_T"] = s.at("sT")
        ln_casts = (emit_attn_ln_casts(w + 1, [("sV", ffn1_V)], eng="gpsimd")
                    if w + 1 < NW else None)
        if w == NW - 1:
            c0xw = [("sV", marks["x_t01_final"]), ("sT", marks["qk_T"])]
            for ec2 in range(EC):
                op("gpsimd", lambda e, o=LNA[:, ec2 * N: ec2 * N + TF],
                   i=X[:, ec2 * N: ec2 * N + TF]: e.tensor_copy(o, i),
                   waits=c0xw if ec2 == 0 else (),
                   inc="sw" if ec2 == EC - 1 else None)
            marks["ffn2_c0_cp"] = ("sw", s.at("sw"))
        o_evac_1 = oevac(1, (0, 1), om1)
        marks["o_evac_1"] = o_evac_1
        if g1 < NCH - 1:
            op("tensor", smm(1, g1), waits=[("sS", marks["s_copy"])], inc="sT")
            sm = s.at("sT")
            op("scalar", lambda e: e.activation(SSB[:, 0:512], PS(6), AF.Copy),
               waits=[("sT", sm)], inc="sS")
            op("scalar", lambda e: e.activation(SSB[:, 512:1024], PS(7), AF.Copy), inc="sS")
            marks["s_copy"] = s.at("sS")

        op("scalar", lambda e: e.activation(DUMW[:], CV[:, 0:1], AF.Ln))
        if w == NW - 1:
            c0xw = [("sV", marks["x_t01_final"]), ("sT", marks["uv_T"])]
            for ec2 in range(4):
                op("scalar", lambda e, o=XSQ[:, ec2 * TF:(ec2 + 1) * TF],
                   i=X[:, ec2 * N: ec2 * N + TF]: e.activation(o, i, AF.Square),
                   waits=c0xw if ec2 == 0 else (),
                   inc="sS" if ec2 == 3 else None)
            marks["ffn2_c0"] = [marks["ffn2_c0_cp"], ("sS", s.at("sS"))]

        o_done = marks["o_evac_1"]
        op("vector", lambda e: e.tensor_mul(ZB[:], OB[:], UB[:]),
           waits=[o_done, ("sS", u_done)], inc="sV")
        zb_done = s.at("sV")
        marks["ob_free"] = ("sV", zb_done)
        if w == NW - 1:
            c0xw2 = [("sV", marks["x_t01_final"]), ("sT", marks["uv_T"])]
            for ec2 in range(4, EC):
                op("vector", lambda e, o=XSQ[:, ec2 * TF:(ec2 + 1) * TF],
                   i=X[:, ec2 * N: ec2 * N + TF]: e.tensor_mul(o, i, i),
                   waits=c0xw2 if ec2 == 4 else (),
                   inc="sV" if ec2 == EC - 1 else None)
            marks["ffn2_c0"].append(("sV", s.at("sV")))
        op("scalar", lambda e: e.activation(OSQ[:], OB[:], AF.Square),
           waits=[o_done, ("sT", s.at("sT"))], inc="sS")
        osq_done = s.at("sS")
        marks["osq_sS"] = osq_done

        def wo_mm(ea, eb, b2):
            def f(e, ea2=ea, eb2=eb, b3=b2):
                for sub, ex in ((0, ea2), (1, eb2)):
                    for h in range(NH):
                        inst = e.matmul(
                            PS(b3, sub * TW, TW),
                            WOT[:, h * 1024 + ex * 128: h * 1024 + ex * 128 + 128],
                            ZB[:, h * TW:(h + 1) * TW],
                            start=(sub == 0 and h == 0), stop=(h == NH - 1),
                            skip_group_check=(sub == 1))
                return inst
            return f

        def wo_evac(ea, eb, b2, wm, rstd_sS):
            for sub, ex in ((0, ea), (1, eb)):
                xs = X[:, ex * N + tok0: ex * N + tok0 + TW]
                op("vector", lambda e, b3=b2, sub2=sub:
                   e.tensor_mul(R2A[:], PS(b3, sub2 * TW, TW), R1A[:]),
                   waits=[("sT", wm), ("sS", rstd_sS)] if sub == 0 else ())
                op("vector", lambda e, o=xs, e3=ex:
                   e.scalar_tensor_tensor(o, R2A[:], BO[:, e3:e3 + 1], o, ALU.add, ALU.add),
                   inc="sV" if sub == 1 else None)
            return s.at("sV")

        op("tensor", wo_mm(0, 1, 0), waits=[("sV", zb_done)]
           + ([wot_scaled] if wot_scaled else []), inc="sT")
        wm0 = s.at("sT")
        ln_next = None
        if w + 1 < NW:
            ln_next = emit_attn_ln_rest(w + 1, ln_casts,
                                        extra_waits=[marks["scan45_free"]])
            marks["scan45_free"] = ("sV", ln_next[0])
        else:
            marks["ffn2_stats"] = emit_ffn_ln_stats(
                0, marks["ffn2_c0"],
                extra_waits=[marks["scan45_free"], ("sV", zb_done)])
        op("tensor", wo_mm(2, 3, 1), waits=[("sV", zb_done)], inc="sT")
        wm1 = s.at("sT")

        def rms_mm(e):
            for h in range(NH):
                inst = e.matmul(PS(2, 0, TW), ONES[:], OSQ[:, h * TW:(h + 1) * TW],
                                start=(h == 0), stop=(h == NH - 1))
            return inst
        rms_waits = [("sS", osq_done)]
        op("tensor", rms_mm, waits=rms_waits, inc="sT")
        op("scalar", lambda e: e.activation(R1A[:], PS(2, 0, TW), AF.Ln, bias=CV[:, 2:3]),
           waits=[("sT", s.at("sT"))], inc="sS")
        rms_ln1 = s.at("sS")
        op("scalar", lambda e: e.activation(R1A[:], R1A[:], AF.Exp, scale=CV[:, 3:4]))
        op("scalar", lambda e: e.activation(R1A[:], R1A[:], AF.Ln, bias=CV[:, 6:7]))
        op("scalar", lambda e: e.activation(R1A[:], R1A[:], AF.Exp, scale=CV[:, 4:5],
                                            bias=CV[:, 5:6]), inc="sS")
        rstd_sS = s.at("sS")
        marks["b23_scal_free"] = s.at("sS")

        op("tensor", wo_mm(4, 5, 3), waits=[("sV", zb_done)], inc="sT")
        wm2 = s.at("sT")
        op("tensor", wo_mm(6, 7, 2), waits=[("sV", zb_done), ("sS", rms_ln1)], inc="sT")
        wm3 = s.at("sT")
        marks["wo_T"] = wm3
        marks["ev_b0"] = wo_evac(0, 1, 0, wm0, rstd_sS)
        marks["ev_b1"] = wo_evac(2, 3, 1, wm1, rstd_sS)
        if ln_next is not None:
            lnw_done = emit_attn_ln_norm(ln_next[1])
        marks["ev_b3"] = wo_evac(4, 5, 3, wm2, rstd_sS)
        marks["ev_b2"] = wo_evac(6, 7, 2, wm3, rstd_sS)
        marks["ev_bank"] = {0: marks["ev_b0"], 1: marks["ev_b1"],
                            2: marks["ev_b2"], 3: marks["ev_b3"]}
        if ln_next is None:
            marks["ffn2_ln0"] = emit_ffn_ln_chain_norm(
                0, marks["ffn2_stats"],
                bufs=(MBF2[:], RBF2[:], BVB[:, 0:512], BVB[:, 512:1024]))
        marks["b01_vec_free"] = s.at("sV")
        if w == 1:
            marks["x_t01_final"] = s.at("sV")


    attn_V = s.at("sV")

    ffn_pass(1, [("sV", attn_V), ("sS", s.at("sS"))], ln0_done=marks["ffn2_ln0"])

    so_total = s.at("so")
    s.emit("sync", lambda e: e.wait_ge(sems["so"], so_total), [], [])

    with nc.Block() as blk:
        def make_body(eng_name):
            def body(eng):
                last_wait = {}
                for fn, waits, incs in s.items[eng_name]:
                    for wsem, wval in waits:
                        if wval <= 0 or last_wait.get(wsem, -1) >= wval:
                            continue
                        eng.wait_ge(sems[wsem], wval)
                        last_wait[wsem] = wval
                    inst = fn(eng)
                    for isem, amt in incs:
                        inst = inst.then_inc(sems[isem], amt)
            return body
        blk.sync(make_body("sync"))
        blk.gpsimd(make_body("gpsimd"))
        blk.tensor(make_body("tensor"))
        blk.vector(make_body("vector"))
        blk.scalar(make_body("scalar"))

    return nc


def _host_inputs(Wq, bq, Wk, bk, Wv, bv, Wu, bu, Wo, bo, scale, w1, b1, w2, b2):
    asf = lambda a: np.ascontiguousarray(np.asarray(a, dtype=np.float32))
    w1 = asf(w1); w2 = asf(w2)
    fmw = lambda W: np.ascontiguousarray(
        asf(W).T.reshape(8, 128, 1024).transpose(1, 0, 2).reshape(128, 8192))
    triu = np.triu(np.ones((128, 128), dtype=np.float32))
    scl = np.ascontiguousarray(asf(scale).reshape(8, 128).T)
    return {
        "w1h": np.ascontiguousarray(w1.reshape(8, 128, 8, 512).transpose(1, 2, 0, 3).reshape(128, 32768)),
        "w2h": np.ascontiguousarray(w2.reshape(32, 128, 8, 128).transpose(1, 2, 0, 3).reshape(128, 32768)),
        "wqt": fmw(Wq), "wkt": fmw(Wk), "wut": fmw(Wu), "wvt": fmw(Wv), "wot": fmw(Wo),
        "ident": np.eye(128, dtype=BF),
        "ones": np.ones((128, 128), dtype=BF),
        "triu4": np.ascontiguousarray(np.tile(triu, (1, 4))),
        "b1l": np.ascontiguousarray(asf(b1).reshape(32, 128).T),
        "b2l": np.ascontiguousarray(asf(b2).reshape(8, 128).T),
        "bql": np.ascontiguousarray(asf(bq).reshape(8, 128).T),
        "bkl": np.ascontiguousarray(asf(bk).reshape(8, 128).T),
        "bul": np.ascontiguousarray(asf(bu).reshape(8, 128).T),
        "bol": np.ascontiguousarray(asf(bo).reshape(8, 128).T),
        "scl": scl,
        "bvb": np.ascontiguousarray(np.broadcast_to(asf(bv), (128, 1024))),
        "cvec": np.ascontiguousarray(np.tile(np.array(
            [1e-5, -0.5, 1e-30, 0.5, -1.0, np.log(32.0), 3.2e-7, 0.0],
            dtype=np.float32), (128, 1))),
    }


def kernel(L0_attn_x, L0_attn_Wq, L0_attn_bq, L0_attn_Wk, L0_attn_bk, L0_attn_Wv,
           L0_attn_bv, L0_attn_Wu, L0_attn_bu, L0_attn_Wo, L0_attn_bo, L0_attn_scale,
           L0_ffn_w1, L0_ffn_b1, L0_ffn_w2, L0_ffn_b2):
    x = np.asarray(L0_attn_x, dtype=np.float32)
    com = _host_inputs(L0_attn_Wq, L0_attn_bq, L0_attn_Wk, L0_attn_bk, L0_attn_Wv,
                       L0_attn_bv, L0_attn_Wu, L0_attn_bu, L0_attn_Wo, L0_attn_bo,
                       L0_attn_scale, L0_ffn_w1, L0_ffn_b1, L0_ffn_w2, L0_ffn_b2)
    nc = build_program()
    in_maps = []
    for b in range(8):
        m = dict(com)
        m["x_fm"] = np.ascontiguousarray(x[b].T)
        in_maps.append(m)
    res = run_bass_kernel_spmd(nc, in_maps, core_ids=list(range(8)))
    out = np.stack([np.asarray(r["out_fm"], dtype=np.float32).T for r in res.results])
    return np.ascontiguousarray(out)


if __name__ == "__main__":
    nc = build_program()
    print("built ok, instructions:", sum(1 for _ in nc.all_instructions()))


# version 50
# speedup vs baseline: 1.0358x; 1.0341x over previous
import numpy as np
import ml_dtypes
import concourse.bass as bass
import concourse.mybir as mybir
from concourse.bass_utils import run_bass_kernel_spmd

F32 = mybir.dt.float32
BF16 = mybir.dt.bfloat16
AF = mybir.ActivationFunctionType
ALU = mybir.AluOpType
BF = ml_dtypes.bfloat16

E = 1024
N = 2048
EC = 8
JC = 32
NH = 8
NCH = 16
TF = 512
NTF = 4
TW = 256
NW = 8

ENGS = ("sync", "gpsimd", "tensor", "vector", "scalar")


class Sched:
    def __init__(self):
        self.items = {e: [] for e in ENGS}
        self.cnt = {}

    def sem(self, name):
        self.cnt[name] = 0

    def at(self, sem):
        return self.cnt[sem]

    def emit(self, eng, fn, waits=(), incs=()):
        for sname, a in incs:
            self.cnt[sname] += a
        self.items[eng].append((fn, [w for w in waits if w is not None], list(incs)))


def build_program():
    nc = bass.Bass("TRN2", target_bir_lowering=False)

    di = lambda name, shape, dt: nc.dram_tensor(name, shape, dt, kind="ExternalInput")
    x_fm = di("x_fm", [E, N], F32)
    w1h = di("w1h", [128, 32768], F32)
    w2h = di("w2h", [128, 32768], F32)
    wqt = di("wqt", [128, 8192], F32)
    wkt = di("wkt", [128, 8192], F32)
    wut = di("wut", [128, 8192], F32)
    wvt = di("wvt", [128, 8192], F32)
    wot = di("wot", [128, 8192], F32)
    ident_d = di("ident", [128, 128], BF16)
    ones_d = di("ones", [128, 128], BF16)
    triu4_d = di("triu4", [128, 512], F32)
    b1_d = di("b1l", [128, 32], F32)
    b2_d = di("b2l", [128, 8], F32)
    bq_d = di("bql", [128, 8], F32)
    bk_d = di("bkl", [128, 8], F32)
    bu_d = di("bul", [128, 8], F32)
    bo_d = di("bol", [128, 8], F32)
    scl_d = di("scl", [128, 8], F32)
    bvb_d = di("bvb", [128, 1024], F32)
    cvec_d = di("cvec", [128, 8], F32)
    out_fm = nc.dram_tensor("out_fm", [E, N], F32, kind="ExternalOutput")
    w1scr = nc.dram_tensor("w1scr", [128, 32768], BF16, kind="Internal")
    w2scr = nc.dram_tensor("w2scr", [128, 32768], BF16, kind="Internal")

    sbt = lambda name, shape, dt: nc.alloc_sbuf_tensor(name, shape, dt)
    X = sbt("X", [128, EC * N], F32)
    ID = sbt("ID", [128, 128], BF16)
    ONES = sbt("ONES", [128, 128], BF16)
    B1 = sbt("B1", [128, 32], F32)
    B2 = sbt("B2", [128, 8], F32)
    B2H = sbt("B2H", [128, 8], F32)
    BQ = sbt("BQ", [128, 8], F32)
    BK = sbt("BK", [128, 8], F32)
    BU = sbt("BU", [128, 8], F32)
    BO = sbt("BO", [128, 8], F32)
    CV = sbt("CV", [128, 8], F32)
    DUMW = sbt("DUMW", [128, 1], F32)

    SCRATCH_BYTES = 145408
    base, _ = nc.bump_sbuf(SCRATCH_BYTES)
    at = lambda name, off, shape, dt: nc.alloc_sbuf_tensor_at(name, shape, dt, offset=base + off)
    HS = at("HS", 0, [128, JC * TF], BF16)
    LNA = at("LNA", 32768, [128, EC * N], BF16)
    XSQ = at("XSQ", 65536, [128, EC * TF], BF16)
    W1S = [at(f"W1S{i}", 73728 + i * 8192, [128, 4096], BF16) for i in range(2)]
    W2S = [at(f"W2S{i}", 90112 + i * 8192, [128, 4096], BF16) for i in range(2)]
    MBF = at("MBF", 106496, [128, TF], BF16)
    RBF = at("RBF", 107520, [128, TF], BF16)
    FT1 = at("FT1", 108544, [128, TF], F32)
    FT2 = at("FT2", 110592, [128, TF], F32)
    LNW = at("LNW", 0, [128, EC * TW], BF16)
    ZB = at("ZB", 4096, [128, NH * TW], BF16)
    QF = at("QF", 8192, [128, NH * TW], BF16)
    KF = at("KF", 12288, [128, NH * TW], BF16)
    UB = at("UB", 16384, [128, NH * TW], BF16)
    VT = at("VT", 20480, [128, 2 * 1024], BF16)
    KT = at("KT", 24576, [128, 2 * 1024], BF16)
    XSA = at("XSA", 28672, [128, EC * TW], BF16)
    WQT = at("WQT", 32768, [128, 8192], BF16)
    WKT = at("WKT", 49152, [128, 8192], BF16)
    WUT = at("WUT", 65536, [128, 8192], BF16)
    WVT = at("WVT", 81920, [128, 8192], BF16)
    WOT = at("WOT", 98304, [128, 8192], BF16)
    SCL = sbt("SCL", [128, 8], F32)
    OB = at("OB", 118784, [128, NH * TW], BF16)
    OSQ = at("OSQ", 122880, [128, NH * TW], BF16)
    SSB = at("SSB", 126976, [128, NH * 128], BF16)
    ITR8 = at("ITR8", 129024, [128, 1024], BF16)
    PEL2 = at("PEL2", 131072, [128, 512], F32)
    T1EL = at("T1EL", 133120, [128, 512], F32)
    T2EL = at("T2EL", 135168, [128, 512], F32)
    US = at("US", 131072, [128, NH * TW], BF16)
    MBA = at("MBA", 135168, [128, TW], BF16)
    RBA = at("RBA", 135680, [128, TW], BF16)
    LNT1 = at("LNT1", 136192, [128, TW], F32)
    R1A = at("R1A", 137216, [128, TW], F32)
    R2A = at("R2A", 138240, [128, TW], F32)
    TRIU4 = at("TRIU4", 139264, [128, 512], F32)
    BVB = at("BVB", 141312, [128, 1024], F32)
    MBF2 = at("MBF2", 139264, [128, 512], BF16)
    RBF2 = at("RBF2", 140288, [128, 512], BF16)

    W1X = [at(f"W1X{i}", 118784 + i * 8192, [128, 4096], BF16) for i in range(2)]

    PSA = nc.alloc_psum_tensor("PSA", [128, 4096], F32)
    PS = lambda bank, off=0, width=512: PSA[:, bank * 512 + off: bank * 512 + off + width]

    s = Sched()
    sems = {}
    for name in ("sq", "gq", "sT", "sV", "sS", "so", "sx", "sw",
                 "u0", "u1", "u2", "u3", "v0", "v1"):
        sems[name] = nc.alloc_semaphore(name)
        s.sem(name)

    def dma(eng, out, in_, waits=(), sem=None):
        name = sem or ("gq" if eng == "gpsimd" else "sq")
        s.emit(eng, lambda e, o=out, i=in_: e.dma_start(out=o, in_=i), waits, [(name, 16)])

    def op(eng, fn, waits=(), inc=None):
        s.emit(eng, fn, waits, [(inc, 1)] if inc else [])

    for dst, src in ((ID, ident_d), (ONES, ones_d), (B1, b1_d),
                     (B2, b2_d), (BQ, bq_d), (BK, bk_d), (BU, bu_d), (BO, bo_d),
                     (CV, cvec_d), (SCL, scl_d)):
        dma("sync", dst[:], src[:])
    n_const = s.at("sq")
    for ec in range(EC):
        s.emit("sync", lambda e, o=X[:, ec * N: ec * N + TF],
               i=x_fm[ec * 128:(ec + 1) * 128, 0:TF]: e.dma_start(out=o, in_=i),
               (), [("sx", 16)])
    x_q1 = ("sx", s.at("sx"))
    for ec in range(EC):
        dma("sync", X[:, ec * N + TF:(ec + 1) * N], x_fm[ec * 128:(ec + 1) * 128, TF:N],
            waits=[x_q1] if ec == 0 else ())
    x_loaded = s.at("sq")
    op("vector", lambda e: e.tensor_scalar_mul(B2H[:], B2[:], 0.5),
       waits=[("sq", n_const)], inc="sV")
    op("scalar", lambda e: e.activation(DUMW[:], CV[:, 0:1], AF.Ln),
       waits=[("sq", n_const)])

    w1_free = [0, 0]
    w2_free = [0, 0]
    w1r_free = [[("sT", 0)] for _ in range(4)]
    w1c_free = [0, 0, 0, 0]
    w2c_free = [0, 0, 0, 0]
    w1_scr_sq = [0] * 8
    w2_scr_sq = [0] * 8

    def load_w1(b, cast_pass, ring4=False):
        if cast_pass:
            ring = (W1S[0], W1S[1], W1X[0], W1X[1])[b % 4]
            slot_w = [("u%d" % (b % 4), 16 * (b // 4))] if b >= 4 else []
            if b >= 1:
                slot_w.append(("sx", 128))
            dma("gpsimd", ring[:], w1h[:, b * 4096:(b + 1) * 4096],
                waits=[("sT", w1c_free[b % 4])] + slot_w)
            ready = ("gq", s.at("gq"))
            dma("scalar", w1scr[:, b * 4096:(b + 1) * 4096], ring[:], waits=[ready],
                sem="u%d" % (b % 4))
            return ready, ring
        if ring4:
            ring = (W1S[0], W1S[1], W1X[0], W1X[1])[b % 4]
            dma("sync", ring[:], w1scr[:, b * 4096:(b + 1) * 4096],
                waits=list(w1r_free[b % 4]) + [("u%d" % (b % 4), 32)])
            return ("sq", s.at("sq")), ring
        slot = b % 2
        src_w = [("u%d" % (b % 4), 16 * (b // 4 + 1))] if b % 4 >= 2 else []
        dma("sync", W1S[slot][:], w1scr[:, b * 4096:(b + 1) * 4096],
            waits=[("sT", w1_free[slot]), ("u%d" % slot, 32)] + src_w)
        return ("sq", s.at("sq")), W1S[slot]

    def load_w2(e2, cast_pass):
        slot = e2 % 2
        if cast_pass:
            slot_w = [("v%d" % slot, 16 * (e2 // 2))] if e2 >= 2 else []
            dma("gpsimd", W2S[slot][:], w2h[:, e2 * 4096:(e2 + 1) * 4096],
                waits=[("sT", w2_free[slot])] + slot_w)
            ready = ("gq", s.at("gq"))
            dma("scalar", w2scr[:, e2 * 4096:(e2 + 1) * 4096], W2S[slot][:], waits=[ready],
                sem="v%d" % slot)
            return ready, W2S[slot]
        dma("sync", W2S[slot][:], w2scr[:, e2 * 4096:(e2 + 1) * 4096],
            waits=[("sT", w2_free[slot]), ("v%d" % slot, 64)])
        return ("sq", s.at("sq")), W2S[slot]

    marks = {}

    def emit_ffn_ln_casts(t, xw, extra_stats_wait=(), sq_scalar=False):
        tok0 = t * TF
        for ec2 in range(EC):
            xs = X[:, ec2 * N + tok0: ec2 * N + tok0 + TF]
            op("vector", lambda e, o=LNA[:, ec2 * N + tok0: ec2 * N + tok0 + TF], i=xs:
               e.tensor_copy(o, i), waits=xw, inc="sV" if sq_scalar else None)
            if sq_scalar:
                op("scalar", lambda e, o=XSQ[:, ec2 * TF:(ec2 + 1) * TF], i=xs:
                   e.activation(o, i, AF.Square), waits=xw, inc="sS")
            else:
                op("vector", lambda e, o=XSQ[:, ec2 * TF:(ec2 + 1) * TF], i=xs:
                   e.tensor_mul(o, i, i), inc="sV")
        if sq_scalar:
            return (s.at("sV"), s.at("sS"))
        return s.at("sV")

    def emit_ffn_ln_stats(t, casts_done, extra_waits=()):
        tok0 = t * TF

        def stats_mm(e, t0=tok0):
            for c2 in range(EC):
                e.matmul(PS(4), ONES[:], LNA[:, c2 * N + t0: c2 * N + t0 + TF],
                         start=(c2 == 0), stop=(c2 == EC - 1))
            for c2 in range(EC):
                inst = e.matmul(PS(5), ONES[:], XSQ[:, c2 * TF:(c2 + 1) * TF],
                                start=(c2 == 0), stop=(c2 == EC - 1))
            return inst
        if isinstance(casts_done, list):
            cw = list(casts_done)
        elif isinstance(casts_done, tuple):
            cw = [("sV", casts_done[0]), ("sS", casts_done[1])]
        else:
            cw = [("sV", casts_done)]
        op("tensor", stats_mm, waits=cw + list(extra_waits), inc="sT")
        marks["ffn_stats_T"] = s.at("sT")
        return s.at("sT")

    def emit_ffn_ln_chain_norm(t, stats_T, chain_wait=None, bufs=None):
        tok0 = t * TF
        mb, rb, f1, f2 = bufs if bufs else (MBF[:], RBF[:], FT1[:], FT2[:])
        op("vector", lambda e: e.tensor_scalar_mul(mb, PS(4), 1.0 / E),
           waits=[("sT", stats_T)] + ([chain_wait] if chain_wait else []), inc="sV")
        op("scalar", lambda e: e.activation(f1, mb, AF.Square),
           waits=[("sV", s.at("sV"))], inc="sS")
        op("vector", lambda e: e.scalar_tensor_tensor(f2, PS(5), 1.0 / E, f1,
                                                      ALU.mult, ALU.subtract),
           waits=[("sS", s.at("sS"))], inc="sV")
        op("scalar", lambda e: e.activation(f1, f2, AF.Ln, bias=CV[:, 0:1]),
           waits=[("sV", s.at("sV"))])
        op("scalar", lambda e: e.activation(rb, f1, AF.Exp, scale=CV[:, 1:2]),
           inc="sS")
        nmarks = []
        for ec2 in range(EC):
            sl = LNA[:, ec2 * N + tok0: ec2 * N + tok0 + TF]
            op("vector", lambda e, a=sl, m2=mb: e.tensor_sub(a, a, m2),
               waits=[("sS", s.at("sS"))] if ec2 == 0 else ())
            op("vector", lambda e, a=sl, r2=rb: e.tensor_mul(a, a, r2), inc="sV")
            nmarks.append(s.at("sV"))
        return (s.at("sV"), nmarks)

    def emit_ffn_ln_rest(t, casts_done, extra_waits=(), chain_wait=None, bufs=None):
        stats_T = emit_ffn_ln_stats(t, casts_done, extra_waits)
        return emit_ffn_ln_chain_norm(t, stats_T, chain_wait, bufs)

    def ffn_pass(pass_idx, x_ready_waits, ln0_done=None):
        if pass_idx == 1:
            w1_free[0] = w1_free[1] = marks["uv_T"]
            w2_free[0] = w2_free[1] = marks["wo_T"]
            w1r_free[0] = [("sT", marks["uv_T"])]
            w1r_free[1] = [("sT", marks["uv_T"])]
            w1r_free[2] = [marks["ob_free"], ("sS", marks["osq_sS"]),
                           ("sT", marks["wo_T"])]
            w1r_free[3] = [("sT", marks["wo_T"])]

        def emit_ln_casts(t, first=False, guard_T=0):
            xw = (list(x_ready_waits) if (pass_idx == 1 or not first) else [x_q1]) \
                + [("sT", guard_T)]
            return emit_ffn_ln_casts(t, xw)

        def emit_ln_rest(t, casts_done):
            return emit_ffn_ln_rest(t, casts_done)

        if ln0_done is None:
            xw0 = (list(x_ready_waits) if pass_idx == 1 else [x_q1]) + [("sT", 0)]
            c0 = emit_ffn_ln_casts(0, xw0, sq_scalar=True)
            ln_done_t = [emit_ln_rest(0, c0)]
            op("scalar", lambda e: e.activation(DUMW[:], CV[:, 0:1], AF.Gelu))
        else:
            ln_done_t = [ln0_done]
        prev_y_done = s.at("sV")

        for t in range(NTF):
            tok0 = t * TF
            ln_done = ln_done_t[t]
            cast_pass = (pass_idx == 0 and t == 0)
            gelu_base = s.at("sS")
            ring4 = (pass_idx == 1 and t == 0)
            for b in range(EC):
                wready, w1t = load_w1(b, cast_pass, ring4=ring4)
                for jl in range(4):
                    j = b * 4 + jl
                    pbank = j % 4
                    waits = [wready]
                    if j < 4:
                        if pass_idx == 1 and t == 0:
                            waits.append(("sV", marks["ev_bank"][pbank]))
                        else:
                            waits.append(("sV", prev_y_done))
                        if t == 0 and pass_idx == 0:
                            waits.append(("sS", s.at("sS")))
                    if j >= 4:
                        waits.append(("sS", gelu_base + j - 3))

                    if j == 0:
                        for half in (0, 1):
                            def g1c(e, wt=w1t, hf=half, t0=tok0):
                                for c3 in range(hf * 4, hf * 4 + 4):
                                    inst = e.matmul(
                                        PS(0), wt[:, c3 * 512: c3 * 512 + 128],
                                        LNA[:, c3 * N + t0: c3 * N + t0 + TF],
                                        start=(c3 == 0), stop=(c3 == EC - 1))
                                return inst
                            w_ = (waits if half == 0 else [])                                 + [("sV", ln_done[1][half * 4 + 3])]
                            op("tensor", g1c, waits=w_,
                               inc="sT" if half == 1 else None)
                    else:
                        waits.append(("sV", ln_done[0]))

                        def g1(e, wt=w1t, jl2=jl, pb2=pbank, t0=tok0):
                            for c2 in range(EC):
                                inst = e.matmul(
                                    PS(pb2),
                                    wt[:, c2 * 512 + jl2 * 128: c2 * 512 + jl2 * 128 + 128],
                                    LNA[:, c2 * N + t0: c2 * N + t0 + TF],
                                    start=(c2 == 0), stop=(c2 == EC - 1))
                            return inst
                        op("tensor", g1, waits=waits, inc="sT")
                    op("scalar", lambda e, j2=j, pb2=pbank:
                       e.activation(HS[:, j2 * TF:(j2 + 1) * TF], PS(pb2), AF.Gelu,
                                    bias=B1[:, j2:j2 + 1]),
                       waits=[("sT", s.at("sT"))], inc="sS")
                if cast_pass:
                    w1c_free[b % 4] = s.at("sT")
                    if b % 4 in (0, 1):
                        w1_free[b % 2] = s.at("sT")
                elif ring4:
                    w1r_free[b % 4] = [("sT", s.at("sT"))]
                    if b % 4 in (0, 1):
                        w1_free[b % 2] = s.at("sT")
                else:
                    w1_free[b % 2] = s.at("sT")
            gelus_done = s.at("sS")
            if pass_idx == 0 and t == NTF - 1:
                marks["ffn1_g1_end"] = s.at("sT")

            if t + 1 < NTF:
                ln_done_t.append(emit_ln_rest(
                    t + 1, emit_ln_casts(t + 1, guard_T=marks.get("ffn_stats_T", 0))))

            for e2 in range(EC):
                xs = X[:, e2 * N + tok0: e2 * N + tok0 + TF]
                op("vector", lambda e, o=xs, e3=e2:
                   e.tensor_scalar_add(o, o, B2H[:, e3:e3 + 1]),
                   waits=[("sV", ln_done[0])] if e2 == 0 else ())
            y_evac_base = s.at("sV")
            for e2 in range(EC):
                wready, w2t = load_w2(e2, cast_pass)
                pbank = e2 % 4
                waits = [wready, ("sS", gelus_done)]
                if e2 >= 4:
                    waits.append(("sV", y_evac_base + e2 - 3))

                def g2(e, wt=w2t, pb2=pbank):
                    for jc2 in range(JC):
                        inst = e.matmul(PS(pb2), wt[:, jc2 * 128:(jc2 + 1) * 128],
                                        HS[:, jc2 * TF:(jc2 + 1) * TF],
                                        start=(jc2 == 0), stop=(jc2 == JC - 1))
                    return inst
                op("tensor", g2, waits=waits, inc="sT")
                xs = X[:, e2 * N + tok0: e2 * N + tok0 + TF]
                op("vector", lambda e, o=xs, pb2=pbank:
                   e.scalar_tensor_tensor(o, PS(pb2), 0.5, o, ALU.mult, ALU.add),
                   waits=[("sT", s.at("sT"))], inc="sV")
                w2_free[e2 % 2] = s.at("sT")
                if pass_idx == 1:
                    s.emit("scalar",
                           lambda e, e3=e2, t0=tok0: e.dma_start(
                               out=out_fm[e3 * 128:(e3 + 1) * 128, t0:t0 + TF],
                               in_=X[:, e3 * N + t0: e3 * N + t0 + TF]),
                           [("sV", s.at("sV"))], [("so", 16)])
                if pass_idx == 0 and t == NTF - 1 and e2 == 6:
                    marks["ffn1_g2_e6"] = s.at("sT")
            prev_y_done = s.at("sV")

    ffn_pass(0, [("sq", x_loaded)])
    ffn1_T, ffn1_V, ffn1_S = s.at("sT"), s.at("sV"), s.at("sS")

    sq_now = ("sq", s.at("sq"))
    stores_done = [("u0", 32), ("u1", 32), ("u2", 32), ("u3", 32),
                   ("v0", 64), ("v1", 64)]
    aw = {}
    dma("gpsimd", WQT[:], wqt[:],
        waits=[("sT", marks["ffn1_g1_end"]), sq_now] + stores_done)
    aw["wqt"] = s.at("gq")
    dma("gpsimd", WKT[:], wkt[:], waits=[("sT", marks["ffn1_g1_end"])])
    aw["wkt"] = s.at("gq")
    dma("gpsimd", WUT[:], wut[:], waits=[("sT", marks["ffn1_g1_end"]), sq_now])
    aw["wut"] = s.at("gq")
    dma("gpsimd", WVT[:], wvt[:], waits=[("sT", marks["ffn1_g2_e6"]), sq_now])
    aw["wvt"] = s.at("gq")
    dma("gpsimd", WOT[:], wot[:], waits=[("sT", ffn1_T), ("sV", ffn1_V), ("sS", ffn1_S)])
    aw["wot"] = s.at("gq")
    dma("sync", TRIU4[:], triu4_d[:])
    dma("sync", BVB[:], bvb_d[:])
    consts_a = s.at("sq")

    def emit_attn_xsq(w, x_waits, eng="vector"):
        tok0 = w * TW
        sem = {"gpsimd": "sw", "scalar": "sS"}.get(eng, "sV")
        for ec2 in range(EC):
            xs = X[:, ec2 * N + tok0: ec2 * N + tok0 + TW]
            if eng == "scalar":
                fn = lambda e, o=XSA[:, ec2 * TW:(ec2 + 1) * TW], i=xs:                     e.activation(o, i, AF.Square)
            else:
                fn = lambda e, o=XSA[:, ec2 * TW:(ec2 + 1) * TW], i=xs:                     e.tensor_mul(o, i, i)
            op(eng, fn,
               waits=(list(x_waits) + [("sT", s.at("sT"))]) if ec2 == 0 else (),
               inc=sem if ec2 == EC - 1 else None)
        return (sem, s.at(sem))

    def emit_attn_ln_casts(w, x_waits, eng="vector"):
        tok0 = w * TW
        sem = "sw" if eng == "gpsimd" else "sV"
        for ec2 in range(EC):
            xs = X[:, ec2 * N + tok0: ec2 * N + tok0 + TW]
            op(eng, lambda e, o=LNW[:, ec2 * TW:(ec2 + 1) * TW], i=xs:
               e.tensor_copy(o, i),
               waits=(list(x_waits) + [("sT", s.at("sT"))]) if ec2 == 0 else (),
               inc=sem if ec2 == EC - 1 else None)
        return (sem, s.at(sem))

    def emit_attn_ln_rest(w, casts_done, extra_waits=()):
        def stats_mm_a(e):
            for c2 in range(EC):
                e.matmul(PS(4, 0, TW), ONES[:], LNW[:, c2 * TW:(c2 + 1) * TW],
                         start=(c2 == 0), stop=(c2 == EC - 1))
            for c2 in range(EC):
                inst = e.matmul(PS(5, 0, TW), ONES[:], XSA[:, c2 * TW:(c2 + 1) * TW],
                                start=(c2 == 0), stop=(c2 == EC - 1))
            return inst
        op("tensor", stats_mm_a,
           waits=[casts_done, ("sS", s.at("sS"))] + list(extra_waits), inc="sT")
        op("vector", lambda e: e.tensor_scalar_mul(MBA[:], PS(4, 0, TW), 1.0 / E),
           waits=[("sT", s.at("sT"))], inc="sV")
        op("scalar", lambda e: e.activation(LNT1[:], MBA[:], AF.Square),
           waits=[("sV", s.at("sV"))], inc="sS")
        op("vector", lambda e: e.scalar_tensor_tensor(LNT1[:], PS(5, 0, TW), 1.0 / E, LNT1[:],
                                                      ALU.mult, ALU.subtract),
           waits=[("sS", s.at("sS"))], inc="sV")
        ln_bank_free = s.at("sV")
        op("scalar", lambda e: e.activation(LNT1[:], LNT1[:], AF.Ln, bias=CV[:, 0:1]),
           waits=[("sV", s.at("sV"))])
        op("scalar", lambda e: e.activation(RBA[:], LNT1[:], AF.Exp, scale=CV[:, 1:2]),
           inc="sS")
        chain_done = s.at("sS")
        return ln_bank_free, chain_done

    def emit_attn_ln_norm(chain_done):
        for ec2 in range(EC):
            sl = LNW[:, ec2 * TW:(ec2 + 1) * TW]
            op("vector", lambda e, a=sl: e.tensor_sub(a, a, MBA[:]),
               waits=[("sS", chain_done)] if ec2 == 0 else ())
            op("vector", lambda e, a=sl: e.tensor_mul(a, a, RBA[:]),
               inc="sV" if ec2 == EC - 1 else None)
        return s.at("sV")

    op("scalar", lambda e: e.activation(DUMW[:], CV[:, 0:1], AF.Ln))
    emit_attn_xsq(0, [("sV", ffn1_V), ("sS", ffn1_S)], eng="scalar")
    cd0 = emit_attn_ln_casts(0, [("sV", ffn1_V), ("sS", ffn1_S)])
    ln_bank_free, _cd = emit_attn_ln_rest(0, cd0)
    lnw_done = emit_attn_ln_norm(_cd)
    wot_scaled = None

    def proj_pair_mm(wsb, pr, bank):
        hc0, hc1 = 2 * pr, 2 * pr + 1

        def f(e, wsb2=wsb, h0=hc0, h1=hc1, b2=bank):
            for sub, hcx in ((0, h0), (1, h1)):
                for c2 in range(EC):
                    inst = e.matmul(
                        PS(b2, sub * TW, TW),
                        wsb2[:, c2 * 1024 + hcx * 128: c2 * 1024 + hcx * 128 + 128],
                        LNW[:, c2 * TW:(c2 + 1) * TW],
                        start=(sub == 0 and c2 == 0), stop=(c2 == EC - 1),
                        skip_group_check=(sub == 1))
            return inst
        return f

    for w in range(NW):
        tok0 = w * TW

        elu_free = {}
        for pi, (wname, wsb, bsb, osb) in enumerate((("wqt", WQT, BQ, QF),
                                                     ("wkt", WKT, BK, KF))):
            for pr in range(4):
                gi = pi * 4 + pr
                bank = gi % 2
                waits = [("gq", aw[wname]), ("sV", lnw_done)]
                if gi >= 2:
                    waits.append(("sV", elu_free[gi - 2]))
                else:
                    waits.append(("sV", marks.get(f"ev_b{bank}", ffn1_V)))
                op("tensor", proj_pair_mm(wsb, pr, bank), waits=waits, inc="sT")
                pm = s.at("sT")
                hc0, hc1 = 2 * pr, 2 * pr + 1
                op("vector", lambda e, b2=bank, b3=bsb, h=hc0:
                   e.tensor_scalar_add(PEL2[:, 0:TW], PS(b2, 0, TW), b3[:, h:h + 1]),
                   waits=[("sT", pm)])
                op("vector", lambda e, b2=bank, b3=bsb, h=hc1:
                   e.tensor_scalar_add(PEL2[:, TW:2 * TW], PS(b2, TW, TW), b3[:, h:h + 1]),
                   inc="sV")
                elu_free[gi] = s.at("sV")
                op("vector", lambda e: e.tensor_single_scalar(T1EL[:], PEL2[:], 0.0, ALU.max))
                op("vector", lambda e: e.tensor_single_scalar(T2EL[:], PEL2[:], 0.0, ALU.min),
                   inc="sV")
                op("scalar", lambda e: e.activation(T2EL[:], T2EL[:], AF.Exp),
                   waits=[("sV", s.at("sV"))], inc="sS")
                op("vector", lambda e, o=osb[:, hc0 * TW: hc0 * TW + 2 * TW]:
                   e.scalar_tensor_tensor(o, T1EL[:], -1.0, T2EL[:], ALU.add, ALU.add),
                   waits=[("sS", s.at("sS"))], inc="sV")
        kf_done = s.at("sV")
        marks["qk_T"] = s.at("sT")

        v_evac_g = {}
        for c in range(2):
            for half in range(2):
                bank = c * 2 + half
                waits = [("gq", aw["wvt"]), ("sq", consts_a)]
                if bank in (0, 1):
                    waits.append(("sV", elu_free[6 + bank]))
                else:
                    waits.append(("sS", marks.get("b23_scal_free", 0)))
                    waits.append(("sV", marks.get("b01_vec_free", ffn1_V)))

                def vmm(e, c2=c, h2=half, b2=bank):
                    for e3 in range(EC):
                        inst = e.matmul(PS(b2),
                                        LNW[:, e3 * TW + c2 * 128: e3 * TW + c2 * 128 + 128],
                                        WVT[:, e3 * 1024 + h2 * 512: e3 * 1024 + h2 * 512 + 512],
                                        start=(e3 == 0), stop=(e3 == EC - 1))
                    return inst
                op("tensor", vmm, waits=waits, inc="sT")
                op("vector", lambda e, c2=c, h2=half, b2=bank:
                   e.tensor_tensor(VT[:, c2 * 1024 + h2 * 512: c2 * 1024 + h2 * 512 + 512],
                                   PS(b2), BVB[:, h2 * 512:(h2 + 1) * 512], ALU.add),
                   waits=[("sT", s.at("sT"))], inc="sV")
                v_evac_g[bank] = s.at("sV")
            op("scalar", lambda e, c2=c:
               e.activation(VT[:, c2 * 1024:(c2 + 1) * 1024], VT[:, c2 * 1024:(c2 + 1) * 1024],
                            AF.Silu),
               waits=[("sV", s.at("sV"))], inc="sS")
        v_done_S = s.at("sS")
        if w == 0:
            for hc in range(NH):
                op("vector", lambda e, h2=hc:
                   e.tensor_scalar_mul(WOT[:, h2 * 1024:(h2 + 1) * 1024],
                                       WOT[:, h2 * 1024:(h2 + 1) * 1024], SCL[:, h2:h2 + 1]),
                   waits=[("gq", aw["wot"]), ("sq", consts_a)] if hc == 0 else (),
                   inc="sV" if hc == NH - 1 else None)
            wot_scaled = ("sV", s.at("sV"))
        if w + 1 < NW:
            emit_attn_xsq(w + 1, [("sV", ffn1_V)], eng="gpsimd")

        u_free = {}

        def emit_u_pair(pr):
            bank = 2 + pr % 2
            waits = [("gq", aw["wut"]), ("sV", lnw_done),
                     ("sV", v_evac_g[bank])]
            if pr >= 2:
                waits.append(("sS", u_free[pr - 2]))
            else:
                waits.append(("sS", marks.get("b23_scal_free", 0)))
            op("tensor", proj_pair_mm(WUT, pr, bank), waits=waits, inc="sT")
            pm = s.at("sT")
            for sub, hcx in ((0, 2 * pr), (1, 2 * pr + 1)):
                op("scalar", lambda e, b2=bank, sub2=sub, h=hcx:
                   e.activation(UB[:, h * TW:(h + 1) * TW], PS(b2, sub2 * TW, TW), AF.Silu,
                                bias=BU[:, h:h + 1]),
                   waits=[("sT", pm)], inc="sS")
            u_free[pr] = s.at("sS")

        g0, g1 = 2 * w, 2 * w + 1

        def ktr(c2, banks):
            def f(e, c3=c2, bk=banks):
                for h in range(NH):
                    inst = e.matmul(PS(bk[h // 4], (h % 4) * 128, 128),
                                    KF[:, h * TW + c3 * 128: h * TW + c3 * 128 + 128],
                                    ID[:], start=(h % 4 == 0), stop=True,
                                    skip_group_check=(h % 4 != 0))
                return inst
            return f

        ktrA_waits = [("sV", kf_done), marks.get("scan45_free", ("sS", 0))]
        if w == 0:
            ktrA_waits.append(("sq", consts_a))
        op("tensor", ktr(0, (4, 5)), waits=ktrA_waits, inc="sT")
        ktm = s.at("sT")
        op("scalar", lambda e: e.activation(KT[:, 0:512], PS(4), AF.Copy),
           waits=[("sT", ktm)])
        op("scalar", lambda e: e.activation(KT[:, 512:1024], PS(5), AF.Copy), inc="sS")
        kt0_done = ("sS", s.at("sS"))
        op("tensor", ktr(1, (0, 1)), waits=[("sV", v_evac_g[1])], inc="sT")
        ktm = s.at("sT")
        op("scalar", lambda e: e.activation(KT[:, 1024:1536], PS(0), AF.Copy),
           waits=[("sT", ktm)])
        op("scalar", lambda e: e.activation(KT[:, 1536:2048], PS(1), AF.Copy), inc="sS")
        kt1_done = ("sS", s.at("sS"))
        emit_u_pair(0)

        def imm(c2, banks):
            def f(e, c3=c2, bk=banks):
                for h in range(NH):
                    inst = e.matmul(PS(bk[h // 4], (h % 4) * 128, 128),
                                    KF[:, h * TW + c3 * 128: h * TW + c3 * 128 + 128],
                                    QF[:, h * TW + c3 * 128: h * TW + c3 * 128 + 128],
                                    start=(h % 4 == 0), stop=True,
                                    skip_group_check=(h % 4 != 0))
                return inst
            return f

        def omm(c2, g2, banks):
            def f(e, c3=c2, g3=g2, bk=banks):
                for h in range(NH):
                    inst = e.matmul(PS(bk[h // 4], (h % 4) * 128, 128),
                                    VT[:, c3 * 1024 + h * 128: c3 * 1024 + h * 128 + 128],
                                    ITR8[:, h * 128:(h + 1) * 128],
                                    start=(h % 4 == 0), stop=(g3 == 0),
                                    skip_group_check=(h % 4 != 0))
                    if g3 > 0:
                        inst = e.matmul(PS(bk[h // 4], (h % 4) * 128, 128),
                                        SSB[:, h * 128:(h + 1) * 128],
                                        QF[:, h * TW + c3 * 128: h * TW + c3 * 128 + 128],
                                        start=False, stop=True, skip_group_check=True)
                return inst
            return f

        def smm(c2, g2):
            def f(e, c3=c2, g3=g2):
                for h in range(NH):
                    inst = e.matmul(PS(6 + h // 4, (h % 4) * 128, 128),
                                    KT[:, c3 * 1024 + h * 128: c3 * 1024 + h * 128 + 128],
                                    VT[:, c3 * 1024 + h * 128: c3 * 1024 + h * 128 + 128],
                                    start=(g3 == 0 and h % 4 == 0), stop=True,
                                    skip_group_check=True)
                return inst
            return f

        def oevac(c2, banks, first_wait):
            ms = []
            for bi in range(2):
                src_ap = PS(banks[bi]).rearrange("p (h n) -> p h n", h=4)
                dst_ap = OB[:].rearrange("p (h n) -> p h n", h=8)[:, bi * 4:(bi + 1) * 4,
                                                                 c2 * 128:(c2 + 1) * 128]
                op("scalar", lambda e, si=src_ap, di_=dst_ap:
                   e.activation(di_, si, AF.Copy),
                   waits=[("sT", first_wait)] if bi == 0 else (), inc="sS")
                ms.append(("sS", s.at("sS")))
            return ms[0], ms[1]

        op("tensor", imm(0, (4, 5)), waits=[kt0_done], inc="sT")
        im0 = s.at("sT")
        op("tensor", imm(1, (0, 1)), waits=[kt1_done, ("sV", v_evac_g[3])], inc="sT")
        im1 = s.at("sT")
        emit_u_pair(1)
        op("vector", lambda e: e.tensor_tensor(ITR8[:, 0:512], PS(4), TRIU4[:], ALU.mult),
           waits=[("sT", im0)])
        op("vector", lambda e: e.tensor_tensor(ITR8[:, 512:1024], PS(5), TRIU4[:], ALU.mult),
           inc="sV")
        mask0 = s.at("sV")
        op("tensor", omm(0, g0, (4, 5)),
           waits=[("sV", mask0), ("sS", max(v_done_S, marks.get("s_copy", 0)))],
           inc="sT")
        om0 = s.at("sT")
        marks["scan45_free"] = None
        o_evac_0 = oevac(0, (4, 5), om0)[1]
        marks["scan45_free"] = o_evac_0
        op("tensor", smm(0, g0), waits=[("sS", max(v_done_S, marks.get("s_copy", 0)))], inc="sT")
        sm = s.at("sT")
        op("scalar", lambda e: e.activation(SSB[:, 0:512], PS(6), AF.Copy),
           waits=[("sT", sm)], inc="sS")
        op("scalar", lambda e: e.activation(SSB[:, 512:1024], PS(7), AF.Copy), inc="sS")
        marks["s_copy"] = s.at("sS")
        emit_u_pair(2)

        ln_next = None

        op("vector", lambda e: e.tensor_tensor(ITR8[:, 0:512], PS(0), TRIU4[:], ALU.mult),
           waits=[("sT", im1), ("sT", om0)])
        op("vector", lambda e: e.tensor_tensor(ITR8[:, 512:1024], PS(1), TRIU4[:], ALU.mult),
           inc="sV")
        mask1 = s.at("sV")
        op("tensor", omm(1, g1, (0, 1)),
           waits=[("sV", mask1), ("sS", marks["s_copy"])], inc="sT")
        om1 = s.at("sT")
        emit_u_pair(3)
        u_done = u_free[3]
        marks["uv_T"] = s.at("sT")
        ln_casts = (emit_attn_ln_casts(w + 1, [("sV", ffn1_V)], eng="gpsimd")
                    if w + 1 < NW else None)
        if w == NW - 1:
            c0xw = [("sV", marks["x_t01_final"]), ("sT", marks["qk_T"])]
            for ec2 in range(EC):
                op("gpsimd", lambda e, o=LNA[:, ec2 * N: ec2 * N + TF],
                   i=X[:, ec2 * N: ec2 * N + TF]: e.tensor_copy(o, i),
                   waits=c0xw if ec2 == 0 else (),
                   inc="sw" if ec2 == EC - 1 else None)
            marks["ffn2_c0_cp"] = ("sw", s.at("sw"))
        o_evac_1_first, o_evac_1 = oevac(1, (0, 1), om1)
        marks["o_evac_1"] = o_evac_1
        marks["o_evac_1_first"] = o_evac_1_first
        if g1 < NCH - 1:
            op("tensor", smm(1, g1), waits=[("sS", marks["s_copy"])], inc="sT")
            sm = s.at("sT")
            op("scalar", lambda e: e.activation(SSB[:, 0:512], PS(6), AF.Copy),
               waits=[("sT", sm)], inc="sS")
            op("scalar", lambda e: e.activation(SSB[:, 512:1024], PS(7), AF.Copy), inc="sS")
            marks["s_copy"] = s.at("sS")

        op("scalar", lambda e: e.activation(DUMW[:], CV[:, 0:1], AF.Ln))
        if w == NW - 1:
            c0xw = [("sV", marks["x_t01_final"]), ("sT", marks["uv_T"])]
            for ec2 in range(4):
                op("scalar", lambda e, o=XSQ[:, ec2 * TF:(ec2 + 1) * TF],
                   i=X[:, ec2 * N: ec2 * N + TF]: e.activation(o, i, AF.Square),
                   waits=c0xw if ec2 == 0 else (),
                   inc="sS" if ec2 == 3 else None)
            marks["ffn2_c0"] = [marks["ffn2_c0_cp"], ("sS", s.at("sS"))]

        o_done = marks["o_evac_1"]
        op("vector", lambda e: e.tensor_mul(ZB[:, 0:4 * TW], OB[:, 0:4 * TW],
                                            UB[:, 0:4 * TW]),
           waits=[marks["o_evac_1_first"], ("sS", u_done)], inc="sV")
        zb03 = s.at("sV")
        op("vector", lambda e: e.tensor_mul(ZB[:, 4 * TW:8 * TW], OB[:, 4 * TW:8 * TW],
                                            UB[:, 4 * TW:8 * TW]),
           waits=[o_done], inc="sV")
        zb_done = s.at("sV")
        marks["ob_free"] = ("sV", zb_done)
        if w == NW - 1:
            c0xw2 = [("sV", marks["x_t01_final"]), ("sT", marks["uv_T"])]
            for ec2 in range(4, EC):
                op("vector", lambda e, o=XSQ[:, ec2 * TF:(ec2 + 1) * TF],
                   i=X[:, ec2 * N: ec2 * N + TF]: e.tensor_mul(o, i, i),
                   waits=c0xw2 if ec2 == 4 else (),
                   inc="sV" if ec2 == EC - 1 else None)
            marks["ffn2_c0"].append(("sV", s.at("sV")))
        op("scalar", lambda e: e.activation(OSQ[:], OB[:], AF.Square),
           waits=[o_done, ("sT", s.at("sT"))], inc="sS")
        osq_done = s.at("sS")
        marks["osq_sS"] = osq_done

        def wo_mm(ea, eb, b2):
            def f(e, ea2=ea, eb2=eb, b3=b2):
                for sub, ex in ((0, ea2), (1, eb2)):
                    for h in range(NH):
                        inst = e.matmul(
                            PS(b3, sub * TW, TW),
                            WOT[:, h * 1024 + ex * 128: h * 1024 + ex * 128 + 128],
                            ZB[:, h * TW:(h + 1) * TW],
                            start=(sub == 0 and h == 0), stop=(h == NH - 1),
                            skip_group_check=(sub == 1))
                return inst
            return f

        def wo_evac(ea, eb, b2, wm, rstd_sS):
            for sub, ex in ((0, ea), (1, eb)):
                xs = X[:, ex * N + tok0: ex * N + tok0 + TW]
                op("vector", lambda e, b3=b2, sub2=sub:
                   e.tensor_mul(R2A[:], PS(b3, sub2 * TW, TW), R1A[:]),
                   waits=[("sT", wm), ("sS", rstd_sS)] if sub == 0 else ())
                op("vector", lambda e, o=xs, e3=ex:
                   e.scalar_tensor_tensor(o, R2A[:], BO[:, e3:e3 + 1], o, ALU.add, ALU.add),
                   inc="sV" if sub == 1 else None)
            return s.at("sV")

        def wo_mm_half(ea, eb, b2, hlf):
            def f(e, ea2=ea, eb2=eb, b3=b2, h4=hlf):
                for sub, ex in ((0, ea2), (1, eb2)):
                    for h in range(h4 * 4, h4 * 4 + 4):
                        inst = e.matmul(
                            PS(b3, sub * TW, TW),
                            WOT[:, h * 1024 + ex * 128: h * 1024 + ex * 128 + 128],
                            ZB[:, h * TW:(h + 1) * TW],
                            start=(sub == 0 and h == 0), stop=(h == NH - 1),
                            skip_group_check=(sub == 1))
                return inst
            return f

        op("tensor", wo_mm_half(0, 1, 0, 0), waits=[("sV", zb03)]
           + ([wot_scaled] if wot_scaled else []))
        op("tensor", wo_mm_half(0, 1, 0, 1), waits=[("sV", zb_done)], inc="sT")
        wm0 = s.at("sT")
        ln_next = None
        if w + 1 < NW:
            ln_next = emit_attn_ln_rest(w + 1, ln_casts,
                                        extra_waits=[marks["scan45_free"]])
            marks["scan45_free"] = ("sV", ln_next[0])
        else:
            marks["ffn2_stats"] = emit_ffn_ln_stats(
                0, marks["ffn2_c0"],
                extra_waits=[marks["scan45_free"], ("sV", zb_done)])
        op("tensor", wo_mm(2, 3, 1), waits=[("sV", zb_done)], inc="sT")
        wm1 = s.at("sT")

        def rms_mm(e):
            for h in range(NH):
                inst = e.matmul(PS(2, 0, TW), ONES[:], OSQ[:, h * TW:(h + 1) * TW],
                                start=(h == 0), stop=(h == NH - 1))
            return inst
        rms_waits = [("sS", osq_done)]
        op("tensor", rms_mm, waits=rms_waits, inc="sT")
        op("scalar", lambda e: e.activation(R1A[:], PS(2, 0, TW), AF.Ln, bias=CV[:, 2:3]),
           waits=[("sT", s.at("sT"))], inc="sS")
        rms_ln1 = s.at("sS")
        op("scalar", lambda e: e.activation(R1A[:], R1A[:], AF.Exp, scale=CV[:, 3:4]))
        op("scalar", lambda e: e.activation(R1A[:], R1A[:], AF.Ln, bias=CV[:, 6:7]))
        op("scalar", lambda e: e.activation(R1A[:], R1A[:], AF.Exp, scale=CV[:, 4:5],
                                            bias=CV[:, 5:6]), inc="sS")
        rstd_sS = s.at("sS")
        marks["b23_scal_free"] = s.at("sS")

        op("tensor", wo_mm(4, 5, 3), waits=[("sV", zb_done)], inc="sT")
        wm2 = s.at("sT")
        op("tensor", wo_mm(6, 7, 2), waits=[("sV", zb_done), ("sS", rms_ln1)], inc="sT")
        wm3 = s.at("sT")
        marks["wo_T"] = wm3
        if ln_next is not None:
            lnw_done = emit_attn_ln_norm(ln_next[1])
        marks["ev_b0"] = wo_evac(0, 1, 0, wm0, rstd_sS)
        marks["ev_b1"] = wo_evac(2, 3, 1, wm1, rstd_sS)
        marks["ev_b3"] = wo_evac(4, 5, 3, wm2, rstd_sS)
        marks["ev_b2"] = wo_evac(6, 7, 2, wm3, rstd_sS)
        marks["ev_bank"] = {0: marks["ev_b0"], 1: marks["ev_b1"],
                            2: marks["ev_b2"], 3: marks["ev_b3"]}
        if ln_next is None:
            marks["ffn2_ln0"] = emit_ffn_ln_chain_norm(
                0, marks["ffn2_stats"],
                bufs=(MBF2[:], RBF2[:], BVB[:, 0:512], BVB[:, 512:1024]))
        marks["b01_vec_free"] = s.at("sV")
        if w == 1:
            marks["x_t01_final"] = s.at("sV")


    attn_V = s.at("sV")

    ffn_pass(1, [("sV", attn_V), ("sS", s.at("sS"))], ln0_done=marks["ffn2_ln0"])

    so_total = s.at("so")
    s.emit("sync", lambda e: e.wait_ge(sems["so"], so_total), [], [])

    with nc.Block() as blk:
        def make_body(eng_name):
            def body(eng):
                last_wait = {}
                for fn, waits, incs in s.items[eng_name]:
                    for wsem, wval in waits:
                        if wval <= 0 or last_wait.get(wsem, -1) >= wval:
                            continue
                        eng.wait_ge(sems[wsem], wval)
                        last_wait[wsem] = wval
                    inst = fn(eng)
                    for isem, amt in incs:
                        inst = inst.then_inc(sems[isem], amt)
            return body
        blk.sync(make_body("sync"))
        blk.gpsimd(make_body("gpsimd"))
        blk.tensor(make_body("tensor"))
        blk.vector(make_body("vector"))
        blk.scalar(make_body("scalar"))

    return nc


def _host_inputs(Wq, bq, Wk, bk, Wv, bv, Wu, bu, Wo, bo, scale, w1, b1, w2, b2):
    asf = lambda a: np.ascontiguousarray(np.asarray(a, dtype=np.float32))
    w1 = asf(w1); w2 = asf(w2)
    fmw = lambda W: np.ascontiguousarray(
        asf(W).T.reshape(8, 128, 1024).transpose(1, 0, 2).reshape(128, 8192))
    triu = np.triu(np.ones((128, 128), dtype=np.float32))
    scl = np.ascontiguousarray(asf(scale).reshape(8, 128).T)
    return {
        "w1h": np.ascontiguousarray(w1.reshape(8, 128, 8, 512).transpose(1, 2, 0, 3).reshape(128, 32768)),
        "w2h": np.ascontiguousarray(w2.reshape(32, 128, 8, 128).transpose(1, 2, 0, 3).reshape(128, 32768)),
        "wqt": fmw(Wq), "wkt": fmw(Wk), "wut": fmw(Wu), "wvt": fmw(Wv), "wot": fmw(Wo),
        "ident": np.eye(128, dtype=BF),
        "ones": np.ones((128, 128), dtype=BF),
        "triu4": np.ascontiguousarray(np.tile(triu, (1, 4))),
        "b1l": np.ascontiguousarray(asf(b1).reshape(32, 128).T),
        "b2l": np.ascontiguousarray(asf(b2).reshape(8, 128).T),
        "bql": np.ascontiguousarray(asf(bq).reshape(8, 128).T),
        "bkl": np.ascontiguousarray(asf(bk).reshape(8, 128).T),
        "bul": np.ascontiguousarray(asf(bu).reshape(8, 128).T),
        "bol": np.ascontiguousarray(asf(bo).reshape(8, 128).T),
        "scl": scl,
        "bvb": np.ascontiguousarray(np.broadcast_to(asf(bv), (128, 1024))),
        "cvec": np.ascontiguousarray(np.tile(np.array(
            [1e-5, -0.5, 1e-30, 0.5, -1.0, np.log(32.0), 3.2e-7, 0.0],
            dtype=np.float32), (128, 1))),
    }


def kernel(L0_attn_x, L0_attn_Wq, L0_attn_bq, L0_attn_Wk, L0_attn_bk, L0_attn_Wv,
           L0_attn_bv, L0_attn_Wu, L0_attn_bu, L0_attn_Wo, L0_attn_bo, L0_attn_scale,
           L0_ffn_w1, L0_ffn_b1, L0_ffn_w2, L0_ffn_b2):
    x = np.asarray(L0_attn_x, dtype=np.float32)
    com = _host_inputs(L0_attn_Wq, L0_attn_bq, L0_attn_Wk, L0_attn_bk, L0_attn_Wv,
                       L0_attn_bv, L0_attn_Wu, L0_attn_bu, L0_attn_Wo, L0_attn_bo,
                       L0_attn_scale, L0_ffn_w1, L0_ffn_b1, L0_ffn_w2, L0_ffn_b2)
    nc = build_program()
    in_maps = []
    for b in range(8):
        m = dict(com)
        m["x_fm"] = np.ascontiguousarray(x[b].T)
        in_maps.append(m)
    res = run_bass_kernel_spmd(nc, in_maps, core_ids=list(range(8)))
    out = np.stack([np.asarray(r["out_fm"], dtype=np.float32).T for r in res.results])
    return np.ascontiguousarray(out)


if __name__ == "__main__":
    nc = build_program()
    print("built ok, instructions:", sum(1 for _ in nc.all_instructions()))


# version 59 (final)
# speedup vs baseline: 1.0608x; 1.0241x over previous
import numpy as np
import ml_dtypes
import concourse.bass as bass
import concourse.mybir as mybir
from concourse.bass_utils import run_bass_kernel_spmd

F32 = mybir.dt.float32
BF16 = mybir.dt.bfloat16
AF = mybir.ActivationFunctionType
ALU = mybir.AluOpType
BF = ml_dtypes.bfloat16

E = 1024
N = 2048
EC = 8
JC = 32
NH = 8
NCH = 16
TF = 512
NTF = 4
TW = 256
NW = 8

ENGS = ("sync", "gpsimd", "tensor", "vector", "scalar")


class Sched:
    def __init__(self):
        self.items = {e: [] for e in ENGS}
        self.cnt = {}

    def sem(self, name):
        self.cnt[name] = 0

    def at(self, sem):
        return self.cnt[sem]

    def emit(self, eng, fn, waits=(), incs=()):
        for sname, a in incs:
            self.cnt[sname] += a
        self.items[eng].append((fn, [w for w in waits if w is not None], list(incs)))


def build_program():
    nc = bass.Bass("TRN2", target_bir_lowering=False)

    di = lambda name, shape, dt: nc.dram_tensor(name, shape, dt, kind="ExternalInput")
    x_fm = di("x_fm", [E, N], F32)
    w1h = di("w1h", [128, 32768], F32)
    w2h = di("w2h", [128, 32768], F32)
    wqt = di("wqt", [128, 8192], F32)
    wkt = di("wkt", [128, 8192], F32)
    wut = di("wut", [128, 8192], F32)
    wvt = di("wvt", [128, 8192], F32)
    wot = di("wot", [128, 8192], F32)
    ident_d = di("ident", [128, 128], BF16)
    ones_d = di("ones", [128, 128], BF16)
    triu4_d = di("triu4", [128, 512], F32)
    b1_d = di("b1l", [128, 32], F32)
    b2_d = di("b2l", [128, 8], F32)
    bq_d = di("bql", [128, 8], F32)
    bk_d = di("bkl", [128, 8], F32)
    bu_d = di("bul", [128, 8], F32)
    bo_d = di("bol", [128, 8], F32)
    scl_d = di("scl", [128, 8], F32)
    bvb_d = di("bvb", [128, 1024], F32)
    cvec_d = di("cvec", [128, 8], F32)
    out_fm = nc.dram_tensor("out_fm", [E, N], F32, kind="ExternalOutput")
    w1scr = nc.dram_tensor("w1scr", [128, 32768], BF16, kind="Internal")
    w2scr = nc.dram_tensor("w2scr", [128, 32768], BF16, kind="Internal")

    sbt = lambda name, shape, dt: nc.alloc_sbuf_tensor(name, shape, dt)
    X = sbt("X", [128, EC * N], F32)
    ID = sbt("ID", [128, 128], BF16)
    ONES = sbt("ONES", [128, 128], BF16)
    B1 = sbt("B1", [128, 32], F32)
    B2 = sbt("B2", [128, 8], F32)
    B2H = sbt("B2H", [128, 8], F32)
    BQ = sbt("BQ", [128, 8], F32)
    BK = sbt("BK", [128, 8], F32)
    BU = sbt("BU", [128, 8], F32)
    BO = sbt("BO", [128, 8], F32)
    CV = sbt("CV", [128, 8], F32)
    DUMW = sbt("DUMW", [128, 1], F32)

    SCRATCH_BYTES = 145408
    base, _ = nc.bump_sbuf(SCRATCH_BYTES)
    at = lambda name, off, shape, dt: nc.alloc_sbuf_tensor_at(name, shape, dt, offset=base + off)
    HS = at("HS", 0, [128, JC * TF], BF16)
    LNA = at("LNA", 32768, [128, EC * N], BF16)
    XSQ = at("XSQ", 65536, [128, EC * TF], BF16)
    W1S = [at(f"W1S{i}", 73728 + i * 8192, [128, 4096], BF16) for i in range(2)]
    W2S = [at(f"W2S{i}", 90112 + i * 8192, [128, 4096], BF16) for i in range(2)]
    MBF = at("MBF", 106496, [128, TF], BF16)
    RBF = at("RBF", 107520, [128, TF], BF16)
    FT1 = at("FT1", 108544, [128, TF], F32)
    FT2 = at("FT2", 110592, [128, TF], F32)
    LNW = at("LNW", 0, [128, EC * TW], BF16)
    ZB = at("ZB", 4096, [128, NH * TW], BF16)
    QF = at("QF", 8192, [128, NH * TW], BF16)
    KF = at("KF", 12288, [128, NH * TW], BF16)
    UB = at("UB", 16384, [128, NH * TW], BF16)
    VT = at("VT", 20480, [128, 2 * 1024], BF16)
    KT = at("KT", 24576, [128, 2 * 1024], BF16)
    XSA = at("XSA", 28672, [128, EC * TW], BF16)
    WQT = at("WQT", 32768, [128, 8192], BF16)
    WKT = at("WKT", 49152, [128, 8192], BF16)
    WUT = at("WUT", 65536, [128, 8192], BF16)
    WVT = at("WVT", 81920, [128, 8192], BF16)
    WOT = at("WOT", 98304, [128, 8192], BF16)
    SCL = sbt("SCL", [128, 8], F32)
    OB = at("OB", 118784, [128, NH * TW], BF16)
    OSQ = at("OSQ", 122880, [128, NH * TW], BF16)
    SSB = at("SSB", 126976, [128, NH * 128], BF16)
    ITR8 = at("ITR8", 129024, [128, 1024], BF16)
    PEL2 = at("PEL2", 131072, [128, 512], F32)
    T1EL = at("T1EL", 133120, [128, 512], F32)
    T2EL = at("T2EL", 135168, [128, 512], F32)
    US = at("US", 131072, [128, NH * TW], BF16)
    MBA = at("MBA", 135168, [128, TW], BF16)
    RBA = at("RBA", 135680, [128, TW], BF16)
    LNT1 = at("LNT1", 136192, [128, TW], F32)
    R1A = at("R1A", 137216, [128, TW], F32)
    R2A = at("R2A", 138240, [128, TW], F32)
    TRIU4 = at("TRIU4", 139264, [128, 512], F32)
    BVB = at("BVB", 141312, [128, 1024], F32)
    MBF2 = at("MBF2", 139264, [128, 512], BF16)
    RBF2 = at("RBF2", 140288, [128, 512], BF16)

    W1X = [at(f"W1X{i}", 118784 + i * 8192, [128, 4096], BF16) for i in range(2)]

    PSA = nc.alloc_psum_tensor("PSA", [128, 4096], F32)
    PS = lambda bank, off=0, width=512: PSA[:, bank * 512 + off: bank * 512 + off + width]

    s = Sched()
    sems = {}
    for name in ("sq", "gq", "sT", "sV", "sS", "so", "sx", "sw",
                 "u0", "u1", "u2", "u3", "v0", "v1"):
        sems[name] = nc.alloc_semaphore(name)
        s.sem(name)

    def dma(eng, out, in_, waits=(), sem=None):
        name = sem or ("gq" if eng == "gpsimd" else "sq")
        s.emit(eng, lambda e, o=out, i=in_: e.dma_start(out=o, in_=i), waits, [(name, 16)])

    def op(eng, fn, waits=(), inc=None):
        s.emit(eng, fn, waits, [(inc, 1)] if inc else [])

    for dst, src in ((ID, ident_d), (ONES, ones_d), (B1, b1_d),
                     (B2, b2_d), (BQ, bq_d), (BK, bk_d), (BU, bu_d), (BO, bo_d),
                     (CV, cvec_d), (SCL, scl_d)):
        dma("sync", dst[:], src[:])
    n_const = s.at("sq")
    for ec in range(EC):
        s.emit("sync", lambda e, o=X[:, ec * N: ec * N + TF],
               i=x_fm[ec * 128:(ec + 1) * 128, 0:TF]: e.dma_start(out=o, in_=i),
               (), [("sx", 16)])
    x_q1 = ("sx", s.at("sx"))
    for ec in range(EC):
        dma("sync", X[:, ec * N + TF:(ec + 1) * N], x_fm[ec * 128:(ec + 1) * 128, TF:N],
            waits=[x_q1] if ec == 0 else ())
    x_loaded = s.at("sq")
    op("vector", lambda e: e.tensor_scalar_mul(B2H[:], B2[:], 0.5),
       waits=[("sq", n_const)], inc="sV")
    op("scalar", lambda e: e.activation(DUMW[:], CV[:, 0:1], AF.Ln),
       waits=[("sq", n_const)])

    w1_free = [0, 0]
    w2_free = [0, 0]
    w1r_free = [[("sT", 0)] for _ in range(4)]
    w1c_free = [0, 0, 0, 0]
    w2c_free = [0, 0, 0, 0]
    w1_scr_sq = [0] * 8
    w2_scr_sq = [0] * 8

    def load_w1(b, cast_pass, ring4=False):
        if cast_pass:
            ring = (W1S[0], W1S[1], W1X[0], W1X[1])[b % 4]
            slot_w = [("u%d" % (b % 4), 16 * (b // 4))] if b >= 4 else []
            slot_w.append(("sx", 128))
            dma("gpsimd", ring[:], w1h[:, b * 4096:(b + 1) * 4096],
                waits=[("sT", w1c_free[b % 4])] + slot_w)
            ready = ("gq", s.at("gq"))
            dma("scalar", w1scr[:, b * 4096:(b + 1) * 4096], ring[:], waits=[ready],
                sem="u%d" % (b % 4))
            return ready, ring
        if ring4:
            ring = (W1S[0], W1S[1], W1X[0], W1X[1])[b % 4]
            dma("sync", ring[:], w1scr[:, b * 4096:(b + 1) * 4096],
                waits=list(w1r_free[b % 4]) + [("u%d" % (b % 4), 32)])
            return ("sq", s.at("sq")), ring
        slot = b % 2
        src_w = [("u%d" % (b % 4), 16 * (b // 4 + 1))] if b % 4 >= 2 else []
        dma("sync", W1S[slot][:], w1scr[:, b * 4096:(b + 1) * 4096],
            waits=[("sT", w1_free[slot]), ("u%d" % slot, 32)] + src_w)
        return ("sq", s.at("sq")), W1S[slot]

    def load_w2(e2, cast_pass):
        slot = e2 % 2
        if cast_pass:
            slot_w = [("v%d" % slot, 16 * (e2 // 2))] if e2 >= 2 else []
            dma("gpsimd", W2S[slot][:], w2h[:, e2 * 4096:(e2 + 1) * 4096],
                waits=[("sT", w2_free[slot])] + slot_w)
            ready = ("gq", s.at("gq"))
            dma("scalar", w2scr[:, e2 * 4096:(e2 + 1) * 4096], W2S[slot][:], waits=[ready],
                sem="v%d" % slot)
            return ready, W2S[slot]
        dma("sync", W2S[slot][:], w2scr[:, e2 * 4096:(e2 + 1) * 4096],
            waits=[("sT", w2_free[slot]), ("v%d" % slot, 64)])
        return ("sq", s.at("sq")), W2S[slot]

    marks = {}

    def emit_ffn_ln_casts(t, xw, extra_stats_wait=(), sq_scalar=False):
        tok0 = t * TF
        for ec2 in range(EC):
            xs = X[:, ec2 * N + tok0: ec2 * N + tok0 + TF]
            op("vector", lambda e, o=LNA[:, ec2 * N + tok0: ec2 * N + tok0 + TF], i=xs:
               e.tensor_copy(o, i), waits=xw, inc="sV" if sq_scalar else None)
            if sq_scalar:
                op("scalar", lambda e, o=XSQ[:, ec2 * TF:(ec2 + 1) * TF], i=xs:
                   e.activation(o, i, AF.Square), waits=xw, inc="sS")
            else:
                op("vector", lambda e, o=XSQ[:, ec2 * TF:(ec2 + 1) * TF], i=xs:
                   e.tensor_mul(o, i, i), inc="sV")
        if sq_scalar:
            return (s.at("sV"), s.at("sS"))
        return s.at("sV")

    def emit_ffn_ln_stats(t, casts_done, extra_waits=()):
        tok0 = t * TF

        def stats_mm(e, t0=tok0):
            for c2 in range(EC):
                e.matmul(PS(4), ONES[:], LNA[:, c2 * N + t0: c2 * N + t0 + TF],
                         start=(c2 == 0), stop=(c2 == EC - 1))
            for c2 in range(EC):
                inst = e.matmul(PS(5), ONES[:], XSQ[:, c2 * TF:(c2 + 1) * TF],
                                start=(c2 == 0), stop=(c2 == EC - 1))
            return inst
        if isinstance(casts_done, list):
            cw = list(casts_done)
        elif isinstance(casts_done, tuple):
            cw = [("sV", casts_done[0]), ("sS", casts_done[1])]
        else:
            cw = [("sV", casts_done)]
        op("tensor", stats_mm, waits=cw + list(extra_waits), inc="sT")
        marks["ffn_stats_T"] = s.at("sT")
        return s.at("sT")

    def emit_ffn_ln_chain(t, stats_T, chain_wait=None, bufs=None):
        mb, rb, f1, f2 = bufs if bufs else (MBF[:], RBF[:], FT1[:], FT2[:])
        op("vector", lambda e: e.tensor_scalar_mul(mb, PS(4), 1.0 / E),
           waits=[("sT", stats_T)] + ([chain_wait] if chain_wait else []), inc="sV")
        op("scalar", lambda e: e.activation(f1, mb, AF.Square),
           waits=[("sV", s.at("sV"))], inc="sS")
        op("vector", lambda e: e.scalar_tensor_tensor(f2, PS(5), 1.0 / E, f1,
                                                      ALU.mult, ALU.subtract),
           waits=[("sS", s.at("sS"))], inc="sV")
        op("scalar", lambda e: e.activation(f1, f2, AF.Ln, bias=CV[:, 0:1]),
           waits=[("sV", s.at("sV"))])
        op("scalar", lambda e: e.activation(rb, f1, AF.Exp, scale=CV[:, 1:2]),
           inc="sS")
        return s.at("sS")

    def emit_ffn_ln_norm_only(t, chain_done, bufs=None):
        tok0 = t * TF
        mb, rb, f1, f2 = bufs if bufs else (MBF[:], RBF[:], FT1[:], FT2[:])
        nmarks = []
        for ec2 in range(EC):
            sl = LNA[:, ec2 * N + tok0: ec2 * N + tok0 + TF]
            op("vector", lambda e, a=sl, m2=mb: e.tensor_sub(a, a, m2),
               waits=[("sS", chain_done)] if ec2 == 0 else ())
            op("vector", lambda e, a=sl, r2=rb: e.tensor_mul(a, a, r2), inc="sV")
            nmarks.append(s.at("sV"))
        return (s.at("sV"), nmarks)

    def emit_ffn_ln_chain_norm(t, stats_T, chain_wait=None, bufs=None):
        cd = emit_ffn_ln_chain(t, stats_T, chain_wait, bufs)
        return emit_ffn_ln_norm_only(t, cd, bufs)

    def emit_ffn_ln_rest(t, casts_done, extra_waits=(), chain_wait=None, bufs=None):
        stats_T = emit_ffn_ln_stats(t, casts_done, extra_waits)
        return emit_ffn_ln_chain_norm(t, stats_T, chain_wait, bufs)

    def ffn_pass(pass_idx, x_ready_waits, ln0_done=None):
        if pass_idx == 1:
            w1_free[0] = w1_free[1] = marks["uv_T"]
            w2_free[0] = w2_free[1] = marks["wo_T"]
            w1r_free[0] = [("sT", marks["uv_T"])]
            w1r_free[1] = [("sT", marks["uv_T"])]
            w1r_free[2] = [marks["ob_free"], ("sS", marks["osq_sS"]),
                           ("sT", marks["wo_T"])]
            w1r_free[3] = [("sT", marks["wo_T"])]

        def emit_ln_casts(t, first=False, guard_T=0):
            xw = (list(x_ready_waits) if (pass_idx == 1 or not first) else [x_q1]) \
                + [("sT", guard_T)]
            return emit_ffn_ln_casts(t, xw)

        def emit_ln_rest(t, casts_done):
            return emit_ffn_ln_rest(t, casts_done)

        if ln0_done is None:
            xw0 = (list(x_ready_waits) if pass_idx == 1 else [x_q1]) + [("sT", 0)]
            c0 = emit_ffn_ln_casts(0, xw0, sq_scalar=True)
            ln_done_t = [emit_ln_rest(0, c0)]
            op("scalar", lambda e: e.activation(DUMW[:], CV[:, 0:1], AF.Gelu))
        else:
            ln_done_t = [ln0_done]
        prev_y_done = s.at("sV")

        for t in range(NTF):
            tok0 = t * TF
            ln_done = ln_done_t[t]
            cast_pass = (pass_idx == 0 and t == 0)
            gelu_base = s.at("sS")
            ring4 = (pass_idx == 1 and t == 0)
            for b in range(EC):
                wready, w1t = load_w1(b, cast_pass, ring4=ring4)
                for jl in range(4):
                    j = b * 4 + jl
                    pbank = j % 4
                    waits = [wready]
                    if j < 4:
                        if pass_idx == 1 and t == 0:
                            waits.append(("sV", marks["ev_bank"][pbank]))
                        else:
                            waits.append(("sV", prev_y_done))
                        if t == 0 and pass_idx == 0:
                            waits.append(("sS", s.at("sS")))
                    if j >= 4:
                        waits.append(("sS", gelu_base + j - 3))

                    if j == 0:
                        for half in (0, 1):
                            def g1c(e, wt=w1t, hf=half, t0=tok0):
                                for c3 in range(hf * 4, hf * 4 + 4):
                                    inst = e.matmul(
                                        PS(0), wt[:, c3 * 512: c3 * 512 + 128],
                                        LNA[:, c3 * N + t0: c3 * N + t0 + TF],
                                        start=(c3 == 0), stop=(c3 == EC - 1))
                                return inst
                            w_ = (waits if half == 0 else [])                                 + [("sV", ln_done[1][half * 4 + 3])]
                            op("tensor", g1c, waits=w_,
                               inc="sT" if half == 1 else None)
                    else:
                        waits.append(("sV", ln_done[0]))

                        def g1(e, wt=w1t, jl2=jl, pb2=pbank, t0=tok0):
                            for c2 in range(EC):
                                inst = e.matmul(
                                    PS(pb2),
                                    wt[:, c2 * 512 + jl2 * 128: c2 * 512 + jl2 * 128 + 128],
                                    LNA[:, c2 * N + t0: c2 * N + t0 + TF],
                                    start=(c2 == 0), stop=(c2 == EC - 1))
                            return inst
                        op("tensor", g1, waits=waits, inc="sT")
                    op("scalar", lambda e, j2=j, pb2=pbank:
                       e.activation(HS[:, j2 * TF:(j2 + 1) * TF], PS(pb2), AF.Gelu,
                                    bias=B1[:, j2:j2 + 1]),
                       waits=[("sT", s.at("sT"))], inc="sS")
                if cast_pass:
                    w1c_free[b % 4] = s.at("sT")
                    if b % 4 in (0, 1):
                        w1_free[b % 2] = s.at("sT")
                elif ring4:
                    w1r_free[b % 4] = [("sT", s.at("sT"))]
                    if b % 4 in (0, 1):
                        w1_free[b % 2] = s.at("sT")
                else:
                    w1_free[b % 2] = s.at("sT")
            gelus_done = s.at("sS")
            if pass_idx == 0 and t == NTF - 1:
                marks["ffn1_g1_end"] = s.at("sT")

            if t + 1 < NTF:
                ln_done_t.append(emit_ln_rest(
                    t + 1, emit_ln_casts(t + 1, guard_T=marks.get("ffn_stats_T", 0))))

            for e2 in range(EC):
                xs = X[:, e2 * N + tok0: e2 * N + tok0 + TF]
                op("vector", lambda e, o=xs, e3=e2:
                   e.tensor_scalar_add(o, o, B2H[:, e3:e3 + 1]),
                   waits=[("sV", ln_done[0])] if e2 == 0 else ())
            y_evac_base = s.at("sV")
            for e2 in range(EC):
                wready, w2t = load_w2(e2, cast_pass)
                pbank = e2 % 4
                waits = [wready, ("sS", gelus_done)]
                if e2 >= 4:
                    waits.append(("sV", y_evac_base + e2 - 3))

                def g2(e, wt=w2t, pb2=pbank):
                    for jc2 in range(JC):
                        inst = e.matmul(PS(pb2), wt[:, jc2 * 128:(jc2 + 1) * 128],
                                        HS[:, jc2 * TF:(jc2 + 1) * TF],
                                        start=(jc2 == 0), stop=(jc2 == JC - 1))
                    return inst
                op("tensor", g2, waits=waits, inc="sT")
                xs = X[:, e2 * N + tok0: e2 * N + tok0 + TF]
                op("vector", lambda e, o=xs, pb2=pbank:
                   e.scalar_tensor_tensor(o, PS(pb2), 0.5, o, ALU.mult, ALU.add),
                   waits=[("sT", s.at("sT"))], inc="sV")
                w2_free[e2 % 2] = s.at("sT")
                if pass_idx == 1:
                    s.emit("scalar",
                           lambda e, e3=e2, t0=tok0: e.dma_start(
                               out=out_fm[e3 * 128:(e3 + 1) * 128, t0:t0 + TF],
                               in_=X[:, e3 * N + t0: e3 * N + t0 + TF]),
                           [("sV", s.at("sV"))], [("so", 16)])
                if pass_idx == 0 and t == NTF - 1 and e2 == 6:
                    marks["ffn1_g2_e6"] = s.at("sT")
            prev_y_done = s.at("sV")

    ffn_pass(0, [("sq", x_loaded)])
    ffn1_T, ffn1_V, ffn1_S = s.at("sT"), s.at("sV"), s.at("sS")

    sq_now = ("sq", s.at("sq"))
    stores_done = [("u0", 32), ("u1", 32), ("u2", 32), ("u3", 32),
                   ("v0", 64), ("v1", 64)]
    aw = {}
    dma("gpsimd", WQT[:], wqt[:],
        waits=[("sT", marks["ffn1_g1_end"]), sq_now] + stores_done)
    aw["wqt"] = s.at("gq")
    dma("gpsimd", WKT[:], wkt[:], waits=[("sT", marks["ffn1_g1_end"])])
    aw["wkt"] = s.at("gq")
    dma("gpsimd", WUT[:], wut[:], waits=[("sT", marks["ffn1_g1_end"]), sq_now])
    aw["wut"] = s.at("gq")
    dma("gpsimd", WVT[:], wvt[:], waits=[("sT", marks["ffn1_g2_e6"]), sq_now])
    aw["wvt"] = s.at("gq")
    dma("gpsimd", WOT[:], wot[:], waits=[("sT", ffn1_T), ("sV", ffn1_V), ("sS", ffn1_S)])
    aw["wot"] = s.at("gq")
    dma("sync", TRIU4[:], triu4_d[:])
    dma("sync", BVB[:], bvb_d[:])
    consts_a = s.at("sq")

    def emit_attn_xsq(w, x_waits, eng="vector"):
        tok0 = w * TW
        sem = {"gpsimd": "sw", "scalar": "sS"}.get(eng, "sV")
        for ec2 in range(EC):
            xs = X[:, ec2 * N + tok0: ec2 * N + tok0 + TW]
            if eng == "scalar":
                fn = lambda e, o=XSA[:, ec2 * TW:(ec2 + 1) * TW], i=xs:                     e.activation(o, i, AF.Square)
            else:
                fn = lambda e, o=XSA[:, ec2 * TW:(ec2 + 1) * TW], i=xs:                     e.tensor_mul(o, i, i)
            op(eng, fn,
               waits=(list(x_waits) + [("sT", s.at("sT"))]) if ec2 == 0 else (),
               inc=sem if ec2 == EC - 1 else None)
        return (sem, s.at(sem))

    def emit_attn_ln_casts(w, x_waits, eng="vector"):
        tok0 = w * TW
        sem = "sw" if eng == "gpsimd" else "sV"
        for ec2 in range(EC):
            xs = X[:, ec2 * N + tok0: ec2 * N + tok0 + TW]
            op(eng, lambda e, o=LNW[:, ec2 * TW:(ec2 + 1) * TW], i=xs:
               e.tensor_copy(o, i),
               waits=(list(x_waits) + [("sT", s.at("sT"))]) if ec2 == 0 else (),
               inc=sem if ec2 == EC - 1 else None)
        return (sem, s.at(sem))

    def emit_attn_ln_rest(w, casts_done, extra_waits=()):
        def stats_mm_a(e):
            for c2 in range(EC):
                e.matmul(PS(4, 0, TW), ONES[:], LNW[:, c2 * TW:(c2 + 1) * TW],
                         start=(c2 == 0), stop=(c2 == EC - 1))
            for c2 in range(EC):
                inst = e.matmul(PS(5, 0, TW), ONES[:], XSA[:, c2 * TW:(c2 + 1) * TW],
                                start=(c2 == 0), stop=(c2 == EC - 1))
            return inst
        cw = list(casts_done) if isinstance(casts_done, list) else [casts_done]
        op("tensor", stats_mm_a,
           waits=cw + [("sS", s.at("sS"))] + list(extra_waits), inc="sT")
        op("vector", lambda e: e.tensor_scalar_mul(MBA[:], PS(4, 0, TW), 1.0 / E),
           waits=[("sT", s.at("sT"))], inc="sV")
        op("scalar", lambda e: e.activation(LNT1[:], MBA[:], AF.Square),
           waits=[("sV", s.at("sV"))], inc="sS")
        op("vector", lambda e: e.scalar_tensor_tensor(LNT1[:], PS(5, 0, TW), 1.0 / E, LNT1[:],
                                                      ALU.mult, ALU.subtract),
           waits=[("sS", s.at("sS"))], inc="sV")
        ln_bank_free = s.at("sV")
        op("scalar", lambda e: e.activation(LNT1[:], LNT1[:], AF.Ln, bias=CV[:, 0:1]),
           waits=[("sV", s.at("sV"))])
        op("scalar", lambda e: e.activation(RBA[:], LNT1[:], AF.Exp, scale=CV[:, 1:2]),
           inc="sS")
        chain_done = s.at("sS")
        return ln_bank_free, chain_done

    def emit_attn_ln_norm(chain_done):
        nmarks = []
        for ec2 in range(EC):
            sl = LNW[:, ec2 * TW:(ec2 + 1) * TW]
            op("vector", lambda e, a=sl: e.tensor_sub(a, a, MBA[:]),
               waits=[("sS", chain_done)] if ec2 == 0 else ())
            op("vector", lambda e, a=sl: e.tensor_mul(a, a, RBA[:]), inc="sV")
            nmarks.append(s.at("sV"))
        return (s.at("sV"), nmarks)

    op("scalar", lambda e: e.activation(DUMW[:], CV[:, 0:1], AF.Ln))
    emit_attn_xsq(0, [("sV", ffn1_V), ("sS", ffn1_S)], eng="scalar")
    cd0 = emit_attn_ln_casts(0, [("sV", ffn1_V), ("sS", ffn1_S)])
    ln_bank_free, _cd = emit_attn_ln_rest(0, cd0)
    lnw_done = emit_attn_ln_norm(_cd)
    wot_scaled = None

    def proj_pair_mm(wsb, pr, bank):
        hc0, hc1 = 2 * pr, 2 * pr + 1

        def f(e, wsb2=wsb, h0=hc0, h1=hc1, b2=bank):
            for sub, hcx in ((0, h0), (1, h1)):
                for c2 in range(EC):
                    inst = e.matmul(
                        PS(b2, sub * TW, TW),
                        wsb2[:, c2 * 1024 + hcx * 128: c2 * 1024 + hcx * 128 + 128],
                        LNW[:, c2 * TW:(c2 + 1) * TW],
                        start=(sub == 0 and c2 == 0), stop=(c2 == EC - 1),
                        skip_group_check=(sub == 1))
            return inst
        return f

    for w in range(NW):
        tok0 = w * TW

        elu_free = {}
        for pi, (wname, wsb, bsb, osb) in enumerate((("wqt", WQT, BQ, QF),
                                                     ("wkt", WKT, BK, KF))):
            for pr in range(4):
                gi = pi * 4 + pr
                bank = gi % 2
                waits = [("gq", aw[wname]), ("sV", lnw_done[0])]
                if gi >= 2:
                    waits.append(("sV", elu_free[gi - 2]))
                else:
                    waits.append(("sV", marks.get(f"ev_b{bank}", ffn1_V)))
                op("tensor", proj_pair_mm(wsb, pr, bank), waits=waits, inc="sT")
                pm = s.at("sT")
                hc0, hc1 = 2 * pr, 2 * pr + 1
                op("vector", lambda e, b2=bank, b3=bsb, h=hc0:
                   e.tensor_scalar_add(PEL2[:, 0:TW], PS(b2, 0, TW), b3[:, h:h + 1]),
                   waits=[("sT", pm)])
                op("vector", lambda e, b2=bank, b3=bsb, h=hc1:
                   e.tensor_scalar_add(PEL2[:, TW:2 * TW], PS(b2, TW, TW), b3[:, h:h + 1]),
                   inc="sV")
                elu_free[gi] = s.at("sV")
                op("vector", lambda e: e.tensor_single_scalar(T1EL[:], PEL2[:], 0.0, ALU.max))
                op("vector", lambda e: e.tensor_single_scalar(T2EL[:], PEL2[:], 0.0, ALU.min),
                   inc="sV")
                op("scalar", lambda e: e.activation(T2EL[:], T2EL[:], AF.Exp),
                   waits=[("sV", s.at("sV"))], inc="sS")
                op("vector", lambda e, o=osb[:, hc0 * TW: hc0 * TW + 2 * TW]:
                   e.scalar_tensor_tensor(o, T1EL[:], -1.0, T2EL[:], ALU.add, ALU.add),
                   waits=[("sS", s.at("sS"))], inc="sV")
        kf_done = s.at("sV")
        marks["qk_T"] = s.at("sT")

        v_evac_g = {}
        for c in range(2):
            for half in range(2):
                bank = c * 2 + half
                waits = [("gq", aw["wvt"]), ("sq", consts_a)]
                if bank in (0, 1):
                    waits.append(("sV", elu_free[6 + bank]))
                else:
                    waits.append(("sS", marks.get("b23_scal_free", 0)))
                    waits.append(("sV", marks.get("b01_vec_free", ffn1_V)))

                def vmm(e, c2=c, h2=half, b2=bank):
                    for e3 in range(EC):
                        inst = e.matmul(PS(b2),
                                        LNW[:, e3 * TW + c2 * 128: e3 * TW + c2 * 128 + 128],
                                        WVT[:, e3 * 1024 + h2 * 512: e3 * 1024 + h2 * 512 + 512],
                                        start=(e3 == 0), stop=(e3 == EC - 1))
                    return inst
                op("tensor", vmm, waits=waits, inc="sT")
                op("vector", lambda e, c2=c, h2=half, b2=bank:
                   e.tensor_tensor(VT[:, c2 * 1024 + h2 * 512: c2 * 1024 + h2 * 512 + 512],
                                   PS(b2), BVB[:, h2 * 512:(h2 + 1) * 512], ALU.add),
                   waits=[("sT", s.at("sT"))], inc="sV")
                v_evac_g[bank] = s.at("sV")
            op("scalar", lambda e, c2=c:
               e.activation(VT[:, c2 * 1024:(c2 + 1) * 1024], VT[:, c2 * 1024:(c2 + 1) * 1024],
                            AF.Silu),
               waits=[("sV", s.at("sV"))], inc="sS")
        v_done_S = s.at("sS")
        if w == 0:
            for hc in range(NH):
                op("vector", lambda e, h2=hc:
                   e.tensor_scalar_mul(WOT[:, h2 * 1024:(h2 + 1) * 1024],
                                       WOT[:, h2 * 1024:(h2 + 1) * 1024], SCL[:, h2:h2 + 1]),
                   waits=[("gq", aw["wot"]), ("sq", consts_a)] if hc == 0 else (),
                   inc="sV" if hc == NH - 1 else None)
            wot_scaled = ("sV", s.at("sV"))
        if w + 1 < NW:
            emit_attn_xsq(w + 1, [("sV", ffn1_V)], eng="gpsimd")

        u_free = {}

        def emit_u_pair(pr):
            bank = 2 + pr % 2
            waits = [("gq", aw["wut"]), ("sV", lnw_done[0]),
                     ("sV", v_evac_g[bank])]
            if pr >= 2:
                waits.append(("sS", u_free[pr - 2]))
            else:
                waits.append(("sS", marks.get("b23_scal_free", 0)))
            op("tensor", proj_pair_mm(WUT, pr, bank), waits=waits, inc="sT")
            pm = s.at("sT")
            for sub, hcx in ((0, 2 * pr), (1, 2 * pr + 1)):
                op("scalar", lambda e, b2=bank, sub2=sub, h=hcx:
                   e.activation(UB[:, h * TW:(h + 1) * TW], PS(b2, sub2 * TW, TW), AF.Silu,
                                bias=BU[:, h:h + 1]),
                   waits=[("sT", pm)], inc="sS")
            u_free[pr] = s.at("sS")

        g0, g1 = 2 * w, 2 * w + 1

        def ktr(c2, banks):
            def f(e, c3=c2, bk=banks):
                for h in range(NH):
                    inst = e.matmul(PS(bk[h // 4], (h % 4) * 128, 128),
                                    KF[:, h * TW + c3 * 128: h * TW + c3 * 128 + 128],
                                    ID[:], start=(h % 4 == 0), stop=True,
                                    skip_group_check=(h % 4 != 0))
                return inst
            return f

        ktrA_waits = [("sV", kf_done), marks.get("scan45_free", ("sS", 0))]
        if w == 0:
            ktrA_waits.append(("sq", consts_a))
        op("tensor", ktr(0, (4, 5)), waits=ktrA_waits, inc="sT")
        ktm = s.at("sT")
        op("scalar", lambda e: e.activation(KT[:, 0:512], PS(4), AF.Copy),
           waits=[("sT", ktm)])
        op("scalar", lambda e: e.activation(KT[:, 512:1024], PS(5), AF.Copy), inc="sS")
        kt0_done = ("sS", s.at("sS"))
        op("tensor", ktr(1, (0, 1)), waits=[("sV", v_evac_g[1])], inc="sT")
        ktm = s.at("sT")
        op("scalar", lambda e: e.activation(KT[:, 1024:1536], PS(0), AF.Copy),
           waits=[("sT", ktm)])
        op("scalar", lambda e: e.activation(KT[:, 1536:2048], PS(1), AF.Copy), inc="sS")
        kt1_done = ("sS", s.at("sS"))
        emit_u_pair(0)

        def imm(c2, banks):
            def f(e, c3=c2, bk=banks):
                for h in range(NH):
                    inst = e.matmul(PS(bk[h // 4], (h % 4) * 128, 128),
                                    KF[:, h * TW + c3 * 128: h * TW + c3 * 128 + 128],
                                    QF[:, h * TW + c3 * 128: h * TW + c3 * 128 + 128],
                                    start=(h % 4 == 0), stop=True,
                                    skip_group_check=(h % 4 != 0))
                return inst
            return f

        def omm(c2, g2, banks):
            def f(e, c3=c2, g3=g2, bk=banks):
                for h in range(NH):
                    inst = e.matmul(PS(bk[h // 4], (h % 4) * 128, 128),
                                    VT[:, c3 * 1024 + h * 128: c3 * 1024 + h * 128 + 128],
                                    ITR8[:, h * 128:(h + 1) * 128],
                                    start=(h % 4 == 0), stop=(g3 == 0),
                                    skip_group_check=(h % 4 != 0))
                    if g3 > 0:
                        inst = e.matmul(PS(bk[h // 4], (h % 4) * 128, 128),
                                        SSB[:, h * 128:(h + 1) * 128],
                                        QF[:, h * TW + c3 * 128: h * TW + c3 * 128 + 128],
                                        start=False, stop=True, skip_group_check=True)
                return inst
            return f

        def smm(c2, g2):
            def f(e, c3=c2, g3=g2):
                for h in range(NH):
                    inst = e.matmul(PS(6 + h // 4, (h % 4) * 128, 128),
                                    KT[:, c3 * 1024 + h * 128: c3 * 1024 + h * 128 + 128],
                                    VT[:, c3 * 1024 + h * 128: c3 * 1024 + h * 128 + 128],
                                    start=(g3 == 0 and h % 4 == 0), stop=True,
                                    skip_group_check=True)
                return inst
            return f

        def oevac(c2, banks, first_wait):
            for bi in range(2):
                src_ap = PS(banks[bi]).rearrange("p (h n) -> p h n", h=4)
                dst_ap = OB[:].rearrange("p (h n) -> p h n", h=8)[:, bi * 4:(bi + 1) * 4,
                                                                 c2 * 128:(c2 + 1) * 128]
                op("scalar", lambda e, si=src_ap, di_=dst_ap:
                   e.activation(di_, si, AF.Copy),
                   waits=[("sT", first_wait)] if bi == 0 else (),
                   inc="sS" if bi == 1 else None)
            return ("sS", s.at("sS"))

        op("tensor", imm(0, (4, 5)), waits=[kt0_done], inc="sT")
        im0 = s.at("sT")
        op("tensor", imm(1, (0, 1)), waits=[kt1_done, ("sV", v_evac_g[3])], inc="sT")
        im1 = s.at("sT")
        emit_u_pair(1)
        op("vector", lambda e: e.tensor_tensor(ITR8[:, 0:512], PS(4), TRIU4[:], ALU.mult),
           waits=[("sT", im0)])
        op("vector", lambda e: e.tensor_tensor(ITR8[:, 512:1024], PS(5), TRIU4[:], ALU.mult),
           inc="sV")
        mask0 = s.at("sV")
        op("tensor", omm(0, g0, (4, 5)),
           waits=[("sV", mask0), ("sS", max(v_done_S, marks.get("s_copy", 0)))],
           inc="sT")
        om0 = s.at("sT")
        marks["scan45_free"] = None
        o_evac_0 = oevac(0, (4, 5), om0)
        marks["scan45_free"] = o_evac_0
        op("tensor", smm(0, g0), waits=[("sS", max(v_done_S, marks.get("s_copy", 0)))], inc="sT")
        sm = s.at("sT")
        op("scalar", lambda e: e.activation(SSB[:, 0:512], PS(6), AF.Copy),
           waits=[("sT", sm)], inc="sS")
        op("scalar", lambda e: e.activation(SSB[:, 512:1024], PS(7), AF.Copy), inc="sS")
        marks["s_copy"] = s.at("sS")
        emit_u_pair(2)

        ln_next = None

        op("vector", lambda e: e.tensor_tensor(ITR8[:, 0:512], PS(0), TRIU4[:], ALU.mult),
           waits=[("sT", im1), ("sT", om0)])
        op("vector", lambda e: e.tensor_tensor(ITR8[:, 512:1024], PS(1), TRIU4[:], ALU.mult),
           inc="sV")
        mask1 = s.at("sV")
        op("tensor", omm(1, g1, (0, 1)),
           waits=[("sV", mask1), ("sS", marks["s_copy"])], inc="sT")
        om1 = s.at("sT")
        emit_u_pair(3)
        u_done = u_free[3]
        marks["uv_T"] = s.at("sT")
        ln_casts = None
        if w + 1 < NW:
            tok1 = (w + 1) * TW
            for ec2 in range(4):
                op("gpsimd", lambda e, o=LNW[:, ec2 * TW:(ec2 + 1) * TW],
                   i=X[:, ec2 * N + tok1: ec2 * N + tok1 + TW]: e.tensor_copy(o, i),
                   waits=[("sV", ffn1_V), ("sT", s.at("sT"))] if ec2 == 0 else (),
                   inc="sw" if ec2 == 3 else None)
            ln_casts = [("sw", s.at("sw"))]
        if w == NW - 1:
            c0xw = [("sV", marks["x_t01_final"]), ("sT", marks["qk_T"])]
            for ec2 in range(EC):
                op("gpsimd", lambda e, o=LNA[:, ec2 * N: ec2 * N + TF],
                   i=X[:, ec2 * N: ec2 * N + TF]: e.tensor_copy(o, i),
                   waits=c0xw if ec2 == 0 else (),
                   inc="sw" if ec2 == EC - 1 else None)
            marks["ffn2_c0_cp"] = ("sw", s.at("sw"))
        if g1 < NCH - 1:
            op("tensor", smm(1, g1), waits=[("sS", marks["s_copy"])], inc="sT")
            sm = s.at("sT")
            op("scalar", lambda e: e.activation(SSB[:, 0:512], PS(6), AF.Copy),
               waits=[("sT", sm)], inc="sS")
            op("scalar", lambda e: e.activation(SSB[:, 512:1024], PS(7), AF.Copy), inc="sS")
            marks["s_copy"] = s.at("sS")

        if w == NW - 1:
            c0xw = [("sV", marks["x_t01_final"]), ("sT", marks["uv_T"])]
            for ec2 in range(4):
                op("scalar", lambda e, o=XSQ[:, ec2 * TF:(ec2 + 1) * TF],
                   i=X[:, ec2 * N: ec2 * N + TF]: e.activation(o, i, AF.Square),
                   waits=c0xw if ec2 == 0 else (),
                   inc="sS" if ec2 == 3 else None)
            marks["ffn2_c0"] = [marks["ffn2_c0_cp"], ("sS", s.at("sS"))]

        zb3 = ZB[:].rearrange("p (h n) -> p h n", h=NH)
        ub3 = UB[:].rearrange("p (h n) -> p h n", h=NH)
        ob3 = OB[:].rearrange("p (h n) -> p h n", h=NH)
        ps01 = PSA[:, 0:1024].rearrange("p (h n) -> p h n", h=NH)
        op("vector", lambda e: e.tensor_mul(zb3[:, 0:4, 128:256], ps01[:, 0:4],
                                            ub3[:, 0:4, 128:256]),
           waits=[("sT", om1), ("sS", u_free[1])], inc="sV")
        op("vector", lambda e: e.tensor_mul(zb3[:, 0:4, 0:128], ob3[:, 0:4, 0:128],
                                            ub3[:, 0:4, 0:128]),
           waits=[o_evac_0], inc="sV")
        zb_a = s.at("sV")
        op("vector", lambda e: e.tensor_mul(zb3[:, 4:8, 128:256], ps01[:, 4:8],
                                            ub3[:, 4:8, 128:256]),
           waits=[("sS", u_done)], inc="sV")
        zb_c1 = s.at("sV")
        op("vector", lambda e: e.tensor_mul(zb3[:, 4:8, 0:128], ob3[:, 4:8, 0:128],
                                            ub3[:, 4:8, 0:128]),
           inc="sV")
        zb_done = s.at("sV")
        marks["ob_free"] = ("sV", zb_done)
        if ln_casts is not None:
            tok1 = (w + 1) * TW
            for ec2 in range(4, EC):
                op("vector", lambda e, o=LNW[:, ec2 * TW:(ec2 + 1) * TW],
                   i=X[:, ec2 * N + tok1: ec2 * N + tok1 + TW]: e.tensor_copy(o, i),
                   waits=[("sV", ffn1_V), ("sT", marks["uv_T"])] if ec2 == 4 else (),
                   inc="sV" if ec2 == EC - 1 else None)
            ln_casts.append(("sV", s.at("sV")))
        if w == NW - 1:
            c0xw2 = [("sV", marks["x_t01_final"]), ("sT", marks["uv_T"])]
            for ec2 in range(4, EC):
                op("vector", lambda e, o=XSQ[:, ec2 * TF:(ec2 + 1) * TF],
                   i=X[:, ec2 * N: ec2 * N + TF]: e.tensor_mul(o, i, i),
                   waits=c0xw2 if ec2 == 4 else (),
                   inc="sV" if ec2 == EC - 1 else None)
            marks["ffn2_c0"].append(("sV", s.at("sV")))
        osq3 = OSQ[:].rearrange("p (h n) -> p h n", h=NH)
        op("scalar", lambda e: e.activation(osq3[:, 0:4, 128:256], ps01[:, 0:4],
                                            AF.Square),
           waits=[("sT", om1), ("sV", zb_a)], inc="sS")
        osq_a = s.at("sS")
        op("scalar", lambda e: e.activation(osq3[:, 4:8, 128:256], ps01[:, 4:8],
                                            AF.Square),
           waits=[("sV", zb_c1)], inc="sS")
        osq_c1 = s.at("sS")
        op("scalar", lambda e: e.activation(osq3[:, :, 0:128], ob3[:, :, 0:128],
                                            AF.Square),
           waits=[o_evac_0], inc="sS")
        osq_done = s.at("sS")
        marks["osq_sS"] = osq_done
        op("scalar", lambda e: e.activation(DUMW[:], CV[:, 0:1], AF.Ln))

        def wo_mm(ea, eb, b2):
            def f(e, ea2=ea, eb2=eb, b3=b2):
                for sub, ex in ((0, ea2), (1, eb2)):
                    for h in range(NH):
                        inst = e.matmul(
                            PS(b3, sub * TW, TW),
                            WOT[:, h * 1024 + ex * 128: h * 1024 + ex * 128 + 128],
                            ZB[:, h * TW:(h + 1) * TW],
                            start=(sub == 0 and h == 0), stop=(h == NH - 1),
                            skip_group_check=(sub == 1))
                return inst
            return f

        def wo_evac(ea, eb, b2, wm, rstd_sS):
            for sub, ex in ((0, ea), (1, eb)):
                xs = X[:, ex * N + tok0: ex * N + tok0 + TW]
                op("vector", lambda e, b3=b2, sub2=sub:
                   e.tensor_mul(R2A[:], PS(b3, sub2 * TW, TW), R1A[:]),
                   waits=[("sT", wm), ("sS", rstd_sS)] if sub == 0 else ())
                op("vector", lambda e, o=xs, e3=ex:
                   e.scalar_tensor_tensor(o, R2A[:], BO[:, e3:e3 + 1], o, ALU.add, ALU.add),
                   inc="sV" if sub == 1 else None)
            return s.at("sV")

        def wo_mm_half(ea, eb, b2, hlf):
            def f(e, ea2=ea, eb2=eb, b3=b2, h4=hlf):
                for sub, ex in ((0, ea2), (1, eb2)):
                    for h in range(h4 * 4, h4 * 4 + 4):
                        inst = e.matmul(
                            PS(b3, sub * TW, TW),
                            WOT[:, h * 1024 + ex * 128: h * 1024 + ex * 128 + 128],
                            ZB[:, h * TW:(h + 1) * TW],
                            start=(sub == 0 and h == 0), stop=(h == NH - 1),
                            skip_group_check=(sub == 1))
                return inst
            return f

        op("tensor", wo_mm_half(0, 1, 0, 0), waits=[("sV", zb_a), ("sS", osq_a)]
           + ([wot_scaled] if wot_scaled else []))
        op("tensor", wo_mm_half(0, 1, 0, 1), waits=[("sV", zb_done), ("sS", osq_c1)],
           inc="sT")
        wm0 = s.at("sT")
        ln_next = None
        if w + 1 < NW:
            ln_next = emit_attn_ln_rest(w + 1, ln_casts,
                                        extra_waits=[marks["scan45_free"]])
            marks["scan45_free"] = ("sV", ln_next[0])
        else:
            marks["ffn2_stats"] = emit_ffn_ln_stats(
                0, marks["ffn2_c0"],
                extra_waits=[marks["scan45_free"], ("sV", zb_done)])
        op("tensor", wo_mm(2, 3, 1), waits=[("sV", zb_done), ("sS", osq_c1)],
           inc="sT")
        wm1 = s.at("sT")

        def rms_mm(e):
            for h in range(NH):
                inst = e.matmul(PS(2, 0, TW), ONES[:], OSQ[:, h * TW:(h + 1) * TW],
                                start=(h == 0), stop=(h == NH - 1))
            return inst
        rms_waits = [("sS", osq_done)]
        op("tensor", rms_mm, waits=rms_waits, inc="sT")
        op("scalar", lambda e: e.activation(R1A[:], PS(2, 0, TW), AF.Ln, bias=CV[:, 2:3]),
           waits=[("sT", s.at("sT"))], inc="sS")
        rms_ln1 = s.at("sS")
        op("scalar", lambda e: e.activation(R1A[:], R1A[:], AF.Exp, scale=CV[:, 3:4]))
        op("scalar", lambda e: e.activation(R1A[:], R1A[:], AF.Ln, bias=CV[:, 6:7]))
        op("scalar", lambda e: e.activation(R1A[:], R1A[:], AF.Exp, scale=CV[:, 4:5],
                                            bias=CV[:, 5:6]), inc="sS")
        rstd_sS = s.at("sS")
        marks["b23_scal_free"] = s.at("sS")

        op("tensor", wo_mm(4, 5, 3), waits=[("sV", zb_done)], inc="sT")
        wm2 = s.at("sT")
        op("tensor", wo_mm(6, 7, 2), waits=[("sV", zb_done), ("sS", rms_ln1)], inc="sT")
        wm3 = s.at("sT")
        marks["wo_T"] = wm3
        if ln_next is not None:
            lnw_done = emit_attn_ln_norm(ln_next[1])
        else:
            ffn2_chain = emit_ffn_ln_chain(
                0, marks["ffn2_stats"],
                bufs=(MBF2[:], RBF2[:], BVB[:, 0:512], BVB[:, 512:1024]))
        marks["ev_b0"] = wo_evac(0, 1, 0, wm0, rstd_sS)
        marks["ev_b1"] = wo_evac(2, 3, 1, wm1, rstd_sS)
        if ln_next is None:
            marks["ffn2_ln0"] = emit_ffn_ln_norm_only(
                0, ffn2_chain,
                bufs=(MBF2[:], RBF2[:], BVB[:, 0:512], BVB[:, 512:1024]))
        marks["ev_b3"] = wo_evac(4, 5, 3, wm2, rstd_sS)
        marks["ev_b2"] = wo_evac(6, 7, 2, wm3, rstd_sS)
        marks["ev_bank"] = {0: marks["ev_b0"], 1: marks["ev_b1"],
                            2: marks["ev_b2"], 3: marks["ev_b3"]}
        marks["b01_vec_free"] = s.at("sV")
        if w == 1:
            marks["x_t01_final"] = s.at("sV")


    attn_V = s.at("sV")

    ffn_pass(1, [("sV", attn_V), ("sS", s.at("sS"))], ln0_done=marks["ffn2_ln0"])

    so_total = s.at("so")
    s.emit("sync", lambda e: e.wait_ge(sems["so"], so_total), [], [])

    with nc.Block() as blk:
        def make_body(eng_name):
            def body(eng):
                last_wait = {}
                for fn, waits, incs in s.items[eng_name]:
                    for wsem, wval in waits:
                        if wval <= 0 or last_wait.get(wsem, -1) >= wval:
                            continue
                        eng.wait_ge(sems[wsem], wval)
                        last_wait[wsem] = wval
                    inst = fn(eng)
                    for isem, amt in incs:
                        inst = inst.then_inc(sems[isem], amt)
            return body
        blk.sync(make_body("sync"))
        blk.gpsimd(make_body("gpsimd"))
        blk.tensor(make_body("tensor"))
        blk.vector(make_body("vector"))
        blk.scalar(make_body("scalar"))

    return nc


def _host_inputs(Wq, bq, Wk, bk, Wv, bv, Wu, bu, Wo, bo, scale, w1, b1, w2, b2):
    asf = lambda a: np.ascontiguousarray(np.asarray(a, dtype=np.float32))
    w1 = asf(w1); w2 = asf(w2)
    fmw = lambda W: np.ascontiguousarray(
        asf(W).T.reshape(8, 128, 1024).transpose(1, 0, 2).reshape(128, 8192))
    triu = np.triu(np.ones((128, 128), dtype=np.float32))
    scl = np.ascontiguousarray(asf(scale).reshape(8, 128).T)
    return {
        "w1h": np.ascontiguousarray(w1.reshape(8, 128, 8, 512).transpose(1, 2, 0, 3).reshape(128, 32768)),
        "w2h": np.ascontiguousarray(w2.reshape(32, 128, 8, 128).transpose(1, 2, 0, 3).reshape(128, 32768)),
        "wqt": fmw(Wq), "wkt": fmw(Wk), "wut": fmw(Wu), "wvt": fmw(Wv), "wot": fmw(Wo),
        "ident": np.eye(128, dtype=BF),
        "ones": np.ones((128, 128), dtype=BF),
        "triu4": np.ascontiguousarray(np.tile(triu, (1, 4))),
        "b1l": np.ascontiguousarray(asf(b1).reshape(32, 128).T),
        "b2l": np.ascontiguousarray(asf(b2).reshape(8, 128).T),
        "bql": np.ascontiguousarray(asf(bq).reshape(8, 128).T),
        "bkl": np.ascontiguousarray(asf(bk).reshape(8, 128).T),
        "bul": np.ascontiguousarray(asf(bu).reshape(8, 128).T),
        "bol": np.ascontiguousarray(asf(bo).reshape(8, 128).T),
        "scl": scl,
        "bvb": np.ascontiguousarray(np.broadcast_to(asf(bv), (128, 1024))),
        "cvec": np.ascontiguousarray(np.tile(np.array(
            [1e-5, -0.5, 1e-30, 0.5, -1.0, np.log(32.0), 3.2e-7, 0.0],
            dtype=np.float32), (128, 1))),
    }


def kernel(L0_attn_x, L0_attn_Wq, L0_attn_bq, L0_attn_Wk, L0_attn_bk, L0_attn_Wv,
           L0_attn_bv, L0_attn_Wu, L0_attn_bu, L0_attn_Wo, L0_attn_bo, L0_attn_scale,
           L0_ffn_w1, L0_ffn_b1, L0_ffn_w2, L0_ffn_b2):
    x = np.asarray(L0_attn_x, dtype=np.float32)
    com = _host_inputs(L0_attn_Wq, L0_attn_bq, L0_attn_Wk, L0_attn_bk, L0_attn_Wv,
                       L0_attn_bv, L0_attn_Wu, L0_attn_bu, L0_attn_Wo, L0_attn_bo,
                       L0_attn_scale, L0_ffn_w1, L0_ffn_b1, L0_ffn_w2, L0_ffn_b2)
    nc = build_program()
    in_maps = []
    for b in range(8):
        m = dict(com)
        m["x_fm"] = np.ascontiguousarray(x[b].T)
        in_maps.append(m)
    res = run_bass_kernel_spmd(nc, in_maps, core_ids=list(range(8)))
    out = np.stack([np.asarray(r["out_fm"], dtype=np.float32).T for r in res.results])
    return np.ascontiguousarray(out)


if __name__ == "__main__":
    nc = build_program()
    print("built ok, instructions:", sum(1 for _ in nc.all_instructions()))
